# version 10
# baseline (speedup 1.0000x reference)
import numpy as np
import concourse.bass as bass
import concourse.mybir as mybir
from concourse.bass_utils import run_bass_kernel_spmd
from contextlib import ExitStack

F32 = mybir.dt.float32
BF16 = mybir.dt.bfloat16
I32 = mybir.dt.int32
AX = mybir.AxisListType.X
OP = mybir.AluOpType
AF = mybir.ActivationFunctionType

D, T, C, FF, R, NC = 2048, 8192, 256, 5504, 1024, 8
EPS = 1e-5
NSUP, SUP, NI = 16, 512, 16
NOT, NDT = FF // 128, D // 128
PHASES = 4
DEBUG_TAPS = False
DEBUG_TAP1 = False

_tp = 2.0 * np.pi
TP_HI = float(np.float32(_tp))
TP_MD = float(np.float32(_tp - TP_HI))
TP_LO = float(_tp - TP_HI - float(np.float32(_tp - TP_HI)))
INV2PI = float(np.float32(1.0 / _tp))
_lc = np.log(10000.0) / 32.0
LC_HI = float(np.float32(_lc))
LC_LO = float(_lc - LC_HI)
PIH = float(np.float32(np.pi / 2))

SLABS = []
for j in range(NOT):
    SLABS.append(("g", j, 0, NI))
    SLABS.append(("u", j, 0, NI))
for j in range(NDT):
    for (c0, c1) in ((0, 16), (16, 32), (32, NOT)):
        SLABS.append(("d", j, c0, c1))
NSLAB = len(SLABS)
NGU = 2 * NOT


class _PlanInstr:
    def then_inc(self, sem, n):
        return self


class _PlanEng:
    def __getattr__(self, name):
        def f(*a, **k):
            return _PlanInstr()
        return f


class Ctr:
    def __init__(self, name):
        self.name, self.v, self.sem = name, 0, None
    def inc(self, instr, n=1):
        if self.sem is not None:
            instr.then_inc(self.sem, n)
        self.v += n
        return self.v


class Env:
    def __init__(self):
        self.M = {}
        self.plan = True
    def mark(self, ctr, name):
        if self.plan:
            self.M[name] = ctr.v
        else:
            assert self.M.get(name) == ctr.v, (name, self.M.get(name), ctr.v)
    def wait(self, eng, ctr, name):
        if not self.plan:
            eng.wait_ge(ctr.sem, self.M[name])


class Arena:
    def __init__(self, nc):
        self.nc = nc
        self.tensors = []
        self.cur = 0
        self.hi = 0
        self.saves = {}
    def alloc(self, name, shape, dtype):
        nbytes = int(np.prod(shape[1:])) * mybir.dt.size(dtype)
        off = (self.cur + 31) // 32 * 32
        self.cur = off + nbytes
        self.hi = max(self.hi, self.cur)
        h = self.nc.virtual_sbuf_tensor(name, list(shape), dtype)
        self.tensors.append((h, off))
        return h
    def save(self, key):
        self.saves[key] = self.cur
    def restore(self, key):
        self.cur = self.saves[key]
    def finalize(self, es):
        backing = es.enter_context(
            self.nc.sbuf_tensor("arena", [128, self.hi], mybir.dt.uint8))
        base = self.nc.lookup_mloc(backing).addr
        for h, off in self.tensors:
            self.nc._tensor(h.name, list(h.shape), h.dtype, type="SB")
            mloc = self.nc.lookup_mloc(h)
            mloc.allocated = not self.nc.target_bir_lowering
            mloc.addr = base + off


def build_nc():
    nc = bass.Bass("TRN2", target_bir_lowering=False, debug=False,
                   detect_race_conditions=False)
    xf = nc.dram_tensor("xf", [D, T], F32, kind="ExternalInput").ap()
    xres = nc.dram_tensor("xres", [C, T], F32, kind="ExternalInput").ap()
    wqs = nc.dram_tensor("wqs", [128, NI * C], F32, kind="ExternalInput").ap()
    wg = nc.dram_tensor("wg", [NOT, 128, NI * 128], F32, kind="ExternalInput").ap()
    wu = nc.dram_tensor("wu", [NOT, 128, NI * 128], F32, kind="ExternalInput").ap()
    wd = nc.dram_tensor("wd", [NDT, 128, NOT * 128], F32, kind="ExternalInput").ap()
    vecs = nc.dram_tensor("vecs", [16, 128], F32, kind="ExternalInput").ap()
    out = nc.dram_tensor("out", [C, T], F32, kind="ExternalOutput").ap()
    gn32 = nc.dram_tensor("gnspill", [2, 128, T], F32, kind="Internal").ap()
    y1d = nc.dram_tensor("y1spill", [C, T], F32,
                         kind="ExternalOutput" if DEBUG_TAPS else "Internal").ap()
    if DEBUG_TAPS:
        d_y1t = nc.dram_tensor("d_y1t", [128, NDT * R], BF16, kind="ExternalOutput").ap()
    if DEBUG_TAP1:
        d_t1 = nc.dram_tensor("d_t1", [128, 4 * 512], F32, kind="ExternalOutput").ap()
        d_t3 = nc.dram_tensor("d_t3", [2, 1024], F32, kind="ExternalOutput").ap()
        d_t5 = nc.dram_tensor("d_t5", [2, 2048], BF16, kind="ExternalOutput").ap()
        d_t6 = nc.dram_tensor("d_t6", [128, 16], F32, kind="ExternalOutput").ap()
        d_t7 = nc.dram_tensor("d_t7", [64, 1024], F32, kind="ExternalOutput").ap()
        d_hb = nc.dram_tensor("d_hb", [128, NOT * R], BF16, kind="ExternalOutput").ap()
        d_t8 = nc.dram_tensor("d_t8", [128, 1024], F32, kind="ExternalOutput").ap()
        d_t9 = nc.dram_tensor("d_t9", [1, 2048], F32, kind="ExternalOutput").ap()
        d_t10 = nc.dram_tensor("d_t10", [1, 1024], F32, kind="ExternalOutput").ap()
        d_t11 = nc.dram_tensor("d_t11", [128, 32], F32, kind="ExternalOutput").ap()

    es = ExitStack()
    A = Arena(nc)

    s_ramp = A.alloc("s_ramp", [128, 128], F32)
    s_idf = A.alloc("s_idf", [128, 128], F32)
    s_idb = A.alloc("s_idb", [128, 128], BF16)
    s_pt = A.alloc("s_pt", [128, 128], BF16)
    s_ones05 = A.alloc("s_ones05", [128, 128], BF16)
    s_onesf = A.alloc("s_onesf", [128, 128], F32)
    s_onesr = A.alloc("s_onesr", [2, 128], BF16)
    s_sel = A.alloc("s_sel", [2, 128], BF16)
    s_g2 = A.alloc("s_g2", [128, 2], BF16)
    s_cc = A.alloc("s_cc", [128, 8], F32)
    s_hh = A.alloc("s_hh", [128, 2], F32)
    s_gw = A.alloc("s_gw", [128, 2], F32)
    s_gb = A.alloc("s_gb", [128, 2], F32)
    s_iv = A.alloc("s_iv", [128, 8], F32)
    s_vec = A.alloc("s_vec", [16, 128], F32)
    s_vcols = A.alloc("s_vcols", [128, 16], F32)
    s_amc = A.alloc("s_amc", [128, 32], F32)
    s_ln = A.alloc("s_ln", [128, 16], F32)
    s_ivr = A.alloc("s_ivr", [1, 128], F32)
    s_rhl = A.alloc("s_rhl", [1, 2 * R], BF16)
    s_H0 = A.alloc("s_H0", [128, T], BF16)
    _H0_off = A.tensors[-1][1]
    s_H1 = A.alloc("s_H1", [128, T], BF16)
    s_y1t = A.alloc("s_y1t", [128, NDT * R], BF16)
    _y1t_off = A.tensors[-1][1]
    s_wq = A.alloc("s_wq", [128, NI * C], BF16)
    s_W0 = A.alloc("s_W0", [128, 1024], F32)
    s_W1 = A.alloc("s_W1", [128, 1024], F32)
    s_W2 = A.alloc("s_W2", [128, 1024], F32)
    s_qi = A.alloc("s_qi", [128, 1024], I32)
    _qi_off = A.tensors[-1][1]
    s_wsl = A.alloc("s_wsl", [128, 2 * 1024], F32)
    s_wab = A.alloc("s_wab", [128, 1024], F32)
    s_e1 = A.alloc("s_e1", [128, 1024], F32)
    s_rs1 = A.alloc("s_rs1", [128, R], F32)
    A.save("hb")
    s_hb = A.alloc("s_hb", [128, NOT * R], BF16)
    hb_end = A.cur
    A.restore("hb")
    s_xf = A.alloc("s_xf", [128, 2 * NI * SUP], BF16)
    s_xr = A.alloc("s_xr", [128, D], F32)
    s_y1q = A.alloc("s_y1q", [128, D], BF16)
    s_tv = A.alloc("s_tv", [1, SUP], F32)
    s_tbl = A.alloc("s_tbl", [64, 2 * SUP], F32)
    s_tbi = A.alloc("s_tbi", [64, SUP], I32)
    s_tbn = A.alloc("s_tbn", [64, 2 * SUP], F32)
    s_cosb = A.alloc("s_cosb", [64, 2 * SUP], F32)
    _cosb_off = A.tensors[-1][1]
    s_sinb = A.alloc("s_sinb", [64, 2 * SUP], F32)
    s_qraw = A.alloc("s_qraw", [128, 2 * SUP], F32)
    s_qbf = A.alloc("s_qbf", [64, SUP], BF16)
    s_hbf = A.alloc("s_hbf", [128, 2 * SUP], BF16)
    s_st = A.alloc("s_st", [2, 2 * SUP], F32)
    s_mhl = A.alloc("s_mhl", [2, 2 * SUP], BF16)
    s_rhl2 = A.alloc("s_rhl2", [2, 2 * SUP], BF16)
    s_srow = A.alloc("s_srow", [2, SUP], F32)
    assert A.cur <= hb_end, (A.cur, hb_end)
    A.cur = hb_end
    A.hi = max(A.hi, A.cur)
    s_gn32 = nc.virtual_sbuf_tensor("s_gn32", [128, 2 * D], F32)
    A.tensors.append((s_gn32, _H0_off))
    s_y1s = nc.virtual_sbuf_tensor("s_y1s", [128, D], F32)
    A.tensors.append((s_y1s, _cosb_off))
    s_y1r = nc.virtual_sbuf_tensor("s_y1r", [128, D], BF16)
    A.tensors.append((s_y1r, _qi_off))
    s_ych = nc.virtual_sbuf_tensor("s_ych", [128, 2 * D], F32)
    A.tensors.append((s_ych, _y1t_off))

    p_acc = es.enter_context(nc.psum_tensor("p_acc", [128, 1024], F32))
    p_ub = es.enter_context(nc.psum_tensor("p_ub", [128, 1024], F32))
    p_st2 = es.enter_context(nc.psum_tensor("p_st2", [128, 1024], F32))
    p_x = es.enter_context(nc.psum_tensor("p_x", [128, 512], F32))
    p_tr = es.enter_context(nc.psum_tensor("p_tr", [128, 1024], BF16))

    env = Env()
    names = ["sLD", "sLG", "sST", "sV", "sS", "sT", "sG", "sXF", "sWSL", "sXR", "sSP", "sVX", "sGN", "sRL", "sW2"]
    ctrs = {n: Ctr(n) for n in names}
    (sLD, sLG, sST, sV, sS, sT, sG, sXF, sWSL, sXR, sSP, sVX, sGN, sRL, sW2) = (ctrs[n] for n in names)

    def vsync(v, instr):
        sVX.inc(instr, 1)
        if not env.plan:
            v.wait_ge(sVX.sem, sVX.v)
        return instr

    HALVES = []
    for sl in range(NSLAB):
        kind, j, c0, c1 = SLABS[sl]
        w = (c1 - c0) * 128
        nh = (w + 1023) // 1024
        for h2 in range(nh):
            HALVES.append((sl, h2, min(1024, w - h2 * 1024)))
    NHALF = len(HALVES)
    HIDX = {}
    for gi, (sl, h2, hw) in enumerate(HALVES):
        HIDX[(sl, h2)] = gi
    NHOF = {}
    for sl in range(NSLAB):
        kind, j, c0, c1 = SLABS[sl]
        NHOF[sl] = ((c1 - c0) * 128 + 1023) // 1024
    if PHASES >= 4:
        HACT = HALVES
    elif PHASES >= 3:
        HACT = [hh2 for hh2 in HALVES if SLABS[hh2[0]][0] in "gu"]
    else:
        HACT = []

    def wq_slab_ap(wslot, off, width):
        return s_wq[:, wslot * 2048 + off:wslot * 2048 + off + width]

    def prog_gpsimd(g):
        g.iota(s_ramp[:], pattern=[[-1, 128]], base=0, channel_multiplier=1,
               allow_small_or_imprecise_dtypes=True)
        g.memset(s_ones05[:], 0.5 / 128.0)
        g.memset(s_onesf[:], 0.5 / 128.0)
        g.memset(s_onesr[:], 1.0)
        g.iota(s_srow[:, 0:128], pattern=[[-1, 2], [0, 64]], base=0,
               channel_multiplier=1, allow_small_or_imprecise_dtypes=True)
        g.memset(s_g2[:, 0:2], 0.0)
        g.memset(s_g2[0:64, 0:1], 1.0 / 64.0)
        g.memset(s_g2[64:128, 1:2], 1.0 / 64.0)
        g.memset(s_cc[:, 0:1], EPS)
        g.memset(s_cc[:, 1:2], PIH)
        g.memset(s_cc[:, 2:3], EPS / 2)
        g.memset(s_cc[:, 3:4], 1e-5)
        sG.inc(g.memset(s_cc[:, 4:5], 0.0))
        env.mark(sG, "g_setup")
        sLG.inc(g.dma_start(out=s_wq[:], in_=wqs[:]), 16)
        env.mark(sLG, "wq_bf")
        xfr = xf.rearrange("(c p) t -> c p t", p=128)
        for u in range(NSUP):
            env.wait(g, sS, "invf")
            if u >= 1:
                env.wait(g, sT, f"ang_{u-1}")
            sG.inc(g.iota(s_tv[:], pattern=[[1, SUP]], base=u * SUP,
                          channel_multiplier=0,
                          allow_small_or_imprecise_dtypes=True))
            env.mark(sG, f"tv_{u}")
            if u == 1 and DEBUG_TAP1:
                env.wait(g, sV, "vcols")
                sSP.inc(g.dma_start(out=d_t6[:, 0:8], in_=s_vcols[:, 0:8]), 16)
                env.wait(g, sS, "invf")
                sSP.inc(g.dma_start(out=d_t6[:, 8:11], in_=s_iv[:, 1:4]), 16)
                env.wait(g, sS, "tbl_0")
                sSP.inc(g.dma_start(out=d_t7[:, 0:512], in_=s_cosb[:, 0:512]), 16)
                sSP.inc(g.dma_start(out=d_t7[:, 512:1024], in_=s_sinb[:, 0:512]), 16)
                env.wait(g, sS, "qevac_0_0")
                sSP.inc(g.dma_start(out=d_t1[:, 0:512], in_=s_qraw[:, 0:512]), 16)
                env.mark(sSP, "tap1")
                env.wait(g, sS, "H_0_0")
                sSP.inc(g.dma_start(out=d_t1[:, 512:1024], in_=s_qraw[:, 0:512]), 16)
                env.mark(sSP, "tap2")
                env.wait(g, sS, "stev_0_0")
                sSP.inc(g.dma_start(out=d_t3[:], in_=s_st[:]), 16)
                env.mark(sSP, "tap3")
                env.wait(g, sV, "gnapp_0_0")
                sSP.inc(g.dma_start(out=d_t1[:, 1024:1536], in_=s_qraw[:, 0:512]), 16)
                env.mark(sSP, "tap4")
                env.wait(g, sV, "sthl_0_0")
                sSP.inc(g.dma_start(out=d_t5[:, 0:1024], in_=s_mhl[:]), 16)
                sSP.inc(g.dma_start(out=d_t5[:, 1024:2048], in_=s_rhl2[:]), 16)
                env.mark(sSP, "tap5")
            buf = u % 2
            if u >= 2:
                env.wait(g, sT, f"mix_done_{u-2}")
            for q4 in range(4):
                dst = s_xf[:, buf * NI * SUP + q4 * 4 * SUP:buf * NI * SUP + (q4 + 1) * 4 * SUP]
                dst = dst.rearrange("p (c t) -> p c t", c=4)
                src = xfr[q4 * 4:(q4 + 1) * 4, :, u * SUP:(u + 1) * SUP].transpose([1, 0, 2])
                sXF.inc(g.dma_start(out=dst, in_=src), 16)
            env.mark(sXF, f"xf_{u}")
        for ch in range(8 if PHASES >= 2 else 0):
            env.wait(g, sV, f"y1_{ch}")
            qt, s4 = ch // 4, ch % 4
            y1r = y1d.rearrange("(q p) t -> q p t", p=128)
            sSP.inc(g.dma_start(out=y1r[qt, :, s4 * D:(s4 + 1) * D], in_=s_y1s[:]), 16)
            env.mark(sSP, f"y1sp_{ch}")
        if DEBUG_TAPS and PHASES >= 2:
            env.wait(g, sV, "rs1b")
            sSP.inc(g.dma_start(out=d_y1t[:], in_=s_y1t[:]), 16)
            if DEBUG_TAP1 and PHASES >= 4:
                env.wait(g, sV, "htree")
                sSP.inc(g.dma_start(out=d_t8[:], in_=s_W0[:]), 16)
                env.mark(sSP, "tapHT")
                env.wait(g, sV, "s2cols")
                sSP.inc(g.dma_start(out=d_t11[:], in_=s_amc[:, 0:32]), 16)
                env.mark(sSP, "tapS2")
            if PHASES >= 4 and DEBUG_TAP1:
                env.wait(g, sV, "hq_all")
                sSP.inc(g.dma_start(out=d_hb[:], in_=s_hb[:]), 16)
        for gi, (sl, h2, hw) in enumerate(HACT):
            kind, j, c0, c1 = SLABS[sl]
            if gi % 2 != 1:
                continue
            if gi >= 2:
                psl, ph2, _ = HALVES[gi - 2]
                env.wait(g, sV, f"wq_done_{psl}_{ph2}")
            if kind == "d":
                src_ap = wd[j][:, c0 * 128 + h2 * 1024:c0 * 128 + h2 * 1024 + hw]
            else:
                src_ap = (wg if kind == "g" else wu)[j][:, h2 * 1024:h2 * 1024 + hw]
            sW2.inc(g.dma_start(out=s_wsl[:, (gi % 2) * 1024:(gi % 2) * 1024 + hw],
                                in_=src_ap), 16)
            env.mark(sW2, f"wsl_{sl}_{h2}")
        for ch in range(8 if PHASES >= 4 else 0):
            env.wait(g, sV, "hq_all")
            if ch >= 1:
                env.wait(g, sV, f"out_{ch-1}")
            qt, s4 = ch // 4, ch % 4
            y1r = y1d.rearrange("(q p) t -> q p t", p=128)
            sSP.inc(g.dma_start(out=s_y1r[:], in_=y1r[qt, :, s4 * D:(s4 + 1) * D]), 16)
            env.mark(sSP, f"y1rl_{ch}")

    def prog_sync(sync):
        sLD.inc(sync.dma_start(out=s_vec[:], in_=vecs[:]), 16)
        env.mark(sLD, "vec_loads")
        for u in range(NSUP):
            for h in range(2):
                env.wait(sync, sS, f"gnf_{u}_{h}")
                sGN.inc(sync.dma_start(out=gn32[h][:, u * SUP:(u + 1) * SUP],
                                       in_=s_qraw[:, h * SUP:(h + 1) * SUP]), 16)
                env.mark(sGN, f"gnsp_{u}_{h}")
        xrr = xres.rearrange("(q p) t -> q p t", p=128)
        for ch in range(8 if PHASES >= 2 else 0):
            qt, s4 = ch // 4, ch % 4
            if ch >= 2:
                env.wait(sync, sV, f"lnv_{ch-2}")
            sRL.inc(sync.dma_start(out=s_gn32[:, (ch % 2) * D:(ch % 2 + 1) * D],
                                   in_=gn32[qt][:, s4 * D:(s4 + 1) * D]), 16)
            env.mark(sRL, f"gnrl_{ch}")
            if ch >= 1:
                env.wait(sync, sV, f"lnv_{ch-1}")
            sXR.inc(sync.dma_start(out=s_xr[:],
                                   in_=xrr[qt, :, s4 * D:(s4 + 1) * D]), 16)
            env.mark(sXR, f"xr_{ch}")
        for gi, (sl, h2, hw) in enumerate(HACT):
            kind, j, c0, c1 = SLABS[sl]
            if gi % 2 != 0:
                continue
            if gi >= 2:
                psl, ph2, _ = HALVES[gi - 2]
                env.wait(sync, sV, f"wq_done_{psl}_{ph2}")
            if kind == "d":
                src_ap = wd[j][:, c0 * 128 + h2 * 1024:c0 * 128 + h2 * 1024 + hw]
            else:
                src_ap = (wg if kind == "g" else wu)[j][:, h2 * 1024:h2 * 1024 + hw]
            sWSL.inc(sync.dma_start(out=s_wsl[:, (gi % 2) * 1024:(gi % 2) * 1024 + hw],
                                    in_=src_ap), 16)
            env.mark(sWSL, f"wsl_{sl}_{h2}")
        outr = out.rearrange("(q p) t -> q p t", p=128)
        for ch in range(8 if PHASES >= 4 else 0):
            env.wait(sync, sV, f"out_{ch}")
            qt, s4 = ch // 4, ch % 4
            sST.inc(sync.dma_start(out=outr[qt, :, s4 * D:(s4 + 1) * D],
                                   in_=s_ych[:, (ch % 2) * D:(ch % 2 + 1) * D]), 16)
            env.mark(sST, f"ost_{ch}")
        if not env.plan:
            sync.wait_ge(sST.sem, sST.v)
            sync.wait_ge(sSP.sem, sSP.v)

    def prog_scalar(s):
        env.wait(s, sLD, "vec_loads")
        env.wait(s, sG, "g_setup")
        env.wait(s, sV, "v_iv")
        sS.inc(s.activation(s_iv[:, 3:4], s_iv[:, 2:3], AF.Identity))
        env.mark(sS, "invf")
        for u in range(NSUP):
            for h in range(2):
                if h == 0:
                    env.wait(s, sV, f"tblr_{u}")
                    par = u % 2
                    s.activation(s_tbl[:, SUP:2 * SUP], s_tbl[:, 0:SUP], AF.Abs)
                    s.activation(s_sinb[:, par * SUP:(par + 1) * SUP],
                                 s_tbl[:, 0:SUP], AF.Sin)
                    sS.inc(s.activation(s_cosb[:, par * SUP:(par + 1) * SUP],
                                        s_tbl[:, SUP:2 * SUP], AF.Sin,
                                        bias=s_cc[0:64, 1:2], scale=s_cc[0:64, 6:7]))
                    env.mark(sS, f"tbl_{u}")
                qsl = s_qraw[:, h * SUP:(h + 1) * SUP]
                env.wait(s, sT, f"mmacc_{u}_{h}")
                if DEBUG_TAP1 and u == 1 and h == 0:
                    env.wait(s, sSP, "tap4")
                if u >= 1:
                    env.wait(s, sGN, f"gnsp_{u-1}_{h}")
                sS.inc(s.copy(qsl, p_acc[:, h * SUP:(h + 1) * SUP]))
                env.mark(sS, f"qevac_{u}_{h}")
                if h == 0:
                    if DEBUG_TAP1 and u == 0:
                        env.wait(s, sSP, "tap1")
                    sS.inc(s.copy(s_qbf[:], qsl[0:64, :]))
                    env.mark(sS, f"qbf_{u}")
                    env.wait(s, sV, f"blend_{u}")
                s.activation(qsl, qsl, AF.Relu)
                sS.inc(s.activation(qsl, qsl, AF.Tanh, bias=s_vcols[:, h:h + 1]))
                env.mark(sS, f"H_{u}_{h}")
                if u > 0 or h > 0:
                    pu, phh = (u, 0) if h == 1 else (u - 1, 1)
                    env.wait(s, sT, f"stats_{pu}_{phh}")
                if DEBUG_TAP1 and u == 0 and h == 0:
                    env.wait(s, sSP, "tap2")
                s.copy(s_hbf[:, 0:SUP], qsl)
                sS.inc(s.activation(s_hbf[:, SUP:2 * SUP], qsl, AF.Square))
                env.mark(sS, f"Hbf_{u}_{h}")
                env.wait(s, sT, f"stats_{u}_{h}")
                s.copy(s_st[:, 0:SUP], p_st2[0:2, 0:SUP])
                sS.inc(s.copy(s_st[:, SUP:2 * SUP], p_st2[0:2, 512:512 + SUP]))
                env.mark(sS, f"stev_{u}_{h}")
                env.wait(s, sV, f"var_{u}_{h}")
                sS.inc(s.activation(s_st[:, SUP:2 * SUP], s_st[:, SUP:2 * SUP],
                                    AF.Sqrt, bias=s_cc[0:2, 3:4]))
                env.mark(sS, f"sqrt_{u}_{h}")
                env.wait(s, sV, f"gnapp_{u}_{h}")
                sS.inc(s.activation(qsl, qsl, AF.Identity,
                                    bias=s_vcols[:, 4 + h:5 + h], scale=s_vcols[:, 2 + h:3 + h]))
                env.mark(sS, f"gnf_{u}_{h}")
        for ch in range(8 if PHASES >= 2 else 0):
            env.wait(s, sV, f"lnv_{ch}")
            sS.inc(s.activation(s_ln[:, 5:6], s_ln[:, 3:4], AF.Sqrt))
            env.mark(sS, f"lnsq_{ch}")
            for rnd in range(2):
                env.wait(s, sT, f"y1tr_{ch}_{rnd}")
                qt, s4 = ch // 4, ch % 4
                b = qt * 4 + s4
                dst = s_y1t.rearrange("p (n r) -> p n r", n=NDT)[
                    :, rnd * 8:(rnd + 1) * 8, b * 128:(b + 1) * 128]
                sS.inc(s.copy(dst, p_tr.rearrange("p (n r) -> p n r", n=8)))
                env.mark(sS, f"y1tev_{ch}_{rnd}")
        for gi, (sl, h2, hw) in enumerate(HACT):
            kind, j, c0, c1 = SLABS[sl]
            env.wait(s, sWSL if gi % 2 == 0 else sW2, f"wsl_{sl}_{h2}")
            if gi >= 1:
                psl, ph2, _ = HALVES[gi - 1]
                env.wait(s, sV, f"wq_done_{psl}_{ph2}")
            sS.inc(s.activation(s_wab[:, 0:hw],
                                s_wsl[:, (gi % 2) * 1024:(gi % 2) * 1024 + hw],
                                AF.Abs))
            env.mark(sS, f"whi_{sl}_{h2}")
            env.wait(s, sT, f"sclmm_{sl}_{h2}")
            sS.inc(s.activation(s_e1[:, 0:hw], p_st2[:, 0:hw], AF.Identity,
                                bias=s_cc[:, 2:3]))
            env.mark(sS, f"e1_{sl}_{h2}")
            if kind == "u" and h2 == NHOF[sl] - 1:
                env.wait(s, sV, f"gmul_{j}")
                sS.inc(s.activation(s_W0[:], s_W0[:], AF.Silu))
                env.mark(sS, f"silu_{j}")
        for ch in range(8 if PHASES >= 4 else 0):
            env.wait(s, sV, f"ln2v_{ch}")
            sS.inc(s.activation(s_ln[:, 13:14], s_ln[:, 11:12], AF.Sqrt,
                                bias=s_cc[:, 3:4]))
            env.mark(sS, f"ln2sq_{ch}")

    def prog_tensor(t):
        env.wait(t, sG, "g_setup")
        env.wait(t, sV, "idf")
        env.wait(t, sLD, "vec_loads")
        sT.inc(t.matmul(p_x[:, 0:128], s_vec[:], s_idf[0:16, :], start=True, stop=True))
        env.mark(sT, "vectr")
        env.wait(t, sV, "vcols")
        env.wait(t, sV, "v_const")
        env.wait(t, sS, "invf")
        sT.inc(t.matmul(p_x[0:1, 0:128], s_iv[:, 3:4], s_idf[:], start=True, stop=True))
        env.mark(sT, "ivrow")
        env.wait(t, sV, "ivrowev")
        env.wait(t, sG, "tv_0")
        sT.inc(t.matmul(p_x[0:64, 0:SUP], s_ivr[0:1, 0:64], s_tv[:],
                        start=True, stop=True))
        env.mark(sT, "ang_0")
        env.wait(t, sLG, "wq_bf")
        for u in range(NSUP):
            env.wait(t, sXF, f"xf_{u}")
            buf = u % 2
            for h in range(2):
                if u >= 1:
                    env.wait(t, sS, f"qevac_{u-1}_{h}")
                mm = None
                for c in range(NI):
                    lw = s_wq[:, c * C + h * 128:c * C + (h + 1) * 128]
                    rhs = s_xf[:, buf * NI * SUP + c * SUP:buf * NI * SUP + (c + 1) * SUP]
                    mm = t.matmul(p_acc[:, h * SUP:(h + 1) * SUP], lw, rhs,
                                  start=(c == 0), stop=(c == NI - 1))
                sT.inc(mm)
                env.mark(sT, f"mmacc_{u}_{h}")
                if h == 0:
                    env.wait(t, sS, f"qbf_{u}")
                    env.wait(t, sV, f"tblfree_{u}")
                    sT.inc(t.matmul(p_x[0:64, 0:SUP], s_pt[0:64, 0:64], s_qbf[:],
                                    start=True, stop=True))
                    env.mark(sT, f"rot_{u}")
                env.wait(t, sS, f"Hbf_{u}_{h}")
                if u > 0 or h > 0:
                    pu, phh = (u, 0) if h == 1 else (u - 1, 1)
                    env.wait(t, sS, f"stev_{pu}_{phh}")
                t.matmul(p_st2[0:2, 0:SUP], s_g2[:, 0:2], s_hbf[:, 0:SUP],
                         start=True, stop=True)
                sT.inc(t.matmul(p_st2[0:2, 512:512 + SUP], s_g2[:, 0:2],
                                s_hbf[:, SUP:2 * SUP], start=True, stop=True))
                env.mark(sT, f"stats_{u}_{h}")
                env.wait(t, sV, f"sthl_{u}_{h}")
                if u > 0 or h > 0:
                    pu, phh = (u, 0) if h == 1 else (u - 1, 1)
                    env.wait(t, sV, f"gnapp_{pu}_{phh}")
                t.matmul(p_ub[:, 0:SUP], s_sel[0:2, :], s_mhl[:, 0:SUP],
                         start=True, stop=False)
                t.matmul(p_ub[:, 0:SUP], s_sel[0:2, :], s_mhl[:, SUP:2 * SUP],
                         start=False, stop=True)
                t.matmul(p_ub[:, 512:1024], s_sel[0:2, :], s_rhl2[:, 0:SUP],
                         start=True, stop=False)
                sT.inc(t.matmul(p_ub[:, 512:1024], s_sel[0:2, :], s_rhl2[:, SUP:2 * SUP],
                                start=False, stop=True))
                env.mark(sT, f"rep_{u}_{h}")
            if u + 1 < NSUP:
                env.wait(t, sV, f"blend_{u}")
                env.wait(t, sG, f"tv_{u+1}")
                sT.inc(t.matmul(p_x[0:64, 0:SUP], s_ivr[0:1, 0:64], s_tv[:],
                                start=True, stop=True))
                env.mark(sT, f"ang_{u+1}")
            env.mark(sT, f"mix_done_{u}")
        for ch in range(8 if PHASES >= 2 else 0):
            env.wait(t, sV, f"y1q_{ch}")
            for rnd in range(2):
                if not (ch == 0 and rnd == 0):
                    pch, prnd = (ch, 0) if rnd == 1 else (ch - 1, 1)
                    env.wait(t, sS, f"y1tev_{pch}_{prnd}")
                tr = None
                for b8 in range(8):
                    cb = rnd * 8 + b8
                    tr = t.transpose(p_tr[:, b8 * 128:(b8 + 1) * 128],
                                     s_y1q[:, cb * 128:(cb + 1) * 128], s_idb[:])
                sT.inc(tr)
                env.mark(sT, f"y1tr_{ch}_{rnd}")
        if PHASES < 2:
            return
        env.wait(t, sV, "rs1cols")
        for b in range(8):
            sT.inc(t.matmul(p_x[0:1, 0:128], s_amc[:, 8 + b:9 + b], s_idf[:],
                            start=True, stop=True))
            env.mark(sT, f"rs1row_{b}")
            env.wait(t, sV, f"rs1rowev_{b}")
        env.wait(t, sV, "rs1hl")
        t.matmul(p_ub[:, 0:512], s_onesr[0:1, :], s_rhl[0:1, 0:512], start=True, stop=False)
        t.matmul(p_ub[:, 0:512], s_onesr[0:1, :], s_rhl[0:1, R:R + 512], start=False, stop=True)
        t.matmul(p_ub[:, 512:1024], s_onesr[0:1, :], s_rhl[0:1, 512:1024],
                 start=True, stop=False)
        sT.inc(t.matmul(p_ub[:, 512:1024], s_onesr[0:1, :], s_rhl[0:1, R + 512:2 * R],
                        start=False, stop=True))
        env.mark(sT, "rs1rep")

        def emit_sclmm(gi):
            sl, h2, hw = HALVES[gi]
            env.wait(t, sS, f"whi_{sl}_{h2}")
            if gi >= 1:
                psl, ph2, _ = HALVES[gi - 1]
                env.wait(t, sS, f"e1_{psl}_{ph2}")
            mm = None
            for qq in range((hw + 511) // 512):
                qw = min(512, hw - qq * 512)
                off = qq * 512
                mm = t.matmul(p_st2[:, off:off + qw], s_onesf[:],
                              s_wab[:, off:off + qw], start=True, stop=True)
            sT.inc(mm)
            env.mark(sT, f"sclmm_{sl}_{h2}")

        def emit_main(sl):
            kind, j, c0, c1 = SLABS[sl]
            wslot = sl % 2
            if kind == "g":
                if j == 0:
                    env.wait(t, sS, "qevac_15_1")
                else:
                    env.wait(t, sV, f"gmul_{j-1}")
                tgt, rhsbuf, nch, cg0 = p_acc, s_y1t, NI, 0
            elif kind == "u":
                if j == 0:
                    env.wait(t, sV, "rs1b")
                else:
                    env.wait(t, sV, f"umul_{j-1}")
                tgt, rhsbuf, nch, cg0 = p_ub, s_y1t, NI, 0
            else:
                if c0 == 0:
                    if j == 0:
                        env.wait(t, sV, "hq_all")
                    else:
                        env.wait(t, sV, f"dmul_{j-1}")
                tgt, rhsbuf, nch, cg0 = p_acc, s_hb, c1 - c0, c0
            lastc = NI - 1 if kind in "gu" else NOT - 1
            mm = None
            for ci in range(nch):
                c = cg0 + ci
                lw = wq_slab_ap(wslot, ci * 128, 128)
                t.matmul(tgt[:, 0:512], lw, rhsbuf[:, c * R:c * R + 512],
                         start=(c == 0), stop=(c == lastc))
                mm = t.matmul(tgt[:, 512:1024], lw, rhsbuf[:, c * R + 512:(c + 1) * R],
                              start=(c == 0), stop=(c == lastc))
            sT.inc(mm)
            env.mark(sT, f"main_{sl}")

        for gi, (sl, h2, hw) in enumerate(HACT):
            kind, j, c0, c1 = SLABS[sl]
            if kind == "d" and c0 == 0 and h2 == 0 and j == 0:
                env.wait(t, sV, "htree")
                for rnd in range(2):
                    if rnd == 1:
                        env.wait(t, sV, "htr_ev_0")
                    tr = None
                    for b4 in range(4):
                        b = rnd * 4 + b4
                        tr = t.transpose(p_x[:, b4 * 128:(b4 + 1) * 128],
                                         s_h32v[:, b * 128:(b + 1) * 128], s_idf[:])
                    sT.inc(tr)
                    env.mark(sT, f"htr_{rnd}")
                env.wait(t, sV, "s2cols")
                for b in range(8):
                    sT.inc(t.matmul(p_x[0:1, 0:128], s_amc[:, b:b + 1], s_idf[:],
                                    start=True, stop=True))
                    env.mark(sT, f"s2row_{b}")
                    env.wait(t, sV, f"s2rowev_{b}")
                env.wait(t, sV, "s2hl")
                env.wait(t, sV, f"umul_{NOT-1}")
                t.matmul(p_ub[:, 0:512], s_onesr[0:1, :], s_rhl[0:1, 0:512],
                         start=True, stop=False)
                t.matmul(p_ub[:, 0:512], s_onesr[0:1, :], s_rhl[0:1, R:R + 512],
                         start=False, stop=True)
                t.matmul(p_ub[:, 512:1024], s_onesr[0:1, :], s_rhl[0:1, 512:1024],
                         start=True, stop=False)
                sT.inc(t.matmul(p_ub[:, 512:1024], s_onesr[0:1, :],
                                s_rhl[0:1, R + 512:2 * R], start=False, stop=True))
                env.mark(sT, "s2rep")
                env.wait(t, sV, "hq_all")
                for b in range(8):
                    sT.inc(t.matmul(p_x[0:1, 0:128], s_amc[:, 16 + b:17 + b], s_idf[:],
                                    start=True, stop=True))
                    env.mark(sT, f"rs2row_{b}")
                    env.wait(t, sV, f"rs2rowev_{b}")
                env.wait(t, sV, "rs2hl")
                t.matmul(p_ub[:, 0:512], s_onesr[0:1, :], s_rhl[0:1, 0:512],
                         start=True, stop=False)
                t.matmul(p_ub[:, 0:512], s_onesr[0:1, :], s_rhl[0:1, R:R + 512],
                         start=False, stop=True)
                t.matmul(p_ub[:, 512:1024], s_onesr[0:1, :], s_rhl[0:1, 512:1024],
                         start=True, stop=False)
                sT.inc(t.matmul(p_ub[:, 512:1024], s_onesr[0:1, :],
                                s_rhl[0:1, R + 512:2 * R], start=False, stop=True))
                env.mark(sT, "rs2rep")
            emit_sclmm(gi)
            if h2 == NHOF[sl] - 1:
                env.wait(t, sV, f"wq_done_{sl}_{h2}")
                emit_main(sl)
        for ch in range(8 if PHASES >= 4 else 0):
            qt, s4 = ch // 4, ch % 4
            b = qt * 4 + s4
            env.wait(t, sV, f"dmul_{NDT-1}")
            for rnd in range(2):
                if not (ch == 0 and rnd == 0):
                    pch, prnd = (ch, 0) if rnd == 1 else (ch - 1, 1)
                    env.wait(t, sV, f"ytok_{pch}_{prnd}")
                tr = None
                for b8 in range(8):
                    c = rnd * 8 + b8
                    dscb = s_H0 if c < 8 else s_H1
                    cc8 = c % 8
                    tr = t.transpose(p_tr[:, b8 * 128:(b8 + 1) * 128],
                                     dscb[:, cc8 * R + b * 128:cc8 * R + (b + 1) * 128],
                                     s_idb[:])
                sT.inc(tr)
                env.mark(sT, f"outtr_{ch}_{rnd}")

    s_h32v = s_W0
    def prog_vector(v):
        env.wait(v, sG, "g_setup")
        v.tensor_scalar(s_sel[:], s_srow[:, 0:128], 0.0, None, op0=OP.is_equal)
        v.tensor_scalar(s_idf[:], s_ramp[:], 0.0, None, op0=OP.is_equal)
        sV.inc(v.tensor_scalar(s_idb[:], s_ramp[:], 0.0, None, op0=OP.is_equal))
        env.mark(sV, "idf")
        v.tensor_copy(s_iv[:, 1:2], s_ramp[:, 0:1])
        env.wait(v, sT, "vectr")
        sV.inc(v.tensor_copy(s_vcols[:], p_x[:, 0:16]))
        env.mark(sV, "vcols")
        if not env.plan:
            v.wait_ge(sV.sem, sV.v)
        vsync(v, v.scalar_tensor_tensor(s_amc[:, 1:2], s_vcols[:, 7:8], -2.0,
                                        s_iv[:, 1:2], op0=OP.mult, op1=OP.add))
        vsync(v, v.tensor_scalar(s_amc[:, 2:3], s_amc[:, 1:2], -1.0, 1.0,
                                 op0=OP.mult, op1=OP.add))
        vsync(v, v.tensor_copy(s_amc[:, 3:4], s_vcols[:, 7:8]))
        v.tensor_scalar(s_W0[:, 0:128], s_ramp[:], -1.0, None, op0=OP.is_equal)
        v.tensor_scalar(s_W1[:, 0:128], s_ramp[:], 1.0, None, op0=OP.is_equal)
        v.tensor_scalar(s_W0[:, 0:128], s_W0[:, 0:128], s_amc[:, 2:3], None, op0=OP.mult)
        v.tensor_scalar(s_W1[:, 0:128], s_W1[:, 0:128], s_amc[:, 1:2], None, op0=OP.mult)
        v.tensor_tensor(s_pt[:], s_W0[:, 0:128], s_W1[:, 0:128], op=OP.subtract)
        sV.inc(v.memset(s_cc[:, 6:7], -1.0))
        env.mark(sV, "v_const")
        env.wait(v, sLD, "vec_loads")
        vsync(v, v.tensor_tensor(s_amc[:, 4:5], s_vcols[:, 8:9], s_vcols[:, 9:10],
                                 op=OP.add))
        vsync(v, v.tensor_tensor(s_amc[:, 5:6], s_amc[:, 4:5], s_amc[:, 4:5],
                                 op=OP.mult))
        vsync(v, v.tensor_tensor(s_amc[:, 6:7], s_amc[:, 5:6], s_amc[:, 5:6],
                                 op=OP.mult))
        vsync(v, v.tensor_tensor(s_amc[:, 7:8], s_amc[:, 6:7], s_amc[:, 6:7],
                                 op=OP.mult))
        vsync(v, v.tensor_tensor(s_iv[:, 1:2], s_amc[:, 7:8], s_amc[:, 7:8],
                                 op=OP.mult))
        for i, src_c in enumerate([s_amc[:, 4:5], s_amc[:, 5:6], s_amc[:, 6:7],
                                   s_amc[:, 7:8], s_iv[:, 1:2]]):
            vsync(v, v.scalar_tensor_tensor(s_amc[:, 2:3], src_c, -1.0,
                                            s_vcols[:, 10 + i:11 + i],
                                            op0=OP.add, op1=OP.mult))
            vsync(v, v.tensor_scalar(s_amc[:, 2:3], s_amc[:, 2:3], 1.0, None,
                                     op0=OP.add))
            if i == 0:
                last_i = v.tensor_copy(s_iv[:, 2:3], s_amc[:, 2:3])
            else:
                last_i = v.tensor_tensor(s_iv[:, 2:3], s_iv[:, 2:3],
                                         s_amc[:, 2:3], op=OP.mult)
            if i < 4:
                vsync(v, last_i)
        sV.inc(last_i)
        env.mark(sV, "v_iv")
        env.wait(v, sT, "ivrow")
        sV.inc(v.tensor_copy(s_ivr[0:1, :], p_x[0:1, 0:128]))
        env.mark(sV, "ivrowev")
        for u in range(NSUP):
            for h in range(2):
                if h == 0:
                    env.wait(v, sT, f"ang_{u}")
                    sV.inc(v.tensor_scalar(s_tbl[:, 0:SUP], p_x[0:64, 0:SUP],
                                            s_vcols[0:64, 6:7], None, op0=OP.mult))
                    env.mark(sV, f"tblfree_{u}")
                    v.tensor_scalar(s_tbl[:, SUP:2 * SUP], s_tbl[:, 0:SUP], INV2PI,
                                    None, op0=OP.mult)
                    v.tensor_copy(s_tbi[:], s_tbl[:, SUP:2 * SUP])
                    v.tensor_tensor(s_tbl[:, SUP:2 * SUP], s_tbl[:, SUP:2 * SUP],
                                    s_tbi[:], op=OP.subtract)
                    v.tensor_scalar(s_tbn[:, 0:SUP], s_tbl[:, SUP:2 * SUP], 0.5,
                                    None, op0=OP.is_ge)
                    v.tensor_scalar(s_tbn[:, SUP:2 * SUP], s_tbl[:, SUP:2 * SUP],
                                    -0.5, None, op0=OP.is_le)
                    v.tensor_tensor(s_tbn[:, 0:SUP], s_tbn[:, 0:SUP],
                                    s_tbn[:, SUP:2 * SUP], op=OP.subtract)
                    v.tensor_tensor(s_tbn[:, 0:SUP], s_tbi[:], s_tbn[:, 0:SUP],
                                    op=OP.add)
                    v.scalar_tensor_tensor(s_tbl[:, 0:SUP], s_tbn[:, 0:SUP], -TP_HI,
                                           s_tbl[:, 0:SUP], op0=OP.mult, op1=OP.add)
                    v.scalar_tensor_tensor(s_tbl[:, 0:SUP], s_tbn[:, 0:SUP], -TP_MD,
                                           s_tbl[:, 0:SUP], op0=OP.mult, op1=OP.add)
                    sV.inc(v.scalar_tensor_tensor(s_tbl[:, 0:SUP], s_tbn[:, 0:SUP],
                                                  -TP_LO, s_tbl[:, 0:SUP],
                                                  op0=OP.mult, op1=OP.add))
                    env.mark(sV, f"tblr_{u}")
                    env.wait(v, sS, f"qevac_{u}_0")
                    env.wait(v, sT, f"rot_{u}")
                    env.wait(v, sS, f"tbl_{u}")
                    par = u % 2
                    qsl = s_qraw[:, 0:SUP]
                    v.tensor_tensor(s_W2[0:64, 0:SUP], qsl[0:64, :],
                                    s_cosb[:, par * SUP:(par + 1) * SUP], op=OP.mult)
                    v.tensor_tensor(s_W2[0:64, 512:1024], p_x[0:64, 0:SUP],
                                    s_sinb[:, par * SUP:(par + 1) * SUP], op=OP.mult)
                    sV.inc(v.tensor_tensor(qsl[0:64, :], s_W2[0:64, 0:SUP],
                                           s_W2[0:64, 512:1024], op=OP.add))
                    env.mark(sV, f"blend_{u}")
                env.wait(v, sS, f"stev_{u}_{h}")
                if DEBUG_TAP1 and u == 0 and h == 0:
                    env.wait(v, sSP, "tap3")
                mu = s_st[:, 0:SUP]
                m2 = s_st[:, SUP:2 * SUP]
                vsync(v, v.tensor_tensor(s_srow[:, 0:SUP], mu, mu, op=OP.mult))
                sV.inc(v.tensor_tensor(m2, m2, s_srow[:, 0:SUP], op=OP.subtract))
                env.mark(sV, f"var_{u}_{h}")
                env.wait(v, sS, f"sqrt_{u}_{h}")
                vsync(v, v.reciprocal(m2, m2))
                vsync(v, v.tensor_copy(s_mhl[:, 0:SUP], mu))
                vsync(v, v.tensor_tensor(s_srow[:, 0:SUP], mu, s_mhl[:, 0:SUP], op=OP.subtract))
                v.tensor_copy(s_mhl[:, SUP:2 * SUP], s_srow[:, 0:SUP])
                vsync(v, v.tensor_copy(s_rhl2[:, 0:SUP], m2))
                vsync(v, v.tensor_tensor(s_srow[:, 0:SUP], m2, s_rhl2[:, 0:SUP], op=OP.subtract))
                sV.inc(v.tensor_copy(s_rhl2[:, SUP:2 * SUP], s_srow[:, 0:SUP]))
                env.mark(sV, f"sthl_{u}_{h}")
                env.wait(v, sT, f"rep_{u}_{h}")
                if DEBUG_TAP1 and u == 0 and h == 1:
                    env.wait(v, sSP, "tap5")
                env.wait(v, sS, f"H_{u}_{h}")
                qsl = s_qraw[:, h * SUP:(h + 1) * SUP]
                v.tensor_tensor(qsl, qsl, p_ub[:, 0:SUP], op=OP.subtract)
                sV.inc(v.tensor_tensor(qsl, qsl, p_ub[:, 512:512 + SUP], op=OP.mult))
                env.mark(sV, f"gnapp_{u}_{h}")
        for ch in range(8 if PHASES >= 2 else 0):
            qt, s4 = ch // 4, ch % 4
            gsl = s_gn32[:, (ch % 2) * D:(ch % 2 + 1) * D]
            env.wait(v, sRL, f"gnrl_{ch}")
            env.wait(v, sXR, f"xr_{ch}")
            if ch >= 1:
                env.wait(v, sSP, f"y1sp_{ch-1}")
            xsl = s_xr[:]
            v.scalar_tensor_tensor(s_y1s[:], xsl, 0.0, gsl, op0=OP.add, op1=OP.add,
                                   accum_out=s_ln[:, 0:1])
            vsync(v, v.scalar_tensor_tensor(gsl, s_y1s[:], 0.0, s_y1s[:], op0=OP.add,
                                            op1=OP.mult, accum_out=s_ln[:, 1:2]))
            vsync(v, v.tensor_scalar(s_ln[:, 2:3], s_ln[:, 0:1], 1.0 / D, None, op0=OP.mult))
            vsync(v, v.tensor_scalar(s_ln[:, 3:4], s_ln[:, 1:2], 1.0 / D, None, op0=OP.mult))
            vsync(v, v.tensor_tensor(s_ln[:, 4:5], s_ln[:, 2:3], s_ln[:, 2:3], op=OP.mult))
            sV.inc(v.tensor_tensor(s_ln[:, 3:4], s_ln[:, 3:4], s_ln[:, 4:5],
                                   op=OP.subtract))
            env.mark(sV, f"lnv_{ch}")
            env.wait(v, sS, f"lnsq_{ch}")
            vsync(v, v.reciprocal(s_ln[:, 5:6], s_ln[:, 5:6]))
            vsync(v, v.tensor_tensor(s_ln[:, 6:7], s_ln[:, 2:3], s_ln[:, 5:6], op=OP.mult))
            vsync(v, v.tensor_scalar(s_ln[:, 6:7], s_ln[:, 6:7], -1.0, None, op0=OP.mult))
            sV.inc(v.tensor_scalar(s_y1s[:], s_y1s[:], s_ln[:, 5:6], s_ln[:, 6:7],
                                   op0=OP.mult, op1=OP.add))
            env.mark(sV, f"y1_{ch}")
            vsync(v, v.reduce_max(s_ln[:, 7:8], s_y1s[:], axis=AX, apply_absolute_value=True))
            vsync(v, v.tensor_scalar(s_ln[:, 7:8], s_ln[:, 7:8], EPS, None, op0=OP.max))
            vsync(v, v.reciprocal(s_ln[:, 8:9], s_ln[:, 7:8]))
            vsync(v, v.tensor_scalar(s_ln[:, 9:10], s_ln[:, 8:9], 127.0, None, op0=OP.mult))
            v.tensor_scalar(s_amc[:, 8 + ch:9 + ch], s_ln[:, 7:8], 1.0 / 127.0, None,
                            op0=OP.mult)
            if ch >= 1:
                env.wait(v, sT, f"y1tr_{ch-1}_1")
            for h2 in range(2):
                sl1024 = slice(h2 * 1024, (h2 + 1) * 1024)
                v.tensor_scalar(s_W0[:], s_y1s[:, sl1024], s_ln[:, 9:10], None,
                                op0=OP.mult)
                v.tensor_copy(s_qi[:], s_W0[:])
                v.tensor_tensor(s_W0[:], s_W0[:], s_qi[:], op=OP.subtract)
                v.tensor_scalar(s_W1[:], s_W0[:], 0.5, None, op0=OP.is_ge)
                v.tensor_scalar(s_W2[:], s_W0[:], -0.5, None, op0=OP.is_le)
                v.tensor_tensor(s_W1[:], s_W1[:], s_W2[:], op=OP.subtract)
                ql = v.tensor_tensor(s_y1q[:, sl1024], s_qi[:], s_W1[:], op=OP.add)
            sV.inc(ql)
            env.mark(sV, f"y1q_{ch}")
        if PHASES < 2:
            return
        env.mark(sV, "rs1cols")
        for b in range(8):
            env.wait(v, sT, f"rs1row_{b}")
            sV.inc(v.tensor_copy(s_W1[0:1, b * 128:(b + 1) * 128], p_x[0:1, 0:128]))
            env.mark(sV, f"rs1rowev_{b}")
        vsync(v, v.tensor_copy(s_rhl[0:1, 0:R], s_W1[0:1, 0:R]))
        vsync(v, v.tensor_copy(s_W0[0:1, 0:R], s_rhl[0:1, 0:R]))
        vsync(v, v.tensor_tensor(s_W0[0:1, 0:R], s_W1[0:1, 0:R], s_W0[0:1, 0:R],
                                 op=OP.subtract))
        sV.inc(v.tensor_copy(s_rhl[0:1, R:2 * R], s_W0[0:1, 0:R]))
        env.mark(sV, "rs1hl")
        env.wait(v, sT, "rs1rep")
        sV.inc(v.tensor_copy(s_rs1[:], p_ub[:, 0:1024]))
        env.mark(sV, "rs1b")

        for gi, (sl, h2, hw) in enumerate(HACT):
            kind, j, c0, c1 = SLABS[sl]
            env.wait(v, sS, f"e1_{sl}_{h2}")
            wsl = s_wsl[:, (gi % 2) * 1024:(gi % 2) * 1024 + hw]
            v.tensor_tensor(s_W1[:, 0:hw], wsl, s_e1[:, 0:hw], op=OP.is_ge)
            v.tensor_tensor(s_W2[:, 0:hw], s_wab[:, 0:hw], s_e1[:, 0:hw], op=OP.is_ge)
            v.scalar_tensor_tensor(s_W1[:, 0:hw], s_W1[:, 0:hw], 2.0, s_W2[:, 0:hw],
                                   op0=OP.mult, op1=OP.subtract)
            sV.inc(v.scalar_tensor_tensor(
                wq_slab_ap(sl % 2, h2 * 1024, hw), s_W1[:, 0:hw], 2.0, s_e1[:, 0:hw],
                op0=OP.mult, op1=OP.mult))
            env.mark(sV, f"wq_done_{sl}_{h2}")
            if h2 == NHOF[sl] - 1:
                if kind == "u":
                    env.wait(v, sT, f"main_{sl-1}")
                    sV.inc(v.tensor_tensor(s_W0[:], p_acc[:, 0:1024], s_rs1[:],
                                           op=OP.mult))
                    env.mark(sV, f"gmul_{j}")
                    env.wait(v, sT, f"main_{sl}")
                    sV.inc(v.tensor_tensor(s_W1[:], p_ub[:, 0:1024], s_rs1[:],
                                           op=OP.mult))
                    env.mark(sV, f"umul_{j}")
                    env.wait(v, sS, f"silu_{j}")
                    if j == 28 and PHASES >= 2:
                        env.wait(v, sSP, "y1sp_7")
                    sV.inc(v.tensor_tensor(s_hb[:, j * R:(j + 1) * R], s_W0[:],
                                           s_W1[:], op=OP.mult))
                    env.mark(sV, f"h_{j}")
                    if j == NOT - 1 and PHASES >= 4:
                        v.tensor_tensor(s_W0[:], s_hb[:, 0:R], s_hb[:, R:2 * R],
                                        op=OP.max)
                        v.tensor_tensor(s_W1[:], s_hb[:, 0:R], s_hb[:, R:2 * R],
                                        op=OP.min)
                        for c in range(2, NOT):
                            v.tensor_tensor(s_W0[:], s_W0[:], s_hb[:, c * R:(c + 1) * R],
                                            op=OP.max)
                            v.tensor_tensor(s_W1[:], s_W1[:], s_hb[:, c * R:(c + 1) * R],
                                            op=OP.min)
                        v.tensor_scalar(s_W2[:], s_W1[:], -1.0, None, op0=OP.mult)
                        sV.inc(v.tensor_tensor(s_W0[:], s_W0[:], s_W2[:], op=OP.max))
                        env.mark(sV, "htree")
                        if DEBUG_TAP1:
                            env.wait(v, sSP, "tapHT")
                        for rnd in range(2):
                            env.wait(v, sT, f"htr_{rnd}")
                            for b4 in range(4):
                                b = rnd * 4 + b4
                                rm_i = v.reduce_max(s_amc[:, 24 + b:25 + b],
                                                    p_x[:, b4 * 128:(b4 + 1) * 128],
                                                    axis=AX)
                            sV.inc(rm_i)
                            env.mark(sV, f"htr_ev_{rnd}")
                        if not env.plan:
                            v.wait_ge(sV.sem, sV.v)
                        vsync(v, v.tensor_scalar(s_amc[:, 24:32], s_amc[:, 24:32],
                                                 EPS, None, op0=OP.max))
                        for b in range(8):
                            r_i = v.reciprocal(s_amc[:, b:b + 1], s_amc[:, 24 + b:25 + b])
                        vsync(v, r_i)
                        vsync(v, v.tensor_scalar(s_amc[:, 0:8], s_amc[:, 0:8], 127.0,
                                                 None, op0=OP.mult))
                        sV.inc(v.tensor_scalar(s_amc[:, 16:24], s_amc[:, 24:32],
                                               1.0 / 127.0, None, op0=OP.mult))
                        env.mark(sV, "s2cols")
                        if DEBUG_TAP1:
                            env.wait(v, sSP, "tapS2")
                        for b in range(8):
                            env.wait(v, sT, f"s2row_{b}")
                            sV.inc(v.tensor_copy(s_W1[0:1, b * 128:(b + 1) * 128],
                                                 p_x[0:1, 0:128]))
                            env.mark(sV, f"s2rowev_{b}")
                        vsync(v, v.tensor_copy(s_rhl[0:1, 0:R], s_W1[0:1, 0:R]))
                        vsync(v, v.tensor_copy(s_W0[0:1, 0:R], s_rhl[0:1, 0:R]))
                        vsync(v, v.tensor_tensor(s_W0[0:1, 0:R], s_W1[0:1, 0:R],
                                                 s_W0[0:1, 0:R], op=OP.subtract))
                        sV.inc(v.tensor_copy(s_rhl[0:1, R:2 * R], s_W0[0:1, 0:R]))
                        env.mark(sV, "s2hl")
                        env.wait(v, sT, "s2rep")
                        for c in range(NOT):
                            v.tensor_tensor(s_W0[:], s_hb[:, c * R:(c + 1) * R],
                                            p_ub[:, 0:1024], op=OP.mult)
                            v.tensor_copy(s_qi[:], s_W0[:])
                            v.tensor_tensor(s_W0[:], s_W0[:], s_qi[:], op=OP.subtract)
                            v.tensor_scalar(s_W1[:], s_W0[:], 0.5, None, op0=OP.is_ge)
                            v.tensor_scalar(s_W2[:], s_W0[:], -0.5, None, op0=OP.is_le)
                            v.tensor_tensor(s_W1[:], s_W1[:], s_W2[:], op=OP.subtract)
                            hq_i = v.tensor_tensor(s_hb[:, c * R:(c + 1) * R],
                                                   s_qi[:], s_W1[:], op=OP.add)
                        sV.inc(hq_i)
                        env.mark(sV, "hq_all")
                        for b in range(8):
                            env.wait(v, sT, f"rs2row_{b}")
                            sV.inc(v.tensor_copy(s_W1[0:1, b * 128:(b + 1) * 128],
                                                 p_x[0:1, 0:128]))
                            env.mark(sV, f"rs2rowev_{b}")
                        vsync(v, v.tensor_copy(s_rhl[0:1, 0:R], s_W1[0:1, 0:R]))
                        vsync(v, v.tensor_copy(s_W0[0:1, 0:R], s_rhl[0:1, 0:R]))
                        vsync(v, v.tensor_tensor(s_W0[0:1, 0:R], s_W1[0:1, 0:R],
                                                 s_W0[0:1, 0:R], op=OP.subtract))
                        sV.inc(v.tensor_copy(s_rhl[0:1, R:2 * R], s_W0[0:1, 0:R]))
                        env.mark(sV, "rs2hl")
                        env.wait(v, sT, "rs2rep")
                        sV.inc(v.tensor_copy(s_rs1[:], p_ub[:, 0:1024]))
                        env.mark(sV, "rs2b")
                elif kind == "d" and c1 == NOT:
                    env.wait(v, sT, f"main_{sl}")
                    dscb = s_H0 if j < 8 else s_H1
                    jj = j % 8
                    sV.inc(v.tensor_tensor(dscb[:, jj * R:(jj + 1) * R],
                                           p_acc[:, 0:1024], s_rs1[:], op=OP.mult))
                    env.mark(sV, f"dmul_{j}")
        for ch in range(8 if PHASES >= 4 else 0):
            qt, s4 = ch // 4, ch % 4
            if ch >= 2:
                env.wait(v, sST, f"ost_{ch-2}")
            env.wait(v, sSP, f"y1rl_{ch}")
            for rnd in range(2):
                env.wait(v, sT, f"outtr_{ch}_{rnd}")
                sV.inc(v.tensor_tensor(
                    s_ych[:, (ch % 2) * D + rnd * 1024:(ch % 2) * D + (rnd + 1) * 1024],
                    p_tr[:, 0:1024], s_y1r[:, rnd * 1024:(rnd + 1) * 1024], op=OP.add))
                env.mark(sV, f"ytok_{ch}_{rnd}")
            ysl = s_ych[:, (ch % 2) * D:(ch % 2 + 1) * D]
            v.scalar_tensor_tensor(s_W0[:], ysl[:, 0:1024], 0.0, ysl[:, 0:1024],
                                   op0=OP.add, op1=OP.add, accum_out=s_ln[:, 0:1])
            v.scalar_tensor_tensor(s_W0[:], ysl[:, 1024:2048], 0.0, ysl[:, 1024:2048],
                                   op0=OP.add, op1=OP.add, accum_out=s_ln[:, 1:2])
            v.scalar_tensor_tensor(s_W0[:], ysl[:, 0:1024], 0.0, ysl[:, 0:1024],
                                   op0=OP.add, op1=OP.mult, accum_out=s_ln[:, 2:3])
            vsync(v, v.scalar_tensor_tensor(s_W0[:], ysl[:, 1024:2048], 0.0,
                                            ysl[:, 1024:2048], op0=OP.add, op1=OP.mult,
                                            accum_out=s_ln[:, 3:4]))
            vsync(v, v.tensor_tensor(s_ln[:, 9:10], s_ln[:, 0:1], s_ln[:, 1:2], op=OP.add))
            vsync(v, v.tensor_tensor(s_ln[:, 10:11], s_ln[:, 2:3], s_ln[:, 3:4], op=OP.add))
            vsync(v, v.tensor_scalar(s_ln[:, 9:10], s_ln[:, 9:10], 1.0 / D, None, op0=OP.mult))
            vsync(v, v.tensor_scalar(s_ln[:, 11:12], s_ln[:, 10:11], 1.0 / D, None, op0=OP.mult))
            vsync(v, v.tensor_tensor(s_ln[:, 12:13], s_ln[:, 9:10], s_ln[:, 9:10], op=OP.mult))
            sV.inc(v.tensor_tensor(s_ln[:, 11:12], s_ln[:, 11:12], s_ln[:, 12:13],
                                   op=OP.subtract))
            env.mark(sV, f"ln2v_{ch}")
            env.wait(v, sS, f"ln2sq_{ch}")
            vsync(v, v.reciprocal(s_ln[:, 13:14], s_ln[:, 13:14]))
            vsync(v, v.tensor_tensor(s_ln[:, 14:15], s_ln[:, 9:10], s_ln[:, 13:14], op=OP.mult))
            vsync(v, v.tensor_scalar(s_ln[:, 14:15], s_ln[:, 14:15], -1.0, None, op0=OP.mult))
            sV.inc(v.tensor_scalar(ysl, ysl, s_ln[:, 13:14], s_ln[:, 14:15],
                                   op0=OP.mult, op1=OP.add))
            env.mark(sV, f"out_{ch}")

    progs = [prog_gpsimd, prog_sync, prog_scalar, prog_tensor, prog_vector]
    env.plan = True
    for f in progs:
        f(_PlanEng())
    env.plan = False
    for c in ctrs.values():
        c.v = 0
    A.finalize(es)
    with es:
        for c in ctrs.values():
            c.sem = es.enter_context(nc.semaphore(c.name))
        block = es.enter_context(nc.Block())
        block.gpsimd(prog_gpsimd)
        block.sync(prog_sync)
        block.scalar(prog_scalar)
        block.tensor(prog_tensor)
        block.vector(prog_vector)
    nc._arena_map = {h.name: (off, list(h.shape), h.dtype) for h, off in A.tensors}
    return nc


def _prep_inputs(L0_attn_X, L0_attn_W_Q, L0_attn_hh_b, L0_attn_gn_weight,
                 L0_attn_gn_bias, L0_ffn_w_gate, L0_ffn_w_up, L0_ffn_w_down):
    X = np.ascontiguousarray(np.asarray(L0_attn_X, dtype=np.float32)[0])
    xfT = np.ascontiguousarray(X.T)
    WQ = np.asarray(L0_attn_W_Q, dtype=np.float32)
    wgT = np.asarray(L0_ffn_w_gate, dtype=np.float32).T
    wuT = np.asarray(L0_ffn_w_up, dtype=np.float32).T
    wdT = np.asarray(L0_ffn_w_down, dtype=np.float32).T
    wg_t = np.ascontiguousarray(wgT.reshape(NI, 128, NOT, 128).transpose(2, 1, 0, 3)
                                ).reshape(NOT, 128, NI * 128)
    wu_t = np.ascontiguousarray(wuT.reshape(NI, 128, NOT, 128).transpose(2, 1, 0, 3)
                                ).reshape(NOT, 128, NI * 128)
    wd_t = np.ascontiguousarray(wdT.reshape(NOT, 128, NDT, 128).transpose(2, 1, 0, 3)
                                ).reshape(NDT, 128, NOT * 128)
    hh = np.asarray(L0_attn_hh_b, dtype=np.float32)
    gwv = np.asarray(L0_attn_gn_weight, dtype=np.float32)
    gbv = np.asarray(L0_attn_gn_bias, dtype=np.float32)
    in_maps = []
    for k in range(NC):
        cols = slice(C * k, C * (k + 1))
        wqs_k = np.ascontiguousarray(
            WQ[:, cols].reshape(NI, 128, C).transpose(1, 0, 2)).reshape(128, NI * C)
        q = 256 * k + np.arange(256)
        rows = (4 * q[:, None] + np.arange(4)[None, :]).reshape(-1)
        xres_k = np.ascontiguousarray(X[rows].reshape(256, 4 * D))
        vecs_k = np.zeros((16, 128), dtype=np.float32)
        vecs_k[0] = hh[cols][0:128]
        vecs_k[1] = hh[cols][128:256]
        vecs_k[2] = gwv[cols][0:128]
        vecs_k[3] = gwv[cols][128:256]
        vecs_k[4] = gbv[cols][0:128]
        vecs_k[5] = gbv[cols][128:256]
        vecs_k[6, :] = 1.0 if k == 0 else 0.0
        vecs_k[7, :] = np.arange(128) // 2
        c64 = float(10000.0 ** (-1.0 / 32.0))
        c_hi = np.float32(c64)
        vecs_k[8, :] = c_hi
        vecs_k[9, :] = np.float32(c64 - float(c_hi))
        jj = np.arange(128) // 2
        for i in range(5):
            vecs_k[10 + i, :] = (jj >> i) & 1
        in_maps.append({"xf": xfT, "xres": xres_k, "wqs": wqs_k,
                        "wg": wg_t, "wu": wu_t, "wd": wd_t, "vecs": vecs_k})
    return in_maps


def kernel(L0_attn_X, L0_attn_S_n, L0_attn_W_Q, L0_attn_hh_b, L0_attn_gn_weight,
           L0_attn_gn_bias, L0_ffn_w_gate, L0_ffn_w_up, L0_ffn_w_down):
    in_maps = _prep_inputs(L0_attn_X, L0_attn_W_Q, L0_attn_hh_b, L0_attn_gn_weight,
                           L0_attn_gn_bias, L0_ffn_w_gate, L0_ffn_w_up, L0_ffn_w_down)
    nc = build_nc()
    res = run_bass_kernel_spmd(nc, in_maps, core_ids=list(range(NC)))
    outp = np.empty((1, T, D), dtype=np.float32)
    for k in range(NC):
        o = np.asarray(res.results[k]["out"], dtype=np.float32).reshape(256, 4, D)
        q = 256 * k + np.arange(256)
        rows = (4 * q[:, None] + np.arange(4)[None, :]).reshape(-1)
        outp[0, rows] = o.reshape(1024, D)
    return outp


# version 12 (final)
# speedup vs baseline: 1.1141x; 1.1141x over previous
import numpy as np
import concourse.bass as bass
import concourse.mybir as mybir
from concourse.bass_utils import run_bass_kernel_spmd
from contextlib import ExitStack

F32 = mybir.dt.float32
BF16 = mybir.dt.bfloat16
I32 = mybir.dt.int32
AX = mybir.AxisListType.X
OP = mybir.AluOpType
AF = mybir.ActivationFunctionType

D, T, C, FF, R, NC = 2048, 8192, 256, 5504, 1024, 8
EPS = 1e-5
NSUP, SUP, NI = 16, 512, 16
NOT, NDT = FF // 128, D // 128
PHASES = 4
DEBUG_TAPS = False
DEBUG_TAP1 = False

_tp = 2.0 * np.pi
TP_HI = float(np.float32(_tp))
TP_MD = float(np.float32(_tp - TP_HI))
TP_LO = float(_tp - TP_HI - float(np.float32(_tp - TP_HI)))
INV2PI = float(np.float32(1.0 / _tp))
_lc = np.log(10000.0) / 32.0
LC_HI = float(np.float32(_lc))
LC_LO = float(_lc - LC_HI)
PIH = float(np.float32(np.pi / 2))

SLABS = []
for j in range(NOT):
    SLABS.append(("g", j, 0, NI))
    SLABS.append(("u", j, 0, NI))
for j in range(NDT):
    for (c0, c1) in ((0, 16), (16, 32), (32, NOT)):
        SLABS.append(("d", j, c0, c1))
NSLAB = len(SLABS)
NGU = 2 * NOT


class _PlanInstr:
    def then_inc(self, sem, n):
        return self


class _PlanEng:
    def __getattr__(self, name):
        def f(*a, **k):
            return _PlanInstr()
        return f


class Ctr:
    def __init__(self, name):
        self.name, self.v, self.sem = name, 0, None
    def inc(self, instr, n=1):
        if self.sem is not None:
            instr.then_inc(self.sem, n)
        self.v += n
        return self.v


class Env:
    def __init__(self):
        self.M = {}
        self.plan = True
    def mark(self, ctr, name):
        if self.plan:
            self.M[name] = ctr.v
        else:
            assert self.M.get(name) == ctr.v, (name, self.M.get(name), ctr.v)
    def wait(self, eng, ctr, name):
        if not self.plan:
            eng.wait_ge(ctr.sem, self.M[name])


class Arena:
    def __init__(self, nc):
        self.nc = nc
        self.tensors = []
        self.cur = 0
        self.hi = 0
        self.saves = {}
    def alloc(self, name, shape, dtype):
        nbytes = int(np.prod(shape[1:])) * mybir.dt.size(dtype)
        off = (self.cur + 31) // 32 * 32
        self.cur = off + nbytes
        self.hi = max(self.hi, self.cur)
        h = self.nc.virtual_sbuf_tensor(name, list(shape), dtype)
        self.tensors.append((h, off))
        return h
    def save(self, key):
        self.saves[key] = self.cur
    def restore(self, key):
        self.cur = self.saves[key]
    def finalize(self, es):
        backing = es.enter_context(
            self.nc.sbuf_tensor("arena", [128, self.hi], mybir.dt.uint8))
        base = self.nc.lookup_mloc(backing).addr
        for h, off in self.tensors:
            self.nc._tensor(h.name, list(h.shape), h.dtype, type="SB")
            mloc = self.nc.lookup_mloc(h)
            mloc.allocated = not self.nc.target_bir_lowering
            mloc.addr = base + off


def build_nc():
    nc = bass.Bass("TRN2", target_bir_lowering=False, debug=False,
                   detect_race_conditions=False)
    xf = nc.dram_tensor("xf", [D, T], F32, kind="ExternalInput").ap()
    xres = nc.dram_tensor("xres", [C, T], F32, kind="ExternalInput").ap()
    wqs = nc.dram_tensor("wqs", [128, NI * C], F32, kind="ExternalInput").ap()
    wg = nc.dram_tensor("wg", [NOT, 128, NI * 128], F32, kind="ExternalInput").ap()
    wu = nc.dram_tensor("wu", [NOT, 128, NI * 128], F32, kind="ExternalInput").ap()
    wd = nc.dram_tensor("wd", [NDT, 128, NOT * 128], F32, kind="ExternalInput").ap()
    vecs = nc.dram_tensor("vecs", [16, 128], F32, kind="ExternalInput").ap()
    out = nc.dram_tensor("out", [C, T], F32, kind="ExternalOutput").ap()
    gn32 = nc.dram_tensor("gnspill", [2, 128, T], F32, kind="Internal").ap()
    y1d = nc.dram_tensor("y1spill", [C, T], F32,
                         kind="ExternalOutput" if DEBUG_TAPS else "Internal").ap()
    if DEBUG_TAPS:
        d_y1t = nc.dram_tensor("d_y1t", [128, NDT * R], BF16, kind="ExternalOutput").ap()
    if DEBUG_TAP1:
        d_t1 = nc.dram_tensor("d_t1", [128, 4 * 512], F32, kind="ExternalOutput").ap()
        d_t3 = nc.dram_tensor("d_t3", [2, 1024], F32, kind="ExternalOutput").ap()
        d_t5 = nc.dram_tensor("d_t5", [2, 2048], BF16, kind="ExternalOutput").ap()
        d_t6 = nc.dram_tensor("d_t6", [128, 16], F32, kind="ExternalOutput").ap()
        d_t7 = nc.dram_tensor("d_t7", [64, 1024], F32, kind="ExternalOutput").ap()
        d_hb = nc.dram_tensor("d_hb", [128, NOT * R], BF16, kind="ExternalOutput").ap()
        d_t8 = nc.dram_tensor("d_t8", [128, 1024], F32, kind="ExternalOutput").ap()
        d_t9 = nc.dram_tensor("d_t9", [1, 2048], F32, kind="ExternalOutput").ap()
        d_t10 = nc.dram_tensor("d_t10", [1, 1024], F32, kind="ExternalOutput").ap()
        d_t11 = nc.dram_tensor("d_t11", [128, 32], F32, kind="ExternalOutput").ap()

    es = ExitStack()
    A = Arena(nc)

    s_ramp = A.alloc("s_ramp", [128, 128], F32)
    s_idf = A.alloc("s_idf", [128, 128], F32)
    s_idb = A.alloc("s_idb", [128, 128], BF16)
    s_pt = A.alloc("s_pt", [128, 128], BF16)
    s_ones05 = A.alloc("s_ones05", [128, 128], BF16)
    s_onesf = A.alloc("s_onesf", [128, 128], F32)
    s_onesr = A.alloc("s_onesr", [2, 128], BF16)
    s_sel = A.alloc("s_sel", [2, 128], BF16)
    s_g2 = A.alloc("s_g2", [128, 2], BF16)
    s_cc = A.alloc("s_cc", [128, 8], F32)
    s_hh = A.alloc("s_hh", [128, 2], F32)
    s_gw = A.alloc("s_gw", [128, 2], F32)
    s_gb = A.alloc("s_gb", [128, 2], F32)
    s_iv = A.alloc("s_iv", [128, 8], F32)
    s_vec = A.alloc("s_vec", [16, 128], F32)
    s_vcols = A.alloc("s_vcols", [128, 16], F32)
    s_amc = A.alloc("s_amc", [128, 32], F32)
    s_ln = A.alloc("s_ln", [128, 16], F32)
    s_ivr = A.alloc("s_ivr", [1, 128], F32)
    s_rhl = A.alloc("s_rhl", [1, 2 * R], BF16)
    s_H0 = A.alloc("s_H0", [128, T], BF16)
    _H0_off = A.tensors[-1][1]
    s_H1 = A.alloc("s_H1", [128, T], BF16)
    s_y1t = A.alloc("s_y1t", [128, NDT * R], BF16)
    _y1t_off = A.tensors[-1][1]
    s_wq = A.alloc("s_wq", [128, NI * C], BF16)
    s_W0 = A.alloc("s_W0", [128, 1024], F32)
    s_W1 = A.alloc("s_W1", [128, 1024], F32)
    s_W2 = A.alloc("s_W2", [128, 1024], F32)
    s_qi = A.alloc("s_qi", [128, 1024], I32)
    _qi_off = A.tensors[-1][1]
    s_wsl = A.alloc("s_wsl", [128, 2 * 1024], F32)
    s_wab = A.alloc("s_wab", [128, 1024], F32)
    s_e1 = A.alloc("s_e1", [128, 1024], F32)
    s_rs1 = A.alloc("s_rs1", [128, R], F32)
    A.save("hb")
    s_hb = A.alloc("s_hb", [128, NOT * R], BF16)
    hb_end = A.cur
    A.restore("hb")
    s_xf = A.alloc("s_xf", [128, 2 * NI * SUP], BF16)
    s_xr = A.alloc("s_xr", [128, D], F32)
    s_y1q = A.alloc("s_y1q", [128, D], BF16)
    s_tv = A.alloc("s_tv", [1, SUP], F32)
    s_tbl = A.alloc("s_tbl", [64, 2 * SUP], F32)
    s_tbi = A.alloc("s_tbi", [64, SUP], I32)
    s_tbn = A.alloc("s_tbn", [64, 2 * SUP], F32)
    s_cosb = A.alloc("s_cosb", [64, 2 * SUP], F32)
    _cosb_off = A.tensors[-1][1]
    s_sinb = A.alloc("s_sinb", [64, 2 * SUP], F32)
    s_qraw = A.alloc("s_qraw", [128, 2 * SUP], F32)
    s_qbf = A.alloc("s_qbf", [64, SUP], BF16)
    s_hbf = A.alloc("s_hbf", [128, 2 * SUP], BF16)
    s_st = A.alloc("s_st", [2, 2 * SUP], F32)
    s_mhl = A.alloc("s_mhl", [2, 2 * SUP], BF16)
    s_rhl2 = A.alloc("s_rhl2", [2, 2 * SUP], BF16)
    s_srow = A.alloc("s_srow", [2, SUP], F32)
    assert A.cur <= hb_end, (A.cur, hb_end)
    A.cur = hb_end
    A.hi = max(A.hi, A.cur)
    s_gn32 = nc.virtual_sbuf_tensor("s_gn32", [128, 2 * D], F32)
    A.tensors.append((s_gn32, _H0_off))
    s_y1s = nc.virtual_sbuf_tensor("s_y1s", [128, D], F32)
    A.tensors.append((s_y1s, _cosb_off))
    s_y1r = nc.virtual_sbuf_tensor("s_y1r", [128, D], BF16)
    A.tensors.append((s_y1r, _qi_off))
    s_ych = nc.virtual_sbuf_tensor("s_ych", [128, 2 * D], F32)
    A.tensors.append((s_ych, _y1t_off))

    p_acc = es.enter_context(nc.psum_tensor("p_acc", [128, 1024], F32))
    p_ub = es.enter_context(nc.psum_tensor("p_ub", [128, 1024], F32))
    p_st2 = es.enter_context(nc.psum_tensor("p_st2", [128, 1024], F32))
    p_x = es.enter_context(nc.psum_tensor("p_x", [128, 512], F32))
    p_tr = es.enter_context(nc.psum_tensor("p_tr", [128, 1024], BF16))

    env = Env()
    names = ["sLD", "sLG", "sST", "sV", "sS", "sT", "sG", "sXF", "sWSL", "sXR", "sSP", "sVX", "sGN", "sRL", "sW2"]
    ctrs = {n: Ctr(n) for n in names}
    (sLD, sLG, sST, sV, sS, sT, sG, sXF, sWSL, sXR, sSP, sVX, sGN, sRL, sW2) = (ctrs[n] for n in names)

    def vsync(v, instr):
        sVX.inc(instr, 1)
        if not env.plan:
            v.wait_ge(sVX.sem, sVX.v)
        return instr

    HALVES = []
    for sl in range(NSLAB):
        kind, j, c0, c1 = SLABS[sl]
        w = (c1 - c0) * 128
        nh = (w + 1023) // 1024
        for h2 in range(nh):
            HALVES.append((sl, h2, min(1024, w - h2 * 1024)))
    NHALF = len(HALVES)
    HIDX = {}
    for gi, (sl, h2, hw) in enumerate(HALVES):
        HIDX[(sl, h2)] = gi
    NHOF = {}
    for sl in range(NSLAB):
        kind, j, c0, c1 = SLABS[sl]
        NHOF[sl] = ((c1 - c0) * 128 + 1023) // 1024
    if PHASES >= 4:
        HACT = HALVES
    elif PHASES >= 3:
        HACT = [hh2 for hh2 in HALVES if SLABS[hh2[0]][0] in "gu"]
    else:
        HACT = []

    def wq_slab_ap(wslot, off, width):
        return s_wq[:, wslot * 2048 + off:wslot * 2048 + off + width]

    def prog_gpsimd(g):
        g.iota(s_ramp[:], pattern=[[-1, 128]], base=0, channel_multiplier=1,
               allow_small_or_imprecise_dtypes=True)
        g.memset(s_ones05[:], 0.5 / 128.0)
        g.memset(s_onesf[:], 0.5 / 128.0)
        g.memset(s_onesr[:], 1.0)
        g.iota(s_srow[:, 0:128], pattern=[[-1, 2], [0, 64]], base=0,
               channel_multiplier=1, allow_small_or_imprecise_dtypes=True)
        g.memset(s_g2[:, 0:2], 0.0)
        g.memset(s_g2[0:64, 0:1], 1.0 / 64.0)
        g.memset(s_g2[64:128, 1:2], 1.0 / 64.0)
        g.memset(s_cc[:, 0:1], EPS)
        g.memset(s_cc[:, 1:2], PIH)
        g.memset(s_cc[:, 2:3], EPS / 2)
        g.memset(s_cc[:, 3:4], 1e-5)
        sG.inc(g.memset(s_cc[:, 4:5], 0.0))
        env.mark(sG, "g_setup")
        sLG.inc(g.dma_start(out=s_wq[:], in_=wqs[:]), 16)
        env.mark(sLG, "wq_bf")
        xfr = xf.rearrange("(c p) t -> c p t", p=128)
        for u in range(NSUP):
            env.wait(g, sS, "invf")
            if u >= 1:
                env.wait(g, sT, f"ang_{u-1}")
            sG.inc(g.iota(s_tv[:], pattern=[[1, SUP]], base=u * SUP,
                          channel_multiplier=0,
                          allow_small_or_imprecise_dtypes=True))
            env.mark(sG, f"tv_{u}")
            if u == 1 and DEBUG_TAP1:
                env.wait(g, sV, "vcols")
                sSP.inc(g.dma_start(out=d_t6[:, 0:8], in_=s_vcols[:, 0:8]), 16)
                env.wait(g, sS, "invf")
                sSP.inc(g.dma_start(out=d_t6[:, 8:11], in_=s_iv[:, 1:4]), 16)
                env.wait(g, sS, "tbl_0")
                sSP.inc(g.dma_start(out=d_t7[:, 0:512], in_=s_cosb[:, 0:512]), 16)
                sSP.inc(g.dma_start(out=d_t7[:, 512:1024], in_=s_sinb[:, 0:512]), 16)
                env.wait(g, sS, "qevac_0_0")
                sSP.inc(g.dma_start(out=d_t1[:, 0:512], in_=s_qraw[:, 0:512]), 16)
                env.mark(sSP, "tap1")
                env.wait(g, sS, "H_0_0")
                sSP.inc(g.dma_start(out=d_t1[:, 512:1024], in_=s_qraw[:, 0:512]), 16)
                env.mark(sSP, "tap2")
                env.wait(g, sS, "stev_0_0")
                sSP.inc(g.dma_start(out=d_t3[:], in_=s_st[:]), 16)
                env.mark(sSP, "tap3")
                env.wait(g, sV, "gnapp_0_0")
                sSP.inc(g.dma_start(out=d_t1[:, 1024:1536], in_=s_qraw[:, 0:512]), 16)
                env.mark(sSP, "tap4")
                env.wait(g, sV, "sthl_0_0")
                sSP.inc(g.dma_start(out=d_t5[:, 0:1024], in_=s_mhl[:]), 16)
                sSP.inc(g.dma_start(out=d_t5[:, 1024:2048], in_=s_rhl2[:]), 16)
                env.mark(sSP, "tap5")
            buf = u % 2
            if u >= 2:
                env.wait(g, sT, f"mix_done_{u-2}")
            for q4 in range(4):
                dst = s_xf[:, buf * NI * SUP + q4 * 4 * SUP:buf * NI * SUP + (q4 + 1) * 4 * SUP]
                dst = dst.rearrange("p (c t) -> p c t", c=4)
                src = xfr[q4 * 4:(q4 + 1) * 4, :, u * SUP:(u + 1) * SUP].transpose([1, 0, 2])
                sXF.inc(g.dma_start(out=dst, in_=src), 16)
            env.mark(sXF, f"xf_{u}")
        for ch in range(8 if PHASES >= 2 else 0):
            env.wait(g, sV, f"y1_{ch}")
            qt, s4 = ch // 4, ch % 4
            y1r = y1d.rearrange("(q p) t -> q p t", p=128)
            sSP.inc(g.dma_start(out=y1r[qt, :, s4 * D:(s4 + 1) * D], in_=s_y1s[:]), 16)
            env.mark(sSP, f"y1sp_{ch}")
        if DEBUG_TAPS and PHASES >= 2:
            env.wait(g, sV, "rs1b")
            sSP.inc(g.dma_start(out=d_y1t[:], in_=s_y1t[:]), 16)
            if DEBUG_TAP1 and PHASES >= 4:
                env.wait(g, sV, "htree")
                sSP.inc(g.dma_start(out=d_t8[:], in_=s_W0[:]), 16)
                env.mark(sSP, "tapHT")
                env.wait(g, sV, "s2cols")
                sSP.inc(g.dma_start(out=d_t11[:], in_=s_amc[:, 0:32]), 16)
                env.mark(sSP, "tapS2")
            if PHASES >= 4 and DEBUG_TAP1:
                env.wait(g, sV, "hq_all")
                sSP.inc(g.dma_start(out=d_hb[:], in_=s_hb[:]), 16)
        for gi, (sl, h2, hw) in enumerate(HACT):
            kind, j, c0, c1 = SLABS[sl]
            if gi % 2 != 1:
                continue
            if gi >= 2:
                psl, ph2, _ = HALVES[gi - 2]
                env.wait(g, sV, f"wq_done_{psl}_{ph2}")
            if kind == "d":
                src_ap = wd[j][:, c0 * 128 + h2 * 1024:c0 * 128 + h2 * 1024 + hw]
            else:
                src_ap = (wg if kind == "g" else wu)[j][:, h2 * 1024:h2 * 1024 + hw]
            sW2.inc(g.dma_start(out=s_wsl[:, (gi % 2) * 1024:(gi % 2) * 1024 + hw],
                                in_=src_ap), 16)
            env.mark(sW2, f"wsl_{sl}_{h2}")
        for ch in range(8 if PHASES >= 4 else 0):
            env.wait(g, sV, "hq_all")
            if ch >= 1:
                env.wait(g, sV, f"out_{ch-1}")
            qt, s4 = ch // 4, ch % 4
            y1r = y1d.rearrange("(q p) t -> q p t", p=128)
            sSP.inc(g.dma_start(out=s_y1r[:], in_=y1r[qt, :, s4 * D:(s4 + 1) * D]), 16)
            env.mark(sSP, f"y1rl_{ch}")

    def prog_sync(sync):
        sLD.inc(sync.dma_start(out=s_vec[:], in_=vecs[:]), 16)
        env.mark(sLD, "vec_loads")
        for u in range(NSUP):
            for h in range(2):
                env.wait(sync, sS, f"gnf_{u}_{h}")
                sGN.inc(sync.dma_start(out=gn32[h][:, u * SUP:(u + 1) * SUP],
                                       in_=s_qraw[:, h * SUP:(h + 1) * SUP]), 16)
                env.mark(sGN, f"gnsp_{u}_{h}")
        xrr = xres.rearrange("(q p) t -> q p t", p=128)
        for ch in range(8 if PHASES >= 2 else 0):
            qt, s4 = ch // 4, ch % 4
            if ch >= 2:
                env.wait(sync, sV, f"lnv_{ch-2}")
            sRL.inc(sync.dma_start(out=s_gn32[:, (ch % 2) * D:(ch % 2 + 1) * D],
                                   in_=gn32[qt][:, s4 * D:(s4 + 1) * D]), 16)
            env.mark(sRL, f"gnrl_{ch}")
            if ch >= 1:
                env.wait(sync, sV, f"lnv_{ch-1}")
            sXR.inc(sync.dma_start(out=s_xr[:],
                                   in_=xrr[qt, :, s4 * D:(s4 + 1) * D]), 16)
            env.mark(sXR, f"xr_{ch}")
        for gi, (sl, h2, hw) in enumerate(HACT):
            kind, j, c0, c1 = SLABS[sl]
            if gi % 2 != 0:
                continue
            if gi >= 2:
                psl, ph2, _ = HALVES[gi - 2]
                env.wait(sync, sV, f"wq_done_{psl}_{ph2}")
            if kind == "d":
                src_ap = wd[j][:, c0 * 128 + h2 * 1024:c0 * 128 + h2 * 1024 + hw]
            else:
                src_ap = (wg if kind == "g" else wu)[j][:, h2 * 1024:h2 * 1024 + hw]
            sWSL.inc(sync.dma_start(out=s_wsl[:, (gi % 2) * 1024:(gi % 2) * 1024 + hw],
                                    in_=src_ap), 16)
            env.mark(sWSL, f"wsl_{sl}_{h2}")
        outr = out.rearrange("(q p) t -> q p t", p=128)
        for ch in range(8 if PHASES >= 4 else 0):
            env.wait(sync, sV, f"out_{ch}")
            qt, s4 = ch // 4, ch % 4
            sST.inc(sync.dma_start(out=outr[qt, :, s4 * D:(s4 + 1) * D],
                                   in_=s_ych[:, (ch % 2) * D:(ch % 2 + 1) * D]), 16)
            env.mark(sST, f"ost_{ch}")
        if not env.plan:
            sync.wait_ge(sST.sem, sST.v)
            sync.wait_ge(sSP.sem, sSP.v)

    def prog_scalar(s):
        env.wait(s, sLD, "vec_loads")
        env.wait(s, sG, "g_setup")
        env.wait(s, sV, "v_iv")
        sS.inc(s.activation(s_iv[:, 3:4], s_iv[:, 2:3], AF.Identity))
        env.mark(sS, "invf")
        for u in range(NSUP):
            for h in range(2):
                if h == 0:
                    env.wait(s, sV, f"tblr_{u}")
                    par = u % 2
                    s.activation(s_tbl[:, SUP:2 * SUP], s_tbl[:, 0:SUP], AF.Abs)
                    s.activation(s_sinb[:, par * SUP:(par + 1) * SUP],
                                 s_tbl[:, 0:SUP], AF.Sin)
                    sS.inc(s.activation(s_cosb[:, par * SUP:(par + 1) * SUP],
                                        s_tbl[:, SUP:2 * SUP], AF.Sin,
                                        bias=s_cc[0:64, 1:2], scale=s_cc[0:64, 6:7]))
                    env.mark(sS, f"tbl_{u}")
                qsl = s_qraw[:, h * SUP:(h + 1) * SUP]
                env.wait(s, sT, f"mmacc_{u}_{h}")
                if DEBUG_TAP1 and u == 1 and h == 0:
                    env.wait(s, sSP, "tap4")
                if u >= 1:
                    env.wait(s, sGN, f"gnsp_{u-1}_{h}")
                sS.inc(s.copy(qsl, p_acc[:, h * SUP:(h + 1) * SUP]))
                env.mark(sS, f"qevac_{u}_{h}")
                if h == 0:
                    if DEBUG_TAP1 and u == 0:
                        env.wait(s, sSP, "tap1")
                    sS.inc(s.copy(s_qbf[:], qsl[0:64, :]))
                    env.mark(sS, f"qbf_{u}")
                    env.wait(s, sV, f"blend_{u}")
                s.activation(qsl, qsl, AF.Relu)
                sS.inc(s.activation(qsl, qsl, AF.Tanh, bias=s_vcols[:, h:h + 1]))
                env.mark(sS, f"H_{u}_{h}")
                if u > 0 or h > 0:
                    pu, phh = (u, 0) if h == 1 else (u - 1, 1)
                    env.wait(s, sT, f"stats_{pu}_{phh}")
                if DEBUG_TAP1 and u == 0 and h == 0:
                    env.wait(s, sSP, "tap2")
                s.copy(s_hbf[:, 0:SUP], qsl)
                sS.inc(s.activation(s_hbf[:, SUP:2 * SUP], qsl, AF.Square))
                env.mark(sS, f"Hbf_{u}_{h}")
                env.wait(s, sT, f"stats_{u}_{h}")
                s.copy(s_st[:, 0:SUP], p_st2[0:2, 0:SUP])
                sS.inc(s.copy(s_st[:, SUP:2 * SUP], p_st2[0:2, 512:512 + SUP]))
                env.mark(sS, f"stev_{u}_{h}")
                env.wait(s, sV, f"var_{u}_{h}")
                sS.inc(s.activation(s_st[:, SUP:2 * SUP], s_st[:, SUP:2 * SUP],
                                    AF.Sqrt, bias=s_cc[0:2, 3:4]))
                env.mark(sS, f"sqrt_{u}_{h}")
                env.wait(s, sV, f"gnapp_{u}_{h}")
                sS.inc(s.activation(qsl, qsl, AF.Identity,
                                    bias=s_vcols[:, 4 + h:5 + h], scale=s_vcols[:, 2 + h:3 + h]))
                env.mark(sS, f"gnf_{u}_{h}")
        for ch in range(8 if PHASES >= 2 else 0):
            env.wait(s, sV, f"lnv_{ch}")
            sS.inc(s.activation(s_ln[:, 5:6], s_ln[:, 3:4], AF.Sqrt))
            env.mark(sS, f"lnsq_{ch}")
            for rnd in range(2):
                env.wait(s, sT, f"y1tr_{ch}_{rnd}")
                qt, s4 = ch // 4, ch % 4
                b = qt * 4 + s4
                dst = s_y1t.rearrange("p (n r) -> p n r", n=NDT)[
                    :, rnd * 8:(rnd + 1) * 8, b * 128:(b + 1) * 128]
                sS.inc(s.copy(dst, p_tr.rearrange("p (n r) -> p n r", n=8)))
                env.mark(sS, f"y1tev_{ch}_{rnd}")
        for gi, (sl, h2, hw) in enumerate(HACT):
            kind, j, c0, c1 = SLABS[sl]
            env.wait(s, sWSL if gi % 2 == 0 else sW2, f"wsl_{sl}_{h2}")
            sS.inc(s.activation(s_wab[:, 0:hw],
                                s_wsl[:, (gi % 2) * 1024:(gi % 2) * 1024 + hw],
                                AF.Abs))
            env.mark(sS, f"whi_{sl}_{h2}")
            env.wait(s, sT, f"sclmm_{sl}_{h2}")
            if gi >= 1:
                psl, ph2, _ = HALVES[gi - 1]
                env.wait(s, sV, f"wq_done_{psl}_{ph2}")
            sS.inc(s.activation(s_e1[:, 0:hw], p_st2[:, 0:hw], AF.Identity,
                                bias=s_cc[:, 2:3]))
            env.mark(sS, f"e1_{sl}_{h2}")
            if kind == "u" and h2 == NHOF[sl] - 1:
                env.wait(s, sV, f"gmul_{j}")
                sS.inc(s.activation(s_W0[:], s_W0[:], AF.Silu))
                env.mark(sS, f"silu_{j}")
        for ch in range(8 if PHASES >= 4 else 0):
            env.wait(s, sV, f"ln2v_{ch}")
            sS.inc(s.activation(s_ln[:, 13:14], s_ln[:, 11:12], AF.Sqrt,
                                bias=s_cc[:, 3:4]))
            env.mark(sS, f"ln2sq_{ch}")

    def prog_tensor(t):
        env.wait(t, sG, "g_setup")
        env.wait(t, sV, "idf")
        env.wait(t, sLD, "vec_loads")
        sT.inc(t.matmul(p_x[:, 0:128], s_vec[:], s_idf[0:16, :], start=True, stop=True))
        env.mark(sT, "vectr")
        env.wait(t, sV, "vcols")
        env.wait(t, sV, "v_const")
        env.wait(t, sS, "invf")
        sT.inc(t.matmul(p_x[0:1, 0:128], s_iv[:, 3:4], s_idf[:], start=True, stop=True))
        env.mark(sT, "ivrow")
        env.wait(t, sV, "ivrowev")
        env.wait(t, sG, "tv_0")
        sT.inc(t.matmul(p_x[0:64, 0:SUP], s_ivr[0:1, 0:64], s_tv[:],
                        start=True, stop=True))
        env.mark(sT, "ang_0")
        env.wait(t, sLG, "wq_bf")
        for u in range(NSUP):
            env.wait(t, sXF, f"xf_{u}")
            buf = u % 2
            for h in range(2):
                if u >= 1:
                    env.wait(t, sS, f"qevac_{u-1}_{h}")
                mm = None
                for c in range(NI):
                    lw = s_wq[:, c * C + h * 128:c * C + (h + 1) * 128]
                    rhs = s_xf[:, buf * NI * SUP + c * SUP:buf * NI * SUP + (c + 1) * SUP]
                    mm = t.matmul(p_acc[:, h * SUP:(h + 1) * SUP], lw, rhs,
                                  start=(c == 0), stop=(c == NI - 1))
                sT.inc(mm)
                env.mark(sT, f"mmacc_{u}_{h}")
                if h == 0:
                    env.wait(t, sS, f"qbf_{u}")
                    env.wait(t, sV, f"tblfree_{u}")
                    sT.inc(t.matmul(p_x[0:64, 0:SUP], s_pt[0:64, 0:64], s_qbf[:],
                                    start=True, stop=True))
                    env.mark(sT, f"rot_{u}")
                env.wait(t, sS, f"Hbf_{u}_{h}")
                if u > 0 or h > 0:
                    pu, phh = (u, 0) if h == 1 else (u - 1, 1)
                    env.wait(t, sS, f"stev_{pu}_{phh}")
                t.matmul(p_st2[0:2, 0:SUP], s_g2[:, 0:2], s_hbf[:, 0:SUP],
                         start=True, stop=True)
                sT.inc(t.matmul(p_st2[0:2, 512:512 + SUP], s_g2[:, 0:2],
                                s_hbf[:, SUP:2 * SUP], start=True, stop=True))
                env.mark(sT, f"stats_{u}_{h}")
                env.wait(t, sV, f"sthl_{u}_{h}")
                if u > 0 or h > 0:
                    pu, phh = (u, 0) if h == 1 else (u - 1, 1)
                    env.wait(t, sV, f"gnapp_{pu}_{phh}")
                t.matmul(p_ub[:, 0:SUP], s_sel[0:2, :], s_mhl[:, 0:SUP],
                         start=True, stop=False)
                t.matmul(p_ub[:, 0:SUP], s_sel[0:2, :], s_mhl[:, SUP:2 * SUP],
                         start=False, stop=True)
                t.matmul(p_ub[:, 512:1024], s_sel[0:2, :], s_rhl2[:, 0:SUP],
                         start=True, stop=False)
                sT.inc(t.matmul(p_ub[:, 512:1024], s_sel[0:2, :], s_rhl2[:, SUP:2 * SUP],
                                start=False, stop=True))
                env.mark(sT, f"rep_{u}_{h}")
            if u + 1 < NSUP:
                env.wait(t, sV, f"blend_{u}")
                env.wait(t, sG, f"tv_{u+1}")
                sT.inc(t.matmul(p_x[0:64, 0:SUP], s_ivr[0:1, 0:64], s_tv[:],
                                start=True, stop=True))
                env.mark(sT, f"ang_{u+1}")
            env.mark(sT, f"mix_done_{u}")
        for ch in range(8 if PHASES >= 2 else 0):
            env.wait(t, sV, f"y1q_{ch}")
            for rnd in range(2):
                if not (ch == 0 and rnd == 0):
                    pch, prnd = (ch, 0) if rnd == 1 else (ch - 1, 1)
                    env.wait(t, sS, f"y1tev_{pch}_{prnd}")
                tr = None
                for b8 in range(8):
                    cb = rnd * 8 + b8
                    tr = t.transpose(p_tr[:, b8 * 128:(b8 + 1) * 128],
                                     s_y1q[:, cb * 128:(cb + 1) * 128], s_idb[:])
                sT.inc(tr)
                env.mark(sT, f"y1tr_{ch}_{rnd}")
        if PHASES < 2:
            return
        env.wait(t, sV, "rs1cols")
        for b in range(8):
            sT.inc(t.matmul(p_x[0:1, 0:128], s_amc[:, 8 + b:9 + b], s_idf[:],
                            start=True, stop=True))
            env.mark(sT, f"rs1row_{b}")
            env.wait(t, sV, f"rs1rowev_{b}")
        env.wait(t, sV, "rs1hl")
        t.matmul(p_ub[:, 0:512], s_onesr[0:1, :], s_rhl[0:1, 0:512], start=True, stop=False)
        t.matmul(p_ub[:, 0:512], s_onesr[0:1, :], s_rhl[0:1, R:R + 512], start=False, stop=True)
        t.matmul(p_ub[:, 512:1024], s_onesr[0:1, :], s_rhl[0:1, 512:1024],
                 start=True, stop=False)
        sT.inc(t.matmul(p_ub[:, 512:1024], s_onesr[0:1, :], s_rhl[0:1, R + 512:2 * R],
                        start=False, stop=True))
        env.mark(sT, "rs1rep")

        def emit_sclmm(gi):
            sl, h2, hw = HALVES[gi]
            env.wait(t, sS, f"whi_{sl}_{h2}")
            if gi >= 1:
                psl, ph2, _ = HALVES[gi - 1]
                env.wait(t, sS, f"e1_{psl}_{ph2}")
            mm = None
            for qq in range((hw + 511) // 512):
                qw = min(512, hw - qq * 512)
                off = qq * 512
                mm = t.matmul(p_st2[:, off:off + qw], s_onesf[:],
                              s_wab[:, off:off + qw], start=True, stop=True)
            sT.inc(mm)
            env.mark(sT, f"sclmm_{sl}_{h2}")

        def emit_main(sl):
            kind, j, c0, c1 = SLABS[sl]
            wslot = sl % 2
            if kind == "g":
                if j == 0:
                    env.wait(t, sS, "qevac_15_1")
                else:
                    env.wait(t, sV, f"gmul_{j-1}")
                tgt, rhsbuf, nch, cg0 = p_acc, s_y1t, NI, 0
            elif kind == "u":
                if j == 0:
                    env.wait(t, sV, "rs1b")
                else:
                    env.wait(t, sV, f"umul_{j-1}")
                tgt, rhsbuf, nch, cg0 = p_ub, s_y1t, NI, 0
            else:
                if c0 == 0 and j > 0:
                    env.wait(t, sV, f"dmul_{j-1}")
                env.wait(t, sV, f"hq_{c1-1}")
                tgt, rhsbuf, nch, cg0 = p_acc, s_hb, c1 - c0, c0
            lastc = NI - 1 if kind in "gu" else NOT - 1
            mm = None
            for ci in range(nch):
                c = cg0 + ci
                lw = wq_slab_ap(wslot, ci * 128, 128)
                t.matmul(tgt[:, 0:512], lw, rhsbuf[:, c * R:c * R + 512],
                         start=(c == 0), stop=(c == lastc))
                mm = t.matmul(tgt[:, 512:1024], lw, rhsbuf[:, c * R + 512:(c + 1) * R],
                              start=(c == 0), stop=(c == lastc))
            sT.inc(mm)
            env.mark(sT, f"main_{sl}")

        for gi, (sl, h2, hw) in enumerate(HACT):
            kind, j, c0, c1 = SLABS[sl]
            if kind == "d" and c0 == 0 and h2 == 0 and j == 0:
                env.wait(t, sV, "htree")
                for rnd in range(2):
                    if rnd == 1:
                        env.wait(t, sV, "htr_ev_0")
                    tr = None
                    for b4 in range(4):
                        b = rnd * 4 + b4
                        tr = t.transpose(p_x[:, b4 * 128:(b4 + 1) * 128],
                                         s_h32v[:, b * 128:(b + 1) * 128], s_idf[:])
                    sT.inc(tr)
                    env.mark(sT, f"htr_{rnd}")
                env.wait(t, sV, "s2cols")
                for b in range(8):
                    sT.inc(t.matmul(p_x[0:1, 0:128], s_amc[:, b:b + 1], s_idf[:],
                                    start=True, stop=True))
                    env.mark(sT, f"s2row_{b}")
                    env.wait(t, sV, f"s2rowev_{b}")
                env.wait(t, sV, "s2hl")
                env.wait(t, sV, f"umul_{NOT-1}")
                t.matmul(p_ub[:, 0:512], s_onesr[0:1, :], s_rhl[0:1, 0:512],
                         start=True, stop=False)
                t.matmul(p_ub[:, 0:512], s_onesr[0:1, :], s_rhl[0:1, R:R + 512],
                         start=False, stop=True)
                t.matmul(p_ub[:, 512:1024], s_onesr[0:1, :], s_rhl[0:1, 512:1024],
                         start=True, stop=False)
                sT.inc(t.matmul(p_ub[:, 512:1024], s_onesr[0:1, :],
                                s_rhl[0:1, R + 512:2 * R], start=False, stop=True))
                env.mark(sT, "s2rep")
                for b in range(8):
                    sT.inc(t.matmul(p_x[0:1, 0:128], s_amc[:, 16 + b:17 + b], s_idf[:],
                                    start=True, stop=True))
                    env.mark(sT, f"rs2row_{b}")
                    env.wait(t, sV, f"rs2rowev_{b}")
                env.wait(t, sV, "rs2hl")
                t.matmul(p_x[:, 0:512], s_onesr[0:1, :], s_rhl[0:1, 0:512],
                         start=True, stop=False)
                sT.inc(t.matmul(p_x[:, 0:512], s_onesr[0:1, :], s_rhl[0:1, R:R + 512],
                                start=False, stop=True))
                env.mark(sT, "rs2rep0")
                env.wait(t, sV, "rs2ev0")
                t.matmul(p_x[:, 0:512], s_onesr[0:1, :], s_rhl[0:1, 512:1024],
                         start=True, stop=False)
                sT.inc(t.matmul(p_x[:, 0:512], s_onesr[0:1, :],
                                s_rhl[0:1, R + 512:2 * R], start=False, stop=True))
                env.mark(sT, "rs2rep1")
            emit_sclmm(gi)
            if h2 == NHOF[sl] - 1:
                env.wait(t, sV, f"wq_done_{sl}_{h2}")
                emit_main(sl)
        for ch in range(8 if PHASES >= 4 else 0):
            qt, s4 = ch // 4, ch % 4
            b = qt * 4 + s4
            env.wait(t, sV, f"dmul_{NDT-1}")
            for rnd in range(2):
                if not (ch == 0 and rnd == 0):
                    pch, prnd = (ch, 0) if rnd == 1 else (ch - 1, 1)
                    env.wait(t, sV, f"ytok_{pch}_{prnd}")
                tr = None
                for b8 in range(8):
                    c = rnd * 8 + b8
                    dscb = s_H0 if c < 8 else s_H1
                    cc8 = c % 8
                    tr = t.transpose(p_tr[:, b8 * 128:(b8 + 1) * 128],
                                     dscb[:, cc8 * R + b * 128:cc8 * R + (b + 1) * 128],
                                     s_idb[:])
                sT.inc(tr)
                env.mark(sT, f"outtr_{ch}_{rnd}")

    s_h32v = s_W0
    def prog_vector(v):
        env.wait(v, sG, "g_setup")
        v.tensor_scalar(s_sel[:], s_srow[:, 0:128], 0.0, None, op0=OP.is_equal)
        v.tensor_scalar(s_idf[:], s_ramp[:], 0.0, None, op0=OP.is_equal)
        sV.inc(v.tensor_scalar(s_idb[:], s_ramp[:], 0.0, None, op0=OP.is_equal))
        env.mark(sV, "idf")
        v.tensor_copy(s_iv[:, 1:2], s_ramp[:, 0:1])
        env.wait(v, sT, "vectr")
        sV.inc(v.tensor_copy(s_vcols[:], p_x[:, 0:16]))
        env.mark(sV, "vcols")
        if not env.plan:
            v.wait_ge(sV.sem, sV.v)
        vsync(v, v.scalar_tensor_tensor(s_amc[:, 1:2], s_vcols[:, 7:8], -2.0,
                                        s_iv[:, 1:2], op0=OP.mult, op1=OP.add))
        vsync(v, v.tensor_scalar(s_amc[:, 2:3], s_amc[:, 1:2], -1.0, 1.0,
                                 op0=OP.mult, op1=OP.add))
        vsync(v, v.tensor_copy(s_amc[:, 3:4], s_vcols[:, 7:8]))
        v.tensor_scalar(s_W0[:, 0:128], s_ramp[:], -1.0, None, op0=OP.is_equal)
        v.tensor_scalar(s_W1[:, 0:128], s_ramp[:], 1.0, None, op0=OP.is_equal)
        v.tensor_scalar(s_W0[:, 0:128], s_W0[:, 0:128], s_amc[:, 2:3], None, op0=OP.mult)
        v.tensor_scalar(s_W1[:, 0:128], s_W1[:, 0:128], s_amc[:, 1:2], None, op0=OP.mult)
        v.tensor_tensor(s_pt[:], s_W0[:, 0:128], s_W1[:, 0:128], op=OP.subtract)
        sV.inc(v.memset(s_cc[:, 6:7], -1.0))
        env.mark(sV, "v_const")
        env.wait(v, sLD, "vec_loads")
        vsync(v, v.tensor_tensor(s_amc[:, 4:5], s_vcols[:, 8:9], s_vcols[:, 9:10],
                                 op=OP.add))
        vsync(v, v.tensor_tensor(s_amc[:, 5:6], s_amc[:, 4:5], s_amc[:, 4:5],
                                 op=OP.mult))
        vsync(v, v.tensor_tensor(s_amc[:, 6:7], s_amc[:, 5:6], s_amc[:, 5:6],
                                 op=OP.mult))
        vsync(v, v.tensor_tensor(s_amc[:, 7:8], s_amc[:, 6:7], s_amc[:, 6:7],
                                 op=OP.mult))
        vsync(v, v.tensor_tensor(s_iv[:, 1:2], s_amc[:, 7:8], s_amc[:, 7:8],
                                 op=OP.mult))
        for i, src_c in enumerate([s_amc[:, 4:5], s_amc[:, 5:6], s_amc[:, 6:7],
                                   s_amc[:, 7:8], s_iv[:, 1:2]]):
            vsync(v, v.scalar_tensor_tensor(s_amc[:, 2:3], src_c, -1.0,
                                            s_vcols[:, 10 + i:11 + i],
                                            op0=OP.add, op1=OP.mult))
            vsync(v, v.tensor_scalar(s_amc[:, 2:3], s_amc[:, 2:3], 1.0, None,
                                     op0=OP.add))
            if i == 0:
                last_i = v.tensor_copy(s_iv[:, 2:3], s_amc[:, 2:3])
            else:
                last_i = v.tensor_tensor(s_iv[:, 2:3], s_iv[:, 2:3],
                                         s_amc[:, 2:3], op=OP.mult)
            if i < 4:
                vsync(v, last_i)
        sV.inc(last_i)
        env.mark(sV, "v_iv")
        env.wait(v, sT, "ivrow")
        sV.inc(v.tensor_copy(s_ivr[0:1, :], p_x[0:1, 0:128]))
        env.mark(sV, "ivrowev")
        for u in range(NSUP):
            for h in range(2):
                if h == 0:
                    env.wait(v, sT, f"ang_{u}")
                    sV.inc(v.tensor_scalar(s_tbl[:, 0:SUP], p_x[0:64, 0:SUP],
                                            s_vcols[0:64, 6:7], None, op0=OP.mult))
                    env.mark(sV, f"tblfree_{u}")
                    v.tensor_scalar(s_tbl[:, SUP:2 * SUP], s_tbl[:, 0:SUP], INV2PI,
                                    None, op0=OP.mult)
                    v.tensor_copy(s_tbi[:], s_tbl[:, SUP:2 * SUP])
                    v.tensor_tensor(s_tbl[:, SUP:2 * SUP], s_tbl[:, SUP:2 * SUP],
                                    s_tbi[:], op=OP.subtract)
                    v.tensor_scalar(s_tbn[:, 0:SUP], s_tbl[:, SUP:2 * SUP], 0.5,
                                    None, op0=OP.is_ge)
                    v.tensor_scalar(s_tbn[:, SUP:2 * SUP], s_tbl[:, SUP:2 * SUP],
                                    -0.5, None, op0=OP.is_le)
                    v.tensor_tensor(s_tbn[:, 0:SUP], s_tbn[:, 0:SUP],
                                    s_tbn[:, SUP:2 * SUP], op=OP.subtract)
                    v.tensor_tensor(s_tbn[:, 0:SUP], s_tbi[:], s_tbn[:, 0:SUP],
                                    op=OP.add)
                    v.scalar_tensor_tensor(s_tbl[:, 0:SUP], s_tbn[:, 0:SUP], -TP_HI,
                                           s_tbl[:, 0:SUP], op0=OP.mult, op1=OP.add)
                    v.scalar_tensor_tensor(s_tbl[:, 0:SUP], s_tbn[:, 0:SUP], -TP_MD,
                                           s_tbl[:, 0:SUP], op0=OP.mult, op1=OP.add)
                    sV.inc(v.scalar_tensor_tensor(s_tbl[:, 0:SUP], s_tbn[:, 0:SUP],
                                                  -TP_LO, s_tbl[:, 0:SUP],
                                                  op0=OP.mult, op1=OP.add))
                    env.mark(sV, f"tblr_{u}")
                    env.wait(v, sS, f"qevac_{u}_0")
                    env.wait(v, sT, f"rot_{u}")
                    env.wait(v, sS, f"tbl_{u}")
                    par = u % 2
                    qsl = s_qraw[:, 0:SUP]
                    v.tensor_tensor(s_W2[0:64, 0:SUP], qsl[0:64, :],
                                    s_cosb[:, par * SUP:(par + 1) * SUP], op=OP.mult)
                    v.tensor_tensor(s_W2[0:64, 512:1024], p_x[0:64, 0:SUP],
                                    s_sinb[:, par * SUP:(par + 1) * SUP], op=OP.mult)
                    sV.inc(v.tensor_tensor(qsl[0:64, :], s_W2[0:64, 0:SUP],
                                           s_W2[0:64, 512:1024], op=OP.add))
                    env.mark(sV, f"blend_{u}")
                env.wait(v, sS, f"stev_{u}_{h}")
                if DEBUG_TAP1 and u == 0 and h == 0:
                    env.wait(v, sSP, "tap3")
                mu = s_st[:, 0:SUP]
                m2 = s_st[:, SUP:2 * SUP]
                vsync(v, v.tensor_tensor(s_srow[:, 0:SUP], mu, mu, op=OP.mult))
                sV.inc(v.tensor_tensor(m2, m2, s_srow[:, 0:SUP], op=OP.subtract))
                env.mark(sV, f"var_{u}_{h}")
                env.wait(v, sS, f"sqrt_{u}_{h}")
                vsync(v, v.reciprocal(m2, m2))
                vsync(v, v.tensor_copy(s_mhl[:, 0:SUP], mu))
                vsync(v, v.tensor_tensor(s_srow[:, 0:SUP], mu, s_mhl[:, 0:SUP], op=OP.subtract))
                v.tensor_copy(s_mhl[:, SUP:2 * SUP], s_srow[:, 0:SUP])
                vsync(v, v.tensor_copy(s_rhl2[:, 0:SUP], m2))
                vsync(v, v.tensor_tensor(s_srow[:, 0:SUP], m2, s_rhl2[:, 0:SUP], op=OP.subtract))
                sV.inc(v.tensor_copy(s_rhl2[:, SUP:2 * SUP], s_srow[:, 0:SUP]))
                env.mark(sV, f"sthl_{u}_{h}")
                env.wait(v, sT, f"rep_{u}_{h}")
                if DEBUG_TAP1 and u == 0 and h == 1:
                    env.wait(v, sSP, "tap5")
                env.wait(v, sS, f"H_{u}_{h}")
                qsl = s_qraw[:, h * SUP:(h + 1) * SUP]
                v.tensor_tensor(qsl, qsl, p_ub[:, 0:SUP], op=OP.subtract)
                sV.inc(v.tensor_tensor(qsl, qsl, p_ub[:, 512:512 + SUP], op=OP.mult))
                env.mark(sV, f"gnapp_{u}_{h}")
        for ch in range(8 if PHASES >= 2 else 0):
            qt, s4 = ch // 4, ch % 4
            gsl = s_gn32[:, (ch % 2) * D:(ch % 2 + 1) * D]
            env.wait(v, sRL, f"gnrl_{ch}")
            env.wait(v, sXR, f"xr_{ch}")
            if ch >= 1:
                env.wait(v, sSP, f"y1sp_{ch-1}")
            xsl = s_xr[:]
            v.scalar_tensor_tensor(s_y1s[:], xsl, 0.0, gsl, op0=OP.add, op1=OP.add,
                                   accum_out=s_ln[:, 0:1])
            vsync(v, v.scalar_tensor_tensor(gsl, s_y1s[:], 0.0, s_y1s[:], op0=OP.add,
                                            op1=OP.mult, accum_out=s_ln[:, 1:2]))
            vsync(v, v.tensor_scalar(s_ln[:, 2:3], s_ln[:, 0:1], 1.0 / D, None, op0=OP.mult))
            vsync(v, v.tensor_scalar(s_ln[:, 3:4], s_ln[:, 1:2], 1.0 / D, None, op0=OP.mult))
            vsync(v, v.tensor_tensor(s_ln[:, 4:5], s_ln[:, 2:3], s_ln[:, 2:3], op=OP.mult))
            sV.inc(v.tensor_tensor(s_ln[:, 3:4], s_ln[:, 3:4], s_ln[:, 4:5],
                                   op=OP.subtract))
            env.mark(sV, f"lnv_{ch}")
            env.wait(v, sS, f"lnsq_{ch}")
            vsync(v, v.reciprocal(s_ln[:, 5:6], s_ln[:, 5:6]))
            vsync(v, v.tensor_tensor(s_ln[:, 6:7], s_ln[:, 2:3], s_ln[:, 5:6], op=OP.mult))
            vsync(v, v.tensor_scalar(s_ln[:, 6:7], s_ln[:, 6:7], -1.0, None, op0=OP.mult))
            sV.inc(v.tensor_scalar(s_y1s[:], s_y1s[:], s_ln[:, 5:6], s_ln[:, 6:7],
                                   op0=OP.mult, op1=OP.add))
            env.mark(sV, f"y1_{ch}")
            vsync(v, v.reduce_max(s_ln[:, 7:8], s_y1s[:], axis=AX, apply_absolute_value=True))
            vsync(v, v.tensor_scalar(s_ln[:, 7:8], s_ln[:, 7:8], EPS, None, op0=OP.max))
            vsync(v, v.reciprocal(s_ln[:, 8:9], s_ln[:, 7:8]))
            vsync(v, v.tensor_scalar(s_ln[:, 9:10], s_ln[:, 8:9], 127.0, None, op0=OP.mult))
            v.tensor_scalar(s_amc[:, 8 + ch:9 + ch], s_ln[:, 7:8], 1.0 / 127.0, None,
                            op0=OP.mult)
            if ch >= 1:
                env.wait(v, sT, f"y1tr_{ch-1}_1")
            for h2 in range(2):
                sl1024 = slice(h2 * 1024, (h2 + 1) * 1024)
                v.tensor_scalar(s_W0[:], s_y1s[:, sl1024], s_ln[:, 9:10], None,
                                op0=OP.mult)
                v.tensor_copy(s_qi[:], s_W0[:])
                v.tensor_tensor(s_W0[:], s_W0[:], s_qi[:], op=OP.subtract)
                v.tensor_scalar(s_W1[:], s_W0[:], 0.5, None, op0=OP.is_ge)
                v.tensor_scalar(s_W2[:], s_W0[:], -0.5, None, op0=OP.is_le)
                v.tensor_tensor(s_W1[:], s_W1[:], s_W2[:], op=OP.subtract)
                ql = v.tensor_tensor(s_y1q[:, sl1024], s_qi[:], s_W1[:], op=OP.add)
            sV.inc(ql)
            env.mark(sV, f"y1q_{ch}")
        if PHASES < 2:
            return
        env.mark(sV, "rs1cols")
        for b in range(8):
            env.wait(v, sT, f"rs1row_{b}")
            sV.inc(v.tensor_copy(s_W1[0:1, b * 128:(b + 1) * 128], p_x[0:1, 0:128]))
            env.mark(sV, f"rs1rowev_{b}")
        vsync(v, v.tensor_copy(s_rhl[0:1, 0:R], s_W1[0:1, 0:R]))
        vsync(v, v.tensor_copy(s_W0[0:1, 0:R], s_rhl[0:1, 0:R]))
        vsync(v, v.tensor_tensor(s_W0[0:1, 0:R], s_W1[0:1, 0:R], s_W0[0:1, 0:R],
                                 op=OP.subtract))
        sV.inc(v.tensor_copy(s_rhl[0:1, R:2 * R], s_W0[0:1, 0:R]))
        env.mark(sV, "rs1hl")
        env.wait(v, sT, "rs1rep")
        sV.inc(v.tensor_copy(s_rs1[:], p_ub[:, 0:1024]))
        env.mark(sV, "rs1b")

        for gi, (sl, h2, hw) in enumerate(HACT):
            kind, j, c0, c1 = SLABS[sl]
            env.wait(v, sS, f"e1_{sl}_{h2}")
            wsl = s_wsl[:, (gi % 2) * 1024:(gi % 2) * 1024 + hw]
            v.tensor_tensor(s_W1[:, 0:hw], wsl, s_e1[:, 0:hw], op=OP.is_ge)
            v.scalar_tensor_tensor(s_W2[:, 0:hw], wsl, -1.0, s_e1[:, 0:hw],
                                   op0=OP.mult, op1=OP.is_ge)
            v.tensor_tensor(s_W1[:, 0:hw], s_W1[:, 0:hw], s_W2[:, 0:hw],
                            op=OP.subtract)
            sV.inc(v.scalar_tensor_tensor(
                wq_slab_ap(sl % 2, h2 * 1024, hw), s_W1[:, 0:hw], 2.0, s_e1[:, 0:hw],
                op0=OP.mult, op1=OP.mult))
            env.mark(sV, f"wq_done_{sl}_{h2}")
            if h2 == NHOF[sl] - 1:
                if kind == "u":
                    env.wait(v, sT, f"main_{sl-1}")
                    sV.inc(v.tensor_tensor(s_W0[:], p_acc[:, 0:1024], s_rs1[:],
                                           op=OP.mult))
                    env.mark(sV, f"gmul_{j}")
                    env.wait(v, sT, f"main_{sl}")
                    sV.inc(v.tensor_tensor(s_W1[:], p_ub[:, 0:1024], s_rs1[:],
                                           op=OP.mult))
                    env.mark(sV, f"umul_{j}")
                    env.wait(v, sS, f"silu_{j}")
                    if j == 28 and PHASES >= 2:
                        env.wait(v, sSP, "y1sp_7")
                    sV.inc(v.tensor_tensor(s_hb[:, j * R:(j + 1) * R], s_W0[:],
                                           s_W1[:], op=OP.mult))
                    env.mark(sV, f"h_{j}")
                    if j == NOT - 1 and PHASES >= 4:
                        v.tensor_tensor(s_W0[:], s_hb[:, 0:R], s_hb[:, R:2 * R],
                                        op=OP.max)
                        v.tensor_tensor(s_W1[:], s_hb[:, 0:R], s_hb[:, R:2 * R],
                                        op=OP.min)
                        for c in range(2, NOT):
                            v.tensor_tensor(s_W0[:], s_W0[:], s_hb[:, c * R:(c + 1) * R],
                                            op=OP.max)
                            v.tensor_tensor(s_W1[:], s_W1[:], s_hb[:, c * R:(c + 1) * R],
                                            op=OP.min)
                        v.tensor_scalar(s_W2[:], s_W1[:], -1.0, None, op0=OP.mult)
                        sV.inc(v.tensor_tensor(s_W0[:], s_W0[:], s_W2[:], op=OP.max))
                        env.mark(sV, "htree")
                        if DEBUG_TAP1:
                            env.wait(v, sSP, "tapHT")
                        for rnd in range(2):
                            env.wait(v, sT, f"htr_{rnd}")
                            for b4 in range(4):
                                b = rnd * 4 + b4
                                rm_i = v.reduce_max(s_amc[:, 24 + b:25 + b],
                                                    p_x[:, b4 * 128:(b4 + 1) * 128],
                                                    axis=AX)
                            sV.inc(rm_i)
                            env.mark(sV, f"htr_ev_{rnd}")
                        if not env.plan:
                            v.wait_ge(sV.sem, sV.v)
                        vsync(v, v.tensor_scalar(s_amc[:, 24:32], s_amc[:, 24:32],
                                                 EPS, None, op0=OP.max))
                        for b in range(8):
                            r_i = v.reciprocal(s_amc[:, b:b + 1], s_amc[:, 24 + b:25 + b])
                        vsync(v, r_i)
                        vsync(v, v.tensor_scalar(s_amc[:, 0:8], s_amc[:, 0:8], 127.0,
                                                 None, op0=OP.mult))
                        sV.inc(v.tensor_scalar(s_amc[:, 16:24], s_amc[:, 24:32],
                                               1.0 / 127.0, None, op0=OP.mult))
                        env.mark(sV, "s2cols")
                        if DEBUG_TAP1:
                            env.wait(v, sSP, "tapS2")
                        for b in range(8):
                            env.wait(v, sT, f"s2row_{b}")
                            sV.inc(v.tensor_copy(s_W1[0:1, b * 128:(b + 1) * 128],
                                                 p_x[0:1, 0:128]))
                            env.mark(sV, f"s2rowev_{b}")
                        vsync(v, v.tensor_copy(s_rhl[0:1, 0:R], s_W1[0:1, 0:R]))
                        vsync(v, v.tensor_copy(s_W0[0:1, 0:R], s_rhl[0:1, 0:R]))
                        vsync(v, v.tensor_tensor(s_W0[0:1, 0:R], s_W1[0:1, 0:R],
                                                 s_W0[0:1, 0:R], op=OP.subtract))
                        sV.inc(v.tensor_copy(s_rhl[0:1, R:2 * R], s_W0[0:1, 0:R]))
                        env.mark(sV, "s2hl")
                        for b in range(8):
                            env.wait(v, sT, f"rs2row_{b}")
                            sV.inc(v.tensor_copy(s_W1[0:1, b * 128:(b + 1) * 128],
                                                 p_x[0:1, 0:128]))
                            env.mark(sV, f"rs2rowev_{b}")
                        vsync(v, v.tensor_copy(s_rhl[0:1, 0:R], s_W1[0:1, 0:R]))
                        vsync(v, v.tensor_copy(s_W0[0:1, 0:R], s_rhl[0:1, 0:R]))
                        vsync(v, v.tensor_tensor(s_W0[0:1, 0:R], s_W1[0:1, 0:R],
                                                 s_W0[0:1, 0:R], op=OP.subtract))
                        sV.inc(v.tensor_copy(s_rhl[0:1, R:2 * R], s_W0[0:1, 0:R]))
                        env.mark(sV, "rs2hl")
                        env.wait(v, sT, "rs2rep0")
                        sV.inc(v.tensor_copy(s_rs1[:, 0:512], p_x[:, 0:512]))
                        env.mark(sV, "rs2ev0")
                        env.wait(v, sT, "rs2rep1")
                        sV.inc(v.tensor_copy(s_rs1[:, 512:1024], p_x[:, 0:512]))
                        env.mark(sV, "rs2b")
                        env.wait(v, sT, "s2rep")
                        for c in range(NOT):
                            v.tensor_tensor(s_W0[:], s_hb[:, c * R:(c + 1) * R],
                                            p_ub[:, 0:1024], op=OP.mult)
                            v.tensor_copy(s_qi[:], s_W0[:])
                            v.tensor_tensor(s_W0[:], s_W0[:], s_qi[:], op=OP.subtract)
                            v.tensor_scalar(s_W1[:], s_W0[:], 0.5, None, op0=OP.is_ge)
                            v.tensor_scalar(s_W2[:], s_W0[:], -0.5, None, op0=OP.is_le)
                            v.tensor_tensor(s_W1[:], s_W1[:], s_W2[:], op=OP.subtract)
                            hq_i = v.tensor_tensor(s_hb[:, c * R:(c + 1) * R],
                                                   s_qi[:], s_W1[:], op=OP.add)
                            sV.inc(hq_i)
                            env.mark(sV, f"hq_{c}")
                        env.mark(sV, "hq_all")
                elif kind == "d" and c1 == NOT:
                    env.wait(v, sT, f"main_{sl}")
                    dscb = s_H0 if j < 8 else s_H1
                    jj = j % 8
                    sV.inc(v.tensor_tensor(dscb[:, jj * R:(jj + 1) * R],
                                           p_acc[:, 0:1024], s_rs1[:], op=OP.mult))
                    env.mark(sV, f"dmul_{j}")
        for ch in range(8 if PHASES >= 4 else 0):
            qt, s4 = ch // 4, ch % 4
            if ch >= 2:
                env.wait(v, sST, f"ost_{ch-2}")
            env.wait(v, sSP, f"y1rl_{ch}")
            for rnd in range(2):
                env.wait(v, sT, f"outtr_{ch}_{rnd}")
                sV.inc(v.tensor_tensor(
                    s_ych[:, (ch % 2) * D + rnd * 1024:(ch % 2) * D + (rnd + 1) * 1024],
                    p_tr[:, 0:1024], s_y1r[:, rnd * 1024:(rnd + 1) * 1024], op=OP.add))
                env.mark(sV, f"ytok_{ch}_{rnd}")
            ysl = s_ych[:, (ch % 2) * D:(ch % 2 + 1) * D]
            v.scalar_tensor_tensor(s_W0[:], ysl[:, 0:1024], 0.0, ysl[:, 0:1024],
                                   op0=OP.add, op1=OP.add, accum_out=s_ln[:, 0:1])
            v.scalar_tensor_tensor(s_W0[:], ysl[:, 1024:2048], 0.0, ysl[:, 1024:2048],
                                   op0=OP.add, op1=OP.add, accum_out=s_ln[:, 1:2])
            v.scalar_tensor_tensor(s_W0[:], ysl[:, 0:1024], 0.0, ysl[:, 0:1024],
                                   op0=OP.add, op1=OP.mult, accum_out=s_ln[:, 2:3])
            vsync(v, v.scalar_tensor_tensor(s_W0[:], ysl[:, 1024:2048], 0.0,
                                            ysl[:, 1024:2048], op0=OP.add, op1=OP.mult,
                                            accum_out=s_ln[:, 3:4]))
            vsync(v, v.tensor_tensor(s_ln[:, 9:10], s_ln[:, 0:1], s_ln[:, 1:2], op=OP.add))
            vsync(v, v.tensor_tensor(s_ln[:, 10:11], s_ln[:, 2:3], s_ln[:, 3:4], op=OP.add))
            vsync(v, v.tensor_scalar(s_ln[:, 9:10], s_ln[:, 9:10], 1.0 / D, None, op0=OP.mult))
            vsync(v, v.tensor_scalar(s_ln[:, 11:12], s_ln[:, 10:11], 1.0 / D, None, op0=OP.mult))
            vsync(v, v.tensor_tensor(s_ln[:, 12:13], s_ln[:, 9:10], s_ln[:, 9:10], op=OP.mult))
            sV.inc(v.tensor_tensor(s_ln[:, 11:12], s_ln[:, 11:12], s_ln[:, 12:13],
                                   op=OP.subtract))
            env.mark(sV, f"ln2v_{ch}")
            env.wait(v, sS, f"ln2sq_{ch}")
            vsync(v, v.reciprocal(s_ln[:, 13:14], s_ln[:, 13:14]))
            vsync(v, v.tensor_tensor(s_ln[:, 14:15], s_ln[:, 9:10], s_ln[:, 13:14], op=OP.mult))
            vsync(v, v.tensor_scalar(s_ln[:, 14:15], s_ln[:, 14:15], -1.0, None, op0=OP.mult))
            sV.inc(v.tensor_scalar(ysl, ysl, s_ln[:, 13:14], s_ln[:, 14:15],
                                   op0=OP.mult, op1=OP.add))
            env.mark(sV, f"out_{ch}")

    progs = [prog_gpsimd, prog_sync, prog_scalar, prog_tensor, prog_vector]
    env.plan = True
    for f in progs:
        f(_PlanEng())
    env.plan = False
    for c in ctrs.values():
        c.v = 0
    A.finalize(es)
    with es:
        for c in ctrs.values():
            c.sem = es.enter_context(nc.semaphore(c.name))
        block = es.enter_context(nc.Block())
        block.gpsimd(prog_gpsimd)
        block.sync(prog_sync)
        block.scalar(prog_scalar)
        block.tensor(prog_tensor)
        block.vector(prog_vector)
    nc._arena_map = {h.name: (off, list(h.shape), h.dtype) for h, off in A.tensors}
    return nc


def _prep_inputs(L0_attn_X, L0_attn_W_Q, L0_attn_hh_b, L0_attn_gn_weight,
                 L0_attn_gn_bias, L0_ffn_w_gate, L0_ffn_w_up, L0_ffn_w_down):
    X = np.ascontiguousarray(np.asarray(L0_attn_X, dtype=np.float32)[0])
    xfT = np.ascontiguousarray(X.T)
    WQ = np.asarray(L0_attn_W_Q, dtype=np.float32)
    wgT = np.asarray(L0_ffn_w_gate, dtype=np.float32).T
    wuT = np.asarray(L0_ffn_w_up, dtype=np.float32).T
    wdT = np.asarray(L0_ffn_w_down, dtype=np.float32).T
    wg_t = np.ascontiguousarray(wgT.reshape(NI, 128, NOT, 128).transpose(2, 1, 0, 3)
                                ).reshape(NOT, 128, NI * 128)
    wu_t = np.ascontiguousarray(wuT.reshape(NI, 128, NOT, 128).transpose(2, 1, 0, 3)
                                ).reshape(NOT, 128, NI * 128)
    wd_t = np.ascontiguousarray(wdT.reshape(NOT, 128, NDT, 128).transpose(2, 1, 0, 3)
                                ).reshape(NDT, 128, NOT * 128)
    hh = np.asarray(L0_attn_hh_b, dtype=np.float32)
    gwv = np.asarray(L0_attn_gn_weight, dtype=np.float32)
    gbv = np.asarray(L0_attn_gn_bias, dtype=np.float32)
    in_maps = []
    for k in range(NC):
        cols = slice(C * k, C * (k + 1))
        wqs_k = np.ascontiguousarray(
            WQ[:, cols].reshape(NI, 128, C).transpose(1, 0, 2)).reshape(128, NI * C)
        q = 256 * k + np.arange(256)
        rows = (4 * q[:, None] + np.arange(4)[None, :]).reshape(-1)
        xres_k = np.ascontiguousarray(X[rows].reshape(256, 4 * D))
        vecs_k = np.zeros((16, 128), dtype=np.float32)
        vecs_k[0] = hh[cols][0:128]
        vecs_k[1] = hh[cols][128:256]
        vecs_k[2] = gwv[cols][0:128]
        vecs_k[3] = gwv[cols][128:256]
        vecs_k[4] = gbv[cols][0:128]
        vecs_k[5] = gbv[cols][128:256]
        vecs_k[6, :] = 1.0 if k == 0 else 0.0
        vecs_k[7, :] = np.arange(128) // 2
        c64 = float(10000.0 ** (-1.0 / 32.0))
        c_hi = np.float32(c64)
        vecs_k[8, :] = c_hi
        vecs_k[9, :] = np.float32(c64 - float(c_hi))
        jj = np.arange(128) // 2
        for i in range(5):
            vecs_k[10 + i, :] = (jj >> i) & 1
        in_maps.append({"xf": xfT, "xres": xres_k, "wqs": wqs_k,
                        "wg": wg_t, "wu": wu_t, "wd": wd_t, "vecs": vecs_k})
    return in_maps


def kernel(L0_attn_X, L0_attn_S_n, L0_attn_W_Q, L0_attn_hh_b, L0_attn_gn_weight,
           L0_attn_gn_bias, L0_ffn_w_gate, L0_ffn_w_up, L0_ffn_w_down):
    in_maps = _prep_inputs(L0_attn_X, L0_attn_W_Q, L0_attn_hh_b, L0_attn_gn_weight,
                           L0_attn_gn_bias, L0_ffn_w_gate, L0_ffn_w_up, L0_ffn_w_down)
    nc = build_nc()
    res = run_bass_kernel_spmd(nc, in_maps, core_ids=list(range(NC)))
    outp = np.empty((1, T, D), dtype=np.float32)
    for k in range(NC):
        o = np.asarray(res.results[k]["out"], dtype=np.float32).reshape(256, 4, D)
        q = 256 * k + np.arange(256)
        rows = (4 * q[:, None] + np.arange(4)[None, :]).reshape(-1)
        outp[0, rows] = o.reshape(1024, D)
    return outp


# version 13
# speedup vs baseline: 1.1368x; 1.0203x over previous
import numpy as np
import concourse.bass as bass
import concourse.mybir as mybir
from concourse.bass_utils import run_bass_kernel_spmd
from contextlib import ExitStack

F32 = mybir.dt.float32
BF16 = mybir.dt.bfloat16
I32 = mybir.dt.int32
AX = mybir.AxisListType.X
OP = mybir.AluOpType
AF = mybir.ActivationFunctionType

D, T, C, FF, R, NC = 2048, 8192, 256, 5504, 1024, 8
EPS = 1e-5
NSUP, SUP, NI = 16, 512, 16
NOT, NDT = FF // 128, D // 128
PHASES = 4
DEBUG_TAPS = False
DEBUG_TAP1 = False

_tp = 2.0 * np.pi
TP_HI = float(np.float32(_tp))
TP_MD = float(np.float32(_tp - TP_HI))
TP_LO = float(_tp - TP_HI - float(np.float32(_tp - TP_HI)))
INV2PI = float(np.float32(1.0 / _tp))
_lc = np.log(10000.0) / 32.0
LC_HI = float(np.float32(_lc))
LC_LO = float(_lc - LC_HI)
PIH = float(np.float32(np.pi / 2))

SLABS = []
for j in range(NOT):
    SLABS.append(("g", j, 0, NI))
    SLABS.append(("u", j, 0, NI))
for j in range(NDT):
    for (c0, c1) in ((0, 16), (16, 32), (32, NOT)):
        SLABS.append(("d", j, c0, c1))
NSLAB = len(SLABS)
NGU = 2 * NOT


class _PlanInstr:
    def then_inc(self, sem, n):
        return self


class _PlanEng:
    def __getattr__(self, name):
        def f(*a, **k):
            return _PlanInstr()
        return f


class Ctr:
    def __init__(self, name):
        self.name, self.v, self.sem = name, 0, None
    def inc(self, instr, n=1):
        if self.sem is not None:
            instr.then_inc(self.sem, n)
        self.v += n
        return self.v


class Env:
    def __init__(self):
        self.M = {}
        self.plan = True
    def mark(self, ctr, name):
        if self.plan:
            self.M[name] = ctr.v
        else:
            assert self.M.get(name) == ctr.v, (name, self.M.get(name), ctr.v)
    def wait(self, eng, ctr, name):
        if not self.plan:
            eng.wait_ge(ctr.sem, self.M[name])


class Arena:
    def __init__(self, nc):
        self.nc = nc
        self.tensors = []
        self.cur = 0
        self.hi = 0
        self.saves = {}
    def alloc(self, name, shape, dtype):
        nbytes = int(np.prod(shape[1:])) * mybir.dt.size(dtype)
        off = (self.cur + 31) // 32 * 32
        self.cur = off + nbytes
        self.hi = max(self.hi, self.cur)
        h = self.nc.virtual_sbuf_tensor(name, list(shape), dtype)
        self.tensors.append((h, off))
        return h
    def save(self, key):
        self.saves[key] = self.cur
    def restore(self, key):
        self.cur = self.saves[key]
    def finalize(self, es):
        backing = es.enter_context(
            self.nc.sbuf_tensor("arena", [128, self.hi], mybir.dt.uint8))
        base = self.nc.lookup_mloc(backing).addr
        for h, off in self.tensors:
            self.nc._tensor(h.name, list(h.shape), h.dtype, type="SB")
            mloc = self.nc.lookup_mloc(h)
            mloc.allocated = not self.nc.target_bir_lowering
            mloc.addr = base + off


def build_nc():
    nc = bass.Bass("TRN2", target_bir_lowering=False, debug=False,
                   detect_race_conditions=False)
    xf = nc.dram_tensor("xf", [D, T], F32, kind="ExternalInput").ap()
    xres = nc.dram_tensor("xres", [C, T], F32, kind="ExternalInput").ap()
    wqs = nc.dram_tensor("wqs", [128, NI * C], F32, kind="ExternalInput").ap()
    wg = nc.dram_tensor("wg", [NOT, 128, NI * 128], F32, kind="ExternalInput").ap()
    wu = nc.dram_tensor("wu", [NOT, 128, NI * 128], F32, kind="ExternalInput").ap()
    wd = nc.dram_tensor("wd", [NDT, 128, NOT * 128], F32, kind="ExternalInput").ap()
    vecs = nc.dram_tensor("vecs", [16, 128], F32, kind="ExternalInput").ap()
    out = nc.dram_tensor("out", [C, T], F32, kind="ExternalOutput").ap()
    gn32 = nc.dram_tensor("gnspill", [2, 128, T], F32, kind="Internal").ap()
    y1d = nc.dram_tensor("y1spill", [C, T], F32,
                         kind="ExternalOutput" if DEBUG_TAPS else "Internal").ap()
    if DEBUG_TAPS:
        d_y1t = nc.dram_tensor("d_y1t", [128, NDT * R], BF16, kind="ExternalOutput").ap()
    if DEBUG_TAP1:
        d_t1 = nc.dram_tensor("d_t1", [128, 4 * 512], F32, kind="ExternalOutput").ap()
        d_t3 = nc.dram_tensor("d_t3", [2, 1024], F32, kind="ExternalOutput").ap()
        d_t5 = nc.dram_tensor("d_t5", [2, 2048], BF16, kind="ExternalOutput").ap()
        d_t6 = nc.dram_tensor("d_t6", [128, 16], F32, kind="ExternalOutput").ap()
        d_t7 = nc.dram_tensor("d_t7", [64, 1024], F32, kind="ExternalOutput").ap()
        d_hb = nc.dram_tensor("d_hb", [128, NOT * R], BF16, kind="ExternalOutput").ap()
        d_t8 = nc.dram_tensor("d_t8", [128, 1024], F32, kind="ExternalOutput").ap()
        d_t9 = nc.dram_tensor("d_t9", [1, 2048], F32, kind="ExternalOutput").ap()
        d_t10 = nc.dram_tensor("d_t10", [1, 1024], F32, kind="ExternalOutput").ap()
        d_t11 = nc.dram_tensor("d_t11", [128, 32], F32, kind="ExternalOutput").ap()

    es = ExitStack()
    A = Arena(nc)

    s_ramp = A.alloc("s_ramp", [128, 128], F32)
    s_idf = A.alloc("s_idf", [128, 128], F32)
    s_idb = A.alloc("s_idb", [128, 128], BF16)
    s_pt = A.alloc("s_pt", [128, 128], BF16)
    s_ones05 = A.alloc("s_ones05", [128, 128], BF16)
    s_onesf = A.alloc("s_onesf", [128, 128], F32)
    s_onesr = A.alloc("s_onesr", [2, 128], BF16)
    s_sel = A.alloc("s_sel", [2, 128], BF16)
    s_g2 = A.alloc("s_g2", [128, 2], BF16)
    s_cc = A.alloc("s_cc", [128, 8], F32)
    s_hh = A.alloc("s_hh", [128, 2], F32)
    s_gw = A.alloc("s_gw", [128, 2], F32)
    s_gb = A.alloc("s_gb", [128, 2], F32)
    s_iv = A.alloc("s_iv", [128, 8], F32)
    s_vec = A.alloc("s_vec", [16, 128], F32)
    s_vcols = A.alloc("s_vcols", [128, 16], F32)
    s_amc = A.alloc("s_amc", [128, 32], F32)
    s_ln = A.alloc("s_ln", [128, 16], F32)
    s_ivr = A.alloc("s_ivr", [1, 128], F32)
    s_rhl = A.alloc("s_rhl", [1, 2 * R], BF16)
    s_H0 = A.alloc("s_H0", [128, T], BF16)
    _H0_off = A.tensors[-1][1]
    s_H1 = A.alloc("s_H1", [128, T], BF16)
    s_y1t = A.alloc("s_y1t", [128, NDT * R], BF16)
    _y1t_off = A.tensors[-1][1]
    s_wq = A.alloc("s_wq", [128, NI * C], BF16)
    s_W0 = A.alloc("s_W0", [128, 1024], F32)
    s_W1 = A.alloc("s_W1", [128, 1024], F32)
    s_W2 = A.alloc("s_W2", [128, 1024], F32)
    s_qi = A.alloc("s_qi", [128, 1024], I32)
    _qi_off = A.tensors[-1][1]
    s_wsl = A.alloc("s_wsl", [128, 2 * 1024], F32)
    s_wab = A.alloc("s_wab", [128, 1024], F32)
    s_e1 = A.alloc("s_e1", [128, 1024], F32)
    s_rs1 = A.alloc("s_rs1", [128, R], F32)
    A.save("hb")
    s_hb = A.alloc("s_hb", [128, NOT * R], BF16)
    hb_end = A.cur
    A.restore("hb")
    s_xf = A.alloc("s_xf", [128, 2 * NI * SUP], BF16)
    s_xr = A.alloc("s_xr", [128, D], F32)
    s_y1q = A.alloc("s_y1q", [128, D], BF16)
    s_tv = A.alloc("s_tv", [1, SUP], F32)
    s_tbl = A.alloc("s_tbl", [64, 2 * SUP], F32)
    s_tbi = A.alloc("s_tbi", [64, SUP], I32)
    s_tbn = A.alloc("s_tbn", [64, 2 * SUP], F32)
    s_cosb = A.alloc("s_cosb", [64, 2 * SUP], F32)
    _cosb_off = A.tensors[-1][1]
    s_sinb = A.alloc("s_sinb", [64, 2 * SUP], F32)
    s_qraw = A.alloc("s_qraw", [128, 2 * SUP], F32)
    s_qbf = A.alloc("s_qbf", [64, SUP], BF16)
    s_hbf = A.alloc("s_hbf", [128, 2 * SUP], BF16)
    s_st = A.alloc("s_st", [2, 2 * SUP], F32)
    s_mhl = A.alloc("s_mhl", [2, 2 * SUP], BF16)
    s_rhl2 = A.alloc("s_rhl2", [2, 2 * SUP], BF16)
    s_srow = A.alloc("s_srow", [2, SUP], F32)
    assert A.cur <= hb_end, (A.cur, hb_end)
    A.cur = hb_end
    A.hi = max(A.hi, A.cur)
    s_gn32 = nc.virtual_sbuf_tensor("s_gn32", [128, 2 * D], F32)
    A.tensors.append((s_gn32, _H0_off))
    s_y1s = nc.virtual_sbuf_tensor("s_y1s", [128, D], F32)
    A.tensors.append((s_y1s, _cosb_off))
    s_y1r = nc.virtual_sbuf_tensor("s_y1r", [128, D], BF16)
    A.tensors.append((s_y1r, _qi_off))
    s_ych = nc.virtual_sbuf_tensor("s_ych", [128, 2 * D], F32)
    A.tensors.append((s_ych, _y1t_off))

    p_acc = es.enter_context(nc.psum_tensor("p_acc", [128, 1024], F32))
    p_ub = es.enter_context(nc.psum_tensor("p_ub", [128, 1024], F32))
    p_st2 = es.enter_context(nc.psum_tensor("p_st2", [128, 1024], F32))
    p_x = es.enter_context(nc.psum_tensor("p_x", [128, 512], F32))
    p_tr = es.enter_context(nc.psum_tensor("p_tr", [128, 1024], BF16))

    env = Env()
    names = ["sLD", "sLG", "sST", "sV", "sS", "sT", "sG", "sXF", "sWSL", "sXR", "sSP", "sVX", "sGN", "sRL", "sW2"]
    ctrs = {n: Ctr(n) for n in names}
    (sLD, sLG, sST, sV, sS, sT, sG, sXF, sWSL, sXR, sSP, sVX, sGN, sRL, sW2) = (ctrs[n] for n in names)

    def vsync(v, instr):
        sVX.inc(instr, 1)
        if not env.plan:
            v.wait_ge(sVX.sem, sVX.v)
        return instr

    HALVES = []
    for sl in range(NSLAB):
        kind, j, c0, c1 = SLABS[sl]
        w = (c1 - c0) * 128
        nh = (w + 1023) // 1024
        for h2 in range(nh):
            HALVES.append((sl, h2, min(1024, w - h2 * 1024)))
    NHALF = len(HALVES)
    HIDX = {}
    for gi, (sl, h2, hw) in enumerate(HALVES):
        HIDX[(sl, h2)] = gi
    NHOF = {}
    for sl in range(NSLAB):
        kind, j, c0, c1 = SLABS[sl]
        NHOF[sl] = ((c1 - c0) * 128 + 1023) // 1024
    if PHASES >= 4:
        HACT = HALVES
    elif PHASES >= 3:
        HACT = [hh2 for hh2 in HALVES if SLABS[hh2[0]][0] in "gu"]
    else:
        HACT = []

    def wq_slab_ap(wslot, off, width):
        return s_wq[:, wslot * 2048 + off:wslot * 2048 + off + width]

    def prog_gpsimd(g):
        g.iota(s_ramp[:], pattern=[[-1, 128]], base=0, channel_multiplier=1,
               allow_small_or_imprecise_dtypes=True)
        g.memset(s_ones05[:], 0.5 / 128.0)
        g.memset(s_onesf[:], 0.5 / 128.0)
        g.memset(s_onesr[:], 1.0)
        g.iota(s_srow[:, 0:128], pattern=[[-1, 2], [0, 64]], base=0,
               channel_multiplier=1, allow_small_or_imprecise_dtypes=True)
        g.memset(s_g2[:, 0:2], 0.0)
        g.memset(s_g2[0:64, 0:1], 1.0 / 64.0)
        g.memset(s_g2[64:128, 1:2], 1.0 / 64.0)
        g.memset(s_cc[:, 0:1], EPS)
        g.memset(s_cc[:, 1:2], PIH)
        g.memset(s_cc[:, 2:3], EPS / 2)
        g.memset(s_cc[:, 3:4], 1e-5)
        sG.inc(g.memset(s_cc[:, 4:5], 0.0))
        env.mark(sG, "g_setup")
        sLG.inc(g.dma_start(out=s_wq[:], in_=wqs[:]), 16)
        env.mark(sLG, "wq_bf")
        xfr = xf.rearrange("(c p) t -> c p t", p=128)
        for u in range(NSUP):
            env.wait(g, sS, "invf")
            if u >= 1:
                env.wait(g, sT, f"ang_{u-1}")
            sG.inc(g.iota(s_tv[:], pattern=[[1, SUP]], base=u * SUP,
                          channel_multiplier=0,
                          allow_small_or_imprecise_dtypes=True))
            env.mark(sG, f"tv_{u}")
            if u == 1 and DEBUG_TAP1:
                env.wait(g, sV, "vcols")
                sSP.inc(g.dma_start(out=d_t6[:, 0:8], in_=s_vcols[:, 0:8]), 16)
                env.wait(g, sS, "invf")
                sSP.inc(g.dma_start(out=d_t6[:, 8:11], in_=s_iv[:, 1:4]), 16)
                env.wait(g, sS, "tbl_0")
                sSP.inc(g.dma_start(out=d_t7[:, 0:512], in_=s_cosb[:, 0:512]), 16)
                sSP.inc(g.dma_start(out=d_t7[:, 512:1024], in_=s_sinb[:, 0:512]), 16)
                env.wait(g, sS, "qevac_0_0")
                sSP.inc(g.dma_start(out=d_t1[:, 0:512], in_=s_qraw[:, 0:512]), 16)
                env.mark(sSP, "tap1")
                env.wait(g, sS, "H_0_0")
                sSP.inc(g.dma_start(out=d_t1[:, 512:1024], in_=s_qraw[:, 0:512]), 16)
                env.mark(sSP, "tap2")
                env.wait(g, sS, "stev_0_0")
                sSP.inc(g.dma_start(out=d_t3[:], in_=s_st[:]), 16)
                env.mark(sSP, "tap3")
                env.wait(g, sV, "gnapp_0_0")
                sSP.inc(g.dma_start(out=d_t1[:, 1024:1536], in_=s_qraw[:, 0:512]), 16)
                env.mark(sSP, "tap4")
                env.wait(g, sV, "sthl_0_0")
                sSP.inc(g.dma_start(out=d_t5[:, 0:1024], in_=s_mhl[:]), 16)
                sSP.inc(g.dma_start(out=d_t5[:, 1024:2048], in_=s_rhl2[:]), 16)
                env.mark(sSP, "tap5")
            buf = u % 2
            if u >= 2:
                env.wait(g, sT, f"mix_done_{u-2}")
            for q4 in range(4):
                dst = s_xf[:, buf * NI * SUP + q4 * 4 * SUP:buf * NI * SUP + (q4 + 1) * 4 * SUP]
                dst = dst.rearrange("p (c t) -> p c t", c=4)
                src = xfr[q4 * 4:(q4 + 1) * 4, :, u * SUP:(u + 1) * SUP].transpose([1, 0, 2])
                sXF.inc(g.dma_start(out=dst, in_=src), 16)
            env.mark(sXF, f"xf_{u}")
        for ch in range(8 if PHASES >= 2 else 0):
            env.wait(g, sV, f"y1_{ch}")
            qt, s4 = ch // 4, ch % 4
            y1r = y1d.rearrange("(q p) t -> q p t", p=128)
            sSP.inc(g.dma_start(out=y1r[qt, :, s4 * D:(s4 + 1) * D], in_=s_y1s[:]), 16)
            env.mark(sSP, f"y1sp_{ch}")
        if DEBUG_TAPS and PHASES >= 2:
            env.wait(g, sV, "rs1b")
            sSP.inc(g.dma_start(out=d_y1t[:], in_=s_y1t[:]), 16)
            if DEBUG_TAP1 and PHASES >= 4:
                env.wait(g, sV, "htree")
                sSP.inc(g.dma_start(out=d_t8[:], in_=s_W0[:]), 16)
                env.mark(sSP, "tapHT")
                env.wait(g, sV, "s2cols")
                sSP.inc(g.dma_start(out=d_t11[:], in_=s_amc[:, 0:32]), 16)
                env.mark(sSP, "tapS2")
            if PHASES >= 4 and DEBUG_TAP1:
                env.wait(g, sV, "hq_all")
                sSP.inc(g.dma_start(out=d_hb[:], in_=s_hb[:]), 16)
        for gi, (sl, h2, hw) in enumerate(HACT):
            kind, j, c0, c1 = SLABS[sl]
            if gi % 2 != 1:
                continue
            if gi >= 2:
                psl, ph2, _ = HALVES[gi - 2]
                env.wait(g, sV, f"wq_done_{psl}_{ph2}")
            if kind == "d":
                src_ap = wd[j][:, c0 * 128 + h2 * 1024:c0 * 128 + h2 * 1024 + hw]
            else:
                src_ap = (wg if kind == "g" else wu)[j][:, h2 * 1024:h2 * 1024 + hw]
            sW2.inc(g.dma_start(out=s_wsl[:, (gi % 2) * 1024:(gi % 2) * 1024 + hw],
                                in_=src_ap), 16)
            env.mark(sW2, f"wsl_{sl}_{h2}")
        for ch in range(8 if PHASES >= 4 else 0):
            env.wait(g, sV, "hq_all")
            if ch >= 1:
                env.wait(g, sV, f"out_{ch-1}")
            qt, s4 = ch // 4, ch % 4
            y1r = y1d.rearrange("(q p) t -> q p t", p=128)
            sSP.inc(g.dma_start(out=s_y1r[:], in_=y1r[qt, :, s4 * D:(s4 + 1) * D]), 16)
            env.mark(sSP, f"y1rl_{ch}")

    def prog_sync(sync):
        sLD.inc(sync.dma_start(out=s_vec[:], in_=vecs[:]), 16)
        env.mark(sLD, "vec_loads")
        for u in range(NSUP):
            for h in range(2):
                env.wait(sync, sS, f"gnf_{u}_{h}")
                sGN.inc(sync.dma_start(out=gn32[h][:, u * SUP:(u + 1) * SUP],
                                       in_=s_qraw[:, h * SUP:(h + 1) * SUP]), 16)
                env.mark(sGN, f"gnsp_{u}_{h}")
        xrr = xres.rearrange("(q p) t -> q p t", p=128)
        for ch in range(8 if PHASES >= 2 else 0):
            qt, s4 = ch // 4, ch % 4
            if ch >= 2:
                env.wait(sync, sV, f"lnv_{ch-2}")
            sRL.inc(sync.dma_start(out=s_gn32[:, (ch % 2) * D:(ch % 2 + 1) * D],
                                   in_=gn32[qt][:, s4 * D:(s4 + 1) * D]), 16)
            env.mark(sRL, f"gnrl_{ch}")
            if ch >= 1:
                env.wait(sync, sV, f"lnv_{ch-1}")
            sXR.inc(sync.dma_start(out=s_xr[:],
                                   in_=xrr[qt, :, s4 * D:(s4 + 1) * D]), 16)
            env.mark(sXR, f"xr_{ch}")
        for gi, (sl, h2, hw) in enumerate(HACT):
            kind, j, c0, c1 = SLABS[sl]
            if gi % 2 != 0:
                continue
            if gi >= 2:
                psl, ph2, _ = HALVES[gi - 2]
                env.wait(sync, sV, f"wq_done_{psl}_{ph2}")
            if kind == "d":
                src_ap = wd[j][:, c0 * 128 + h2 * 1024:c0 * 128 + h2 * 1024 + hw]
            else:
                src_ap = (wg if kind == "g" else wu)[j][:, h2 * 1024:h2 * 1024 + hw]
            sWSL.inc(sync.dma_start(out=s_wsl[:, (gi % 2) * 1024:(gi % 2) * 1024 + hw],
                                    in_=src_ap), 16)
            env.mark(sWSL, f"wsl_{sl}_{h2}")
        outr = out.rearrange("(q p) t -> q p t", p=128)
        for ch in range(8 if PHASES >= 4 else 0):
            env.wait(sync, sV, f"out_{ch}")
            qt, s4 = ch // 4, ch % 4
            sST.inc(sync.dma_start(out=outr[qt, :, s4 * D:(s4 + 1) * D],
                                   in_=s_ych[:, (ch % 2) * D:(ch % 2 + 1) * D]), 16)
            env.mark(sST, f"ost_{ch}")
        if not env.plan:
            sync.wait_ge(sST.sem, sST.v)
            sync.wait_ge(sSP.sem, sSP.v)

    def prog_scalar(s):
        env.wait(s, sLD, "vec_loads")
        env.wait(s, sG, "g_setup")
        env.wait(s, sV, "v_iv")
        sS.inc(s.activation(s_iv[:, 3:4], s_iv[:, 2:3], AF.Identity))
        env.mark(sS, "invf")
        for u in range(NSUP):
            for h in range(2):
                if h == 0:
                    env.wait(s, sV, f"tblr_{u}")
                    par = u % 2
                    s.activation(s_tbl[:, SUP:2 * SUP], s_tbl[:, 0:SUP], AF.Abs)
                    s.activation(s_sinb[:, par * SUP:(par + 1) * SUP],
                                 s_tbl[:, 0:SUP], AF.Sin)
                    sS.inc(s.activation(s_cosb[:, par * SUP:(par + 1) * SUP],
                                        s_tbl[:, SUP:2 * SUP], AF.Sin,
                                        bias=s_cc[0:64, 1:2], scale=s_cc[0:64, 6:7]))
                    env.mark(sS, f"tbl_{u}")
                qsl = s_qraw[:, h * SUP:(h + 1) * SUP]
                env.wait(s, sT, f"mmacc_{u}_{h}")
                if DEBUG_TAP1 and u == 1 and h == 0:
                    env.wait(s, sSP, "tap4")
                if u >= 1:
                    env.wait(s, sGN, f"gnsp_{u-1}_{h}")
                sS.inc(s.copy(qsl, p_acc[:, h * SUP:(h + 1) * SUP]))
                env.mark(sS, f"qevac_{u}_{h}")
                if h == 0:
                    if DEBUG_TAP1 and u == 0:
                        env.wait(s, sSP, "tap1")
                    sS.inc(s.copy(s_qbf[:], qsl[0:64, :]))
                    env.mark(sS, f"qbf_{u}")
                    env.wait(s, sV, f"blend_{u}")
                s.activation(qsl, qsl, AF.Relu)
                sS.inc(s.activation(qsl, qsl, AF.Tanh, bias=s_vcols[:, h:h + 1]))
                env.mark(sS, f"H_{u}_{h}")
                if u > 0 or h > 0:
                    pu, phh = (u, 0) if h == 1 else (u - 1, 1)
                    env.wait(s, sT, f"stats_{pu}_{phh}")
                if DEBUG_TAP1 and u == 0 and h == 0:
                    env.wait(s, sSP, "tap2")
                s.copy(s_hbf[:, 0:SUP], qsl)
                sS.inc(s.activation(s_hbf[:, SUP:2 * SUP], qsl, AF.Square))
                env.mark(sS, f"Hbf_{u}_{h}")
                env.wait(s, sT, f"stats_{u}_{h}")
                s.copy(s_st[:, 0:SUP], p_st2[0:2, 0:SUP])
                sS.inc(s.copy(s_st[:, SUP:2 * SUP], p_st2[0:2, 512:512 + SUP]))
                env.mark(sS, f"stev_{u}_{h}")
                env.wait(s, sV, f"var_{u}_{h}")
                sS.inc(s.activation(s_st[:, SUP:2 * SUP], s_st[:, SUP:2 * SUP],
                                    AF.Sqrt, bias=s_cc[0:2, 3:4]))
                env.mark(sS, f"sqrt_{u}_{h}")
                env.wait(s, sV, f"gnapp_{u}_{h}")
                sS.inc(s.activation(qsl, qsl, AF.Identity,
                                    bias=s_vcols[:, 4 + h:5 + h], scale=s_vcols[:, 2 + h:3 + h]))
                env.mark(sS, f"gnf_{u}_{h}")
        for ch in range(8 if PHASES >= 2 else 0):
            env.wait(s, sV, f"lnv_{ch}")
            sS.inc(s.activation(s_ln[:, 5:6], s_ln[:, 3:4], AF.Sqrt))
            env.mark(sS, f"lnsq_{ch}")
            for rnd in range(2):
                env.wait(s, sT, f"y1tr_{ch}_{rnd}")
                qt, s4 = ch // 4, ch % 4
                b = qt * 4 + s4
                dst = s_y1t.rearrange("p (n r) -> p n r", n=NDT)[
                    :, rnd * 8:(rnd + 1) * 8, b * 128:(b + 1) * 128]
                sS.inc(s.copy(dst, p_tr.rearrange("p (n r) -> p n r", n=8)))
                env.mark(sS, f"y1tev_{ch}_{rnd}")
        for gi, (sl, h2, hw) in enumerate(HACT):
            kind, j, c0, c1 = SLABS[sl]
            env.wait(s, sWSL if gi % 2 == 0 else sW2, f"wsl_{sl}_{h2}")
            sS.inc(s.activation(s_wab[:, 0:hw],
                                s_wsl[:, (gi % 2) * 1024:(gi % 2) * 1024 + hw],
                                AF.Abs))
            env.mark(sS, f"whi_{sl}_{h2}")
            env.wait(s, sT, f"sclmm_{sl}_{h2}")
            if gi >= 1:
                psl, ph2, _ = HALVES[gi - 1]
                env.wait(s, sV, f"wq_done_{psl}_{ph2}")
            sS.inc(s.activation(s_e1[:, 0:hw], p_st2[:, 0:hw], AF.Identity,
                                bias=s_cc[:, 2:3]))
            env.mark(sS, f"e1_{sl}_{h2}")
            if kind == "u" and h2 == NHOF[sl] - 1:
                env.wait(s, sV, f"gmul_{j}")
                sS.inc(s.activation(s_W0[:], s_W0[:], AF.Silu))
                env.mark(sS, f"silu_{j}")
        for ch in range(8 if PHASES >= 4 else 0):
            env.wait(s, sV, f"ln2v_{ch}")
            sS.inc(s.activation(s_ln[:, 13:14], s_ln[:, 11:12], AF.Sqrt,
                                bias=s_cc[:, 3:4]))
            env.mark(sS, f"ln2sq_{ch}")

    def prog_tensor(t):
        env.wait(t, sG, "g_setup")
        env.wait(t, sV, "idf")
        env.wait(t, sLD, "vec_loads")
        sT.inc(t.matmul(p_x[:, 0:128], s_vec[:], s_idf[0:16, :], start=True, stop=True))
        env.mark(sT, "vectr")
        env.wait(t, sV, "vcols")
        env.wait(t, sV, "v_const")
        env.wait(t, sS, "invf")
        sT.inc(t.matmul(p_x[0:1, 0:128], s_iv[:, 3:4], s_idf[:], start=True, stop=True))
        env.mark(sT, "ivrow")
        env.wait(t, sV, "ivrowev")
        env.wait(t, sG, "tv_0")
        sT.inc(t.matmul(p_x[0:64, 0:SUP], s_ivr[0:1, 0:64], s_tv[:],
                        start=True, stop=True))
        env.mark(sT, "ang_0")
        env.wait(t, sLG, "wq_bf")
        for u in range(NSUP):
            env.wait(t, sXF, f"xf_{u}")
            buf = u % 2
            for h in range(2):
                if u >= 1:
                    env.wait(t, sS, f"qevac_{u-1}_{h}")
                mm = None
                for c in range(NI):
                    lw = s_wq[:, c * C + h * 128:c * C + (h + 1) * 128]
                    rhs = s_xf[:, buf * NI * SUP + c * SUP:buf * NI * SUP + (c + 1) * SUP]
                    mm = t.matmul(p_acc[:, h * SUP:(h + 1) * SUP], lw, rhs,
                                  start=(c == 0), stop=(c == NI - 1))
                sT.inc(mm)
                env.mark(sT, f"mmacc_{u}_{h}")
                if h == 0:
                    env.wait(t, sS, f"qbf_{u}")
                    env.wait(t, sV, f"tblfree_{u}")
                    sT.inc(t.matmul(p_x[0:64, 0:SUP], s_pt[0:64, 0:64], s_qbf[:],
                                    start=True, stop=True))
                    env.mark(sT, f"rot_{u}")
                env.wait(t, sS, f"Hbf_{u}_{h}")
                if u > 0 or h > 0:
                    pu, phh = (u, 0) if h == 1 else (u - 1, 1)
                    env.wait(t, sS, f"stev_{pu}_{phh}")
                t.matmul(p_st2[0:2, 0:SUP], s_g2[:, 0:2], s_hbf[:, 0:SUP],
                         start=True, stop=True)
                sT.inc(t.matmul(p_st2[0:2, 512:512 + SUP], s_g2[:, 0:2],
                                s_hbf[:, SUP:2 * SUP], start=True, stop=True))
                env.mark(sT, f"stats_{u}_{h}")
                env.wait(t, sV, f"sthl_{u}_{h}")
                if u > 0 or h > 0:
                    pu, phh = (u, 0) if h == 1 else (u - 1, 1)
                    env.wait(t, sV, f"gnapp_{pu}_{phh}")
                t.matmul(p_ub[:, 0:SUP], s_sel[0:2, :], s_mhl[:, 0:SUP],
                         start=True, stop=False)
                t.matmul(p_ub[:, 0:SUP], s_sel[0:2, :], s_mhl[:, SUP:2 * SUP],
                         start=False, stop=True)
                t.matmul(p_ub[:, 512:1024], s_sel[0:2, :], s_rhl2[:, 0:SUP],
                         start=True, stop=False)
                sT.inc(t.matmul(p_ub[:, 512:1024], s_sel[0:2, :], s_rhl2[:, SUP:2 * SUP],
                                start=False, stop=True))
                env.mark(sT, f"rep_{u}_{h}")
            if u + 1 < NSUP:
                env.wait(t, sV, f"blend_{u}")
                env.wait(t, sG, f"tv_{u+1}")
                sT.inc(t.matmul(p_x[0:64, 0:SUP], s_ivr[0:1, 0:64], s_tv[:],
                                start=True, stop=True))
                env.mark(sT, f"ang_{u+1}")
            env.mark(sT, f"mix_done_{u}")
        for ch in range(8 if PHASES >= 2 else 0):
            env.wait(t, sV, f"y1q_{ch}")
            for rnd in range(2):
                if not (ch == 0 and rnd == 0):
                    pch, prnd = (ch, 0) if rnd == 1 else (ch - 1, 1)
                    env.wait(t, sS, f"y1tev_{pch}_{prnd}")
                tr = None
                for b8 in range(8):
                    cb = rnd * 8 + b8
                    tr = t.transpose(p_tr[:, b8 * 128:(b8 + 1) * 128],
                                     s_y1q[:, cb * 128:(cb + 1) * 128], s_idb[:])
                sT.inc(tr)
                env.mark(sT, f"y1tr_{ch}_{rnd}")
        if PHASES < 2:
            return
        env.wait(t, sV, "rs1cols")
        for b in range(8):
            sT.inc(t.matmul(p_x[0:1, 0:128], s_amc[:, 8 + b:9 + b], s_idf[:],
                            start=True, stop=True))
            env.mark(sT, f"rs1row_{b}")
            env.wait(t, sV, f"rs1rowev_{b}")
        env.wait(t, sV, "rs1hl")
        t.matmul(p_ub[:, 0:512], s_onesr[0:1, :], s_rhl[0:1, 0:512], start=True, stop=False)
        t.matmul(p_ub[:, 0:512], s_onesr[0:1, :], s_rhl[0:1, R:R + 512], start=False, stop=True)
        t.matmul(p_ub[:, 512:1024], s_onesr[0:1, :], s_rhl[0:1, 512:1024],
                 start=True, stop=False)
        sT.inc(t.matmul(p_ub[:, 512:1024], s_onesr[0:1, :], s_rhl[0:1, R + 512:2 * R],
                        start=False, stop=True))
        env.mark(sT, "rs1rep")

        def emit_sclmm(gi):
            sl, h2, hw = HALVES[gi]
            env.wait(t, sS, f"whi_{sl}_{h2}")
            if gi >= 1:
                psl, ph2, _ = HALVES[gi - 1]
                env.wait(t, sS, f"e1_{psl}_{ph2}")
            mm = None
            for qq in range((hw + 511) // 512):
                qw = min(512, hw - qq * 512)
                off = qq * 512
                mm = t.matmul(p_st2[:, off:off + qw], s_onesf[:],
                              s_wab[:, off:off + qw], start=True, stop=True)
            sT.inc(mm)
            env.mark(sT, f"sclmm_{sl}_{h2}")

        def emit_main(sl):
            kind, j, c0, c1 = SLABS[sl]
            wslot = sl % 2
            if kind == "g":
                if j == 0:
                    env.wait(t, sS, "qevac_15_1")
                else:
                    env.wait(t, sV, f"gmul_{j-1}")
                tgt, rhsbuf, nch, cg0 = p_acc, s_y1t, NI, 0
            elif kind == "u":
                if j == 0:
                    env.wait(t, sV, "rs1b")
                else:
                    env.wait(t, sV, f"umul_{j-1}")
                tgt, rhsbuf, nch, cg0 = p_ub, s_y1t, NI, 0
            else:
                if c0 == 0 and j > 0:
                    env.wait(t, sV, f"dmul_{j-1}")
                env.wait(t, sV, f"hq_{c1-1}")
                tgt, rhsbuf, nch, cg0 = p_acc, s_hb, c1 - c0, c0
            lastc = NI - 1 if kind in "gu" else NOT - 1
            mm = None
            for ci in range(nch):
                c = cg0 + ci
                lw = wq_slab_ap(wslot, ci * 128, 128)
                t.matmul(tgt[:, 0:512], lw, rhsbuf[:, c * R:c * R + 512],
                         start=(c == 0), stop=(c == lastc))
                mm = t.matmul(tgt[:, 512:1024], lw, rhsbuf[:, c * R + 512:(c + 1) * R],
                              start=(c == 0), stop=(c == lastc))
            sT.inc(mm)
            env.mark(sT, f"main_{sl}")

        for gi, (sl, h2, hw) in enumerate(HACT):
            kind, j, c0, c1 = SLABS[sl]
            if kind == "d" and c0 == 0 and h2 == 0 and j == 0:
                env.wait(t, sV, "htree")
                for rnd in range(2):
                    if rnd == 1:
                        env.wait(t, sV, "htr_ev_0")
                    tr = None
                    for b4 in range(4):
                        b = rnd * 4 + b4
                        tr = t.transpose(p_x[:, b4 * 128:(b4 + 1) * 128],
                                         s_h32v[:, b * 128:(b + 1) * 128], s_idf[:])
                    sT.inc(tr)
                    env.mark(sT, f"htr_{rnd}")
                env.wait(t, sV, "s2cols")
                for b in range(8):
                    sT.inc(t.matmul(p_x[0:1, 0:128], s_amc[:, b:b + 1], s_idf[:],
                                    start=True, stop=True))
                    env.mark(sT, f"s2row_{b}")
                    env.wait(t, sV, f"s2rowev_{b}")
                env.wait(t, sV, "s2hl")
                env.wait(t, sV, f"umul_{NOT-1}")
                t.matmul(p_ub[:, 0:512], s_onesr[0:1, :], s_rhl[0:1, 0:512],
                         start=True, stop=False)
                t.matmul(p_ub[:, 0:512], s_onesr[0:1, :], s_rhl[0:1, R:R + 512],
                         start=False, stop=True)
                t.matmul(p_ub[:, 512:1024], s_onesr[0:1, :], s_rhl[0:1, 512:1024],
                         start=True, stop=False)
                sT.inc(t.matmul(p_ub[:, 512:1024], s_onesr[0:1, :],
                                s_rhl[0:1, R + 512:2 * R], start=False, stop=True))
                env.mark(sT, "s2rep")
                for b in range(8):
                    sT.inc(t.matmul(p_x[0:1, 0:128], s_amc[:, 16 + b:17 + b], s_idf[:],
                                    start=True, stop=True))
                    env.mark(sT, f"rs2row_{b}")
                    env.wait(t, sV, f"rs2rowev_{b}")
                env.wait(t, sV, "rs2hl")
                t.matmul(p_x[:, 0:512], s_onesr[0:1, :], s_rhl[0:1, 0:512],
                         start=True, stop=False)
                sT.inc(t.matmul(p_x[:, 0:512], s_onesr[0:1, :], s_rhl[0:1, R:R + 512],
                                start=False, stop=True))
                env.mark(sT, "rs2rep0")
                env.wait(t, sV, "rs2ev0")
                t.matmul(p_x[:, 0:512], s_onesr[0:1, :], s_rhl[0:1, 512:1024],
                         start=True, stop=False)
                sT.inc(t.matmul(p_x[:, 0:512], s_onesr[0:1, :],
                                s_rhl[0:1, R + 512:2 * R], start=False, stop=True))
                env.mark(sT, "rs2rep1")
            emit_sclmm(gi)
            if h2 == NHOF[sl] - 1:
                env.wait(t, sV, f"wq_done_{sl}_{h2}")
                emit_main(sl)
        for ch in range(8 if PHASES >= 4 else 0):
            qt, s4 = ch // 4, ch % 4
            b = qt * 4 + s4
            env.wait(t, sV, f"dmul_{NDT-1}")
            for rnd in range(2):
                if not (ch == 0 and rnd == 0):
                    pch, prnd = (ch, 0) if rnd == 1 else (ch - 1, 1)
                    env.wait(t, sV, f"ytok_{pch}_{prnd}")
                tr = None
                for b8 in range(8):
                    c = rnd * 8 + b8
                    dscb = s_H0 if c < 8 else s_H1
                    cc8 = c % 8
                    tr = t.transpose(p_tr[:, b8 * 128:(b8 + 1) * 128],
                                     dscb[:, cc8 * R + b * 128:cc8 * R + (b + 1) * 128],
                                     s_idb[:])
                sT.inc(tr)
                env.mark(sT, f"outtr_{ch}_{rnd}")

    s_h32v = s_W0
    def prog_vector(v):
        env.wait(v, sG, "g_setup")
        v.tensor_scalar(s_sel[:], s_srow[:, 0:128], 0.0, None, op0=OP.is_equal)
        v.tensor_scalar(s_idf[:], s_ramp[:], 0.0, None, op0=OP.is_equal)
        sV.inc(v.tensor_scalar(s_idb[:], s_ramp[:], 0.0, None, op0=OP.is_equal))
        env.mark(sV, "idf")
        v.tensor_copy(s_iv[:, 1:2], s_ramp[:, 0:1])
        env.wait(v, sT, "vectr")
        sV.inc(v.tensor_copy(s_vcols[:], p_x[:, 0:16]))
        env.mark(sV, "vcols")
        if not env.plan:
            v.wait_ge(sV.sem, sV.v)
        vsync(v, v.scalar_tensor_tensor(s_amc[:, 1:2], s_vcols[:, 7:8], -2.0,
                                        s_iv[:, 1:2], op0=OP.mult, op1=OP.add))
        vsync(v, v.tensor_scalar(s_amc[:, 2:3], s_amc[:, 1:2], -1.0, 1.0,
                                 op0=OP.mult, op1=OP.add))
        vsync(v, v.tensor_copy(s_amc[:, 3:4], s_vcols[:, 7:8]))
        v.tensor_scalar(s_W0[:, 0:128], s_ramp[:], -1.0, None, op0=OP.is_equal)
        v.tensor_scalar(s_W1[:, 0:128], s_ramp[:], 1.0, None, op0=OP.is_equal)
        v.tensor_scalar(s_W0[:, 0:128], s_W0[:, 0:128], s_amc[:, 2:3], None, op0=OP.mult)
        v.tensor_scalar(s_W1[:, 0:128], s_W1[:, 0:128], s_amc[:, 1:2], None, op0=OP.mult)
        v.tensor_tensor(s_pt[:], s_W0[:, 0:128], s_W1[:, 0:128], op=OP.subtract)
        sV.inc(v.memset(s_cc[:, 6:7], -1.0))
        env.mark(sV, "v_const")
        env.wait(v, sLD, "vec_loads")
        vsync(v, v.tensor_tensor(s_amc[:, 4:5], s_vcols[:, 8:9], s_vcols[:, 9:10],
                                 op=OP.add))
        vsync(v, v.tensor_tensor(s_amc[:, 5:6], s_amc[:, 4:5], s_amc[:, 4:5],
                                 op=OP.mult))
        vsync(v, v.tensor_tensor(s_amc[:, 6:7], s_amc[:, 5:6], s_amc[:, 5:6],
                                 op=OP.mult))
        vsync(v, v.tensor_tensor(s_amc[:, 7:8], s_amc[:, 6:7], s_amc[:, 6:7],
                                 op=OP.mult))
        vsync(v, v.tensor_tensor(s_iv[:, 1:2], s_amc[:, 7:8], s_amc[:, 7:8],
                                 op=OP.mult))
        for i, src_c in enumerate([s_amc[:, 4:5], s_amc[:, 5:6], s_amc[:, 6:7],
                                   s_amc[:, 7:8], s_iv[:, 1:2]]):
            vsync(v, v.scalar_tensor_tensor(s_amc[:, 2:3], src_c, -1.0,
                                            s_vcols[:, 10 + i:11 + i],
                                            op0=OP.add, op1=OP.mult))
            vsync(v, v.tensor_scalar(s_amc[:, 2:3], s_amc[:, 2:3], 1.0, None,
                                     op0=OP.add))
            if i == 0:
                last_i = v.tensor_copy(s_iv[:, 2:3], s_amc[:, 2:3])
            else:
                last_i = v.tensor_tensor(s_iv[:, 2:3], s_iv[:, 2:3],
                                         s_amc[:, 2:3], op=OP.mult)
            if i < 4:
                vsync(v, last_i)
        sV.inc(last_i)
        env.mark(sV, "v_iv")
        env.wait(v, sT, "ivrow")
        sV.inc(v.tensor_copy(s_ivr[0:1, :], p_x[0:1, 0:128]))
        env.mark(sV, "ivrowev")
        for u in range(NSUP):
            for h in range(2):
                if h == 0:
                    env.wait(v, sT, f"ang_{u}")
                    sV.inc(v.tensor_scalar(s_tbl[:, 0:SUP], p_x[0:64, 0:SUP],
                                            s_vcols[0:64, 6:7], None, op0=OP.mult))
                    env.mark(sV, f"tblfree_{u}")
                    v.tensor_scalar(s_tbl[:, SUP:2 * SUP], s_tbl[:, 0:SUP], INV2PI,
                                    None, op0=OP.mult)
                    v.tensor_copy(s_tbi[:], s_tbl[:, SUP:2 * SUP])
                    v.tensor_tensor(s_tbl[:, SUP:2 * SUP], s_tbl[:, SUP:2 * SUP],
                                    s_tbi[:], op=OP.subtract)
                    v.tensor_scalar(s_tbn[:, 0:SUP], s_tbl[:, SUP:2 * SUP], 0.5,
                                    None, op0=OP.is_ge)
                    v.tensor_scalar(s_tbn[:, SUP:2 * SUP], s_tbl[:, SUP:2 * SUP],
                                    -0.5, None, op0=OP.is_le)
                    v.tensor_tensor(s_tbn[:, 0:SUP], s_tbn[:, 0:SUP],
                                    s_tbn[:, SUP:2 * SUP], op=OP.subtract)
                    v.tensor_tensor(s_tbn[:, 0:SUP], s_tbi[:], s_tbn[:, 0:SUP],
                                    op=OP.add)
                    v.scalar_tensor_tensor(s_tbl[:, 0:SUP], s_tbn[:, 0:SUP], -TP_HI,
                                           s_tbl[:, 0:SUP], op0=OP.mult, op1=OP.add)
                    v.scalar_tensor_tensor(s_tbl[:, 0:SUP], s_tbn[:, 0:SUP], -TP_MD,
                                           s_tbl[:, 0:SUP], op0=OP.mult, op1=OP.add)
                    sV.inc(v.scalar_tensor_tensor(s_tbl[:, 0:SUP], s_tbn[:, 0:SUP],
                                                  -TP_LO, s_tbl[:, 0:SUP],
                                                  op0=OP.mult, op1=OP.add))
                    env.mark(sV, f"tblr_{u}")
                    env.wait(v, sS, f"qevac_{u}_0")
                    env.wait(v, sT, f"rot_{u}")
                    env.wait(v, sS, f"tbl_{u}")
                    par = u % 2
                    qsl = s_qraw[:, 0:SUP]
                    v.tensor_tensor(s_W2[0:64, 0:SUP], qsl[0:64, :],
                                    s_cosb[:, par * SUP:(par + 1) * SUP], op=OP.mult)
                    v.tensor_tensor(s_W2[0:64, 512:1024], p_x[0:64, 0:SUP],
                                    s_sinb[:, par * SUP:(par + 1) * SUP], op=OP.mult)
                    sV.inc(v.tensor_tensor(qsl[0:64, :], s_W2[0:64, 0:SUP],
                                           s_W2[0:64, 512:1024], op=OP.add))
                    env.mark(sV, f"blend_{u}")
                env.wait(v, sS, f"stev_{u}_{h}")
                if DEBUG_TAP1 and u == 0 and h == 0:
                    env.wait(v, sSP, "tap3")
                mu = s_st[:, 0:SUP]
                m2 = s_st[:, SUP:2 * SUP]
                vsync(v, v.tensor_tensor(s_srow[:, 0:SUP], mu, mu, op=OP.mult))
                sV.inc(v.tensor_tensor(m2, m2, s_srow[:, 0:SUP], op=OP.subtract))
                env.mark(sV, f"var_{u}_{h}")
                env.wait(v, sS, f"sqrt_{u}_{h}")
                vsync(v, v.reciprocal(m2, m2))
                vsync(v, v.tensor_copy(s_mhl[:, 0:SUP], mu))
                vsync(v, v.tensor_tensor(s_srow[:, 0:SUP], mu, s_mhl[:, 0:SUP], op=OP.subtract))
                v.tensor_copy(s_mhl[:, SUP:2 * SUP], s_srow[:, 0:SUP])
                vsync(v, v.tensor_copy(s_rhl2[:, 0:SUP], m2))
                vsync(v, v.tensor_tensor(s_srow[:, 0:SUP], m2, s_rhl2[:, 0:SUP], op=OP.subtract))
                sV.inc(v.tensor_copy(s_rhl2[:, SUP:2 * SUP], s_srow[:, 0:SUP]))
                env.mark(sV, f"sthl_{u}_{h}")
                env.wait(v, sT, f"rep_{u}_{h}")
                if DEBUG_TAP1 and u == 0 and h == 1:
                    env.wait(v, sSP, "tap5")
                env.wait(v, sS, f"H_{u}_{h}")
                qsl = s_qraw[:, h * SUP:(h + 1) * SUP]
                v.tensor_tensor(qsl, qsl, p_ub[:, 0:SUP], op=OP.subtract)
                sV.inc(v.tensor_tensor(qsl, qsl, p_ub[:, 512:512 + SUP], op=OP.mult))
                env.mark(sV, f"gnapp_{u}_{h}")
        for ch in range(8 if PHASES >= 2 else 0):
            qt, s4 = ch // 4, ch % 4
            gsl = s_gn32[:, (ch % 2) * D:(ch % 2 + 1) * D]
            env.wait(v, sRL, f"gnrl_{ch}")
            env.wait(v, sXR, f"xr_{ch}")
            if ch >= 1:
                env.wait(v, sSP, f"y1sp_{ch-1}")
            xsl = s_xr[:]
            v.scalar_tensor_tensor(s_y1s[:], xsl, 0.0, gsl, op0=OP.add, op1=OP.add,
                                   accum_out=s_ln[:, 0:1])
            vsync(v, v.scalar_tensor_tensor(gsl, s_y1s[:], 0.0, s_y1s[:], op0=OP.add,
                                            op1=OP.mult, accum_out=s_ln[:, 1:2]))
            vsync(v, v.tensor_scalar(s_ln[:, 2:3], s_ln[:, 0:1], 1.0 / D, None, op0=OP.mult))
            vsync(v, v.tensor_scalar(s_ln[:, 3:4], s_ln[:, 1:2], 1.0 / D, None, op0=OP.mult))
            vsync(v, v.tensor_tensor(s_ln[:, 4:5], s_ln[:, 2:3], s_ln[:, 2:3], op=OP.mult))
            sV.inc(v.tensor_tensor(s_ln[:, 3:4], s_ln[:, 3:4], s_ln[:, 4:5],
                                   op=OP.subtract))
            env.mark(sV, f"lnv_{ch}")
            env.wait(v, sS, f"lnsq_{ch}")
            vsync(v, v.reciprocal(s_ln[:, 5:6], s_ln[:, 5:6]))
            vsync(v, v.tensor_tensor(s_ln[:, 6:7], s_ln[:, 2:3], s_ln[:, 5:6], op=OP.mult))
            vsync(v, v.tensor_scalar(s_ln[:, 6:7], s_ln[:, 6:7], -1.0, None, op0=OP.mult))
            sV.inc(v.tensor_scalar(s_y1s[:], s_y1s[:], s_ln[:, 5:6], s_ln[:, 6:7],
                                   op0=OP.mult, op1=OP.add))
            env.mark(sV, f"y1_{ch}")
            vsync(v, v.reduce_max(s_ln[:, 7:8], s_y1s[:], axis=AX, apply_absolute_value=True))
            vsync(v, v.tensor_scalar(s_ln[:, 7:8], s_ln[:, 7:8], EPS, None, op0=OP.max))
            vsync(v, v.reciprocal(s_ln[:, 8:9], s_ln[:, 7:8]))
            vsync(v, v.tensor_scalar(s_ln[:, 9:10], s_ln[:, 8:9], 127.0, None, op0=OP.mult))
            v.tensor_scalar(s_amc[:, 8 + ch:9 + ch], s_ln[:, 7:8], 1.0 / 127.0, None,
                            op0=OP.mult)
            if ch >= 1:
                env.wait(v, sT, f"y1tr_{ch-1}_1")
            for h2 in range(2):
                sl1024 = slice(h2 * 1024, (h2 + 1) * 1024)
                v.tensor_scalar(s_W0[:], s_y1s[:, sl1024], s_ln[:, 9:10], None,
                                op0=OP.mult)
                v.tensor_copy(s_qi[:], s_W0[:])
                v.tensor_tensor(s_W0[:], s_W0[:], s_qi[:], op=OP.subtract)
                v.tensor_scalar(s_W1[:], s_W0[:], 0.5, None, op0=OP.is_ge)
                v.tensor_scalar(s_W2[:], s_W0[:], -0.5, None, op0=OP.is_le)
                v.tensor_tensor(s_W1[:], s_W1[:], s_W2[:], op=OP.subtract)
                ql = v.tensor_tensor(s_y1q[:, sl1024], s_qi[:], s_W1[:], op=OP.add)
            sV.inc(ql)
            env.mark(sV, f"y1q_{ch}")
        if PHASES < 2:
            return
        env.mark(sV, "rs1cols")
        for b in range(8):
            env.wait(v, sT, f"rs1row_{b}")
            sV.inc(v.tensor_copy(s_W1[0:1, b * 128:(b + 1) * 128], p_x[0:1, 0:128]))
            env.mark(sV, f"rs1rowev_{b}")
        vsync(v, v.tensor_copy(s_rhl[0:1, 0:R], s_W1[0:1, 0:R]))
        vsync(v, v.tensor_copy(s_W0[0:1, 0:R], s_rhl[0:1, 0:R]))
        vsync(v, v.tensor_tensor(s_W0[0:1, 0:R], s_W1[0:1, 0:R], s_W0[0:1, 0:R],
                                 op=OP.subtract))
        sV.inc(v.tensor_copy(s_rhl[0:1, R:2 * R], s_W0[0:1, 0:R]))
        env.mark(sV, "rs1hl")
        env.wait(v, sT, "rs1rep")
        sV.inc(v.tensor_copy(s_rs1[:], p_ub[:, 0:1024]))
        env.mark(sV, "rs1b")

        for gi, (sl, h2, hw) in enumerate(HACT):
            kind, j, c0, c1 = SLABS[sl]
            env.wait(v, sS, f"e1_{sl}_{h2}")
            wsl = s_wsl[:, (gi % 2) * 1024:(gi % 2) * 1024 + hw]
            v.tensor_tensor(s_W1[:, 0:hw], wsl, s_e1[:, 0:hw], op=OP.is_ge)
            v.scalar_tensor_tensor(s_W2[:, 0:hw], wsl, -1.0, s_e1[:, 0:hw],
                                   op0=OP.mult, op1=OP.is_ge)
            v.tensor_tensor(s_W1[:, 0:hw], s_W1[:, 0:hw], s_W2[:, 0:hw],
                            op=OP.subtract)
            sV.inc(v.scalar_tensor_tensor(
                wq_slab_ap(sl % 2, h2 * 1024, hw), s_W1[:, 0:hw], 2.0, s_e1[:, 0:hw],
                op0=OP.mult, op1=OP.mult))
            env.mark(sV, f"wq_done_{sl}_{h2}")
            if h2 == NHOF[sl] - 1:
                if kind == "u":
                    env.wait(v, sT, f"main_{sl-1}")
                    sV.inc(v.tensor_tensor(s_W0[:], p_acc[:, 0:1024], s_rs1[:],
                                           op=OP.mult))
                    env.mark(sV, f"gmul_{j}")
                    env.wait(v, sT, f"main_{sl}")
                    sV.inc(v.tensor_tensor(s_W1[:], p_ub[:, 0:1024], s_rs1[:],
                                           op=OP.mult))
                    env.mark(sV, f"umul_{j}")
                    env.wait(v, sS, f"silu_{j}")
                    if j == 28 and PHASES >= 2:
                        env.wait(v, sSP, "y1sp_7")
                    sV.inc(v.tensor_tensor(s_hb[:, j * R:(j + 1) * R], s_W0[:],
                                           s_W1[:], op=OP.mult))
                    env.mark(sV, f"h_{j}")
                    if PHASES >= 4:
                        gmx = s_gn32[:, 0:1024]
                        gmn = s_gn32[:, 1024:2048]
                        hsl = s_hb[:, j * R:(j + 1) * R]
                        if j == 0:
                            v.tensor_copy(gmx, hsl)
                            v.tensor_copy(gmn, hsl)
                        else:
                            v.tensor_tensor(gmx, gmx, hsl, op=OP.max)
                            v.tensor_tensor(gmn, gmn, hsl, op=OP.min)
                    if j == NOT - 1 and PHASES >= 4:
                        v.tensor_scalar(s_W2[:], s_gn32[:, 1024:2048], -1.0, None,
                                        op0=OP.mult)
                        sV.inc(v.tensor_tensor(s_W0[:], s_gn32[:, 0:1024], s_W2[:],
                                               op=OP.max))
                        env.mark(sV, "htree")
                        if DEBUG_TAP1:
                            env.wait(v, sSP, "tapHT")
                        for rnd in range(2):
                            env.wait(v, sT, f"htr_{rnd}")
                            for b4 in range(4):
                                b = rnd * 4 + b4
                                rm_i = v.reduce_max(s_amc[:, 24 + b:25 + b],
                                                    p_x[:, b4 * 128:(b4 + 1) * 128],
                                                    axis=AX)
                            sV.inc(rm_i)
                            env.mark(sV, f"htr_ev_{rnd}")
                        if not env.plan:
                            v.wait_ge(sV.sem, sV.v)
                        vsync(v, v.tensor_scalar(s_amc[:, 24:32], s_amc[:, 24:32],
                                                 EPS, None, op0=OP.max))
                        for b in range(8):
                            r_i = v.reciprocal(s_amc[:, b:b + 1], s_amc[:, 24 + b:25 + b])
                        vsync(v, r_i)
                        vsync(v, v.tensor_scalar(s_amc[:, 0:8], s_amc[:, 0:8], 127.0,
                                                 None, op0=OP.mult))
                        sV.inc(v.tensor_scalar(s_amc[:, 16:24], s_amc[:, 24:32],
                                               1.0 / 127.0, None, op0=OP.mult))
                        env.mark(sV, "s2cols")
                        if DEBUG_TAP1:
                            env.wait(v, sSP, "tapS2")
                        for b in range(8):
                            env.wait(v, sT, f"s2row_{b}")
                            sV.inc(v.tensor_copy(s_W1[0:1, b * 128:(b + 1) * 128],
                                                 p_x[0:1, 0:128]))
                            env.mark(sV, f"s2rowev_{b}")
                        vsync(v, v.tensor_copy(s_rhl[0:1, 0:R], s_W1[0:1, 0:R]))
                        vsync(v, v.tensor_copy(s_W0[0:1, 0:R], s_rhl[0:1, 0:R]))
                        vsync(v, v.tensor_tensor(s_W0[0:1, 0:R], s_W1[0:1, 0:R],
                                                 s_W0[0:1, 0:R], op=OP.subtract))
                        sV.inc(v.tensor_copy(s_rhl[0:1, R:2 * R], s_W0[0:1, 0:R]))
                        env.mark(sV, "s2hl")
                        for b in range(8):
                            env.wait(v, sT, f"rs2row_{b}")
                            sV.inc(v.tensor_copy(s_W1[0:1, b * 128:(b + 1) * 128],
                                                 p_x[0:1, 0:128]))
                            env.mark(sV, f"rs2rowev_{b}")
                        vsync(v, v.tensor_copy(s_rhl[0:1, 0:R], s_W1[0:1, 0:R]))
                        vsync(v, v.tensor_copy(s_W0[0:1, 0:R], s_rhl[0:1, 0:R]))
                        vsync(v, v.tensor_tensor(s_W0[0:1, 0:R], s_W1[0:1, 0:R],
                                                 s_W0[0:1, 0:R], op=OP.subtract))
                        sV.inc(v.tensor_copy(s_rhl[0:1, R:2 * R], s_W0[0:1, 0:R]))
                        env.mark(sV, "rs2hl")
                        env.wait(v, sT, "rs2rep0")
                        sV.inc(v.tensor_copy(s_rs1[:, 0:512], p_x[:, 0:512]))
                        env.mark(sV, "rs2ev0")
                        env.wait(v, sT, "rs2rep1")
                        sV.inc(v.tensor_copy(s_rs1[:, 512:1024], p_x[:, 0:512]))
                        env.mark(sV, "rs2b")
                        env.wait(v, sT, "s2rep")
                        for c in range(NOT):
                            v.tensor_tensor(s_W0[:], s_hb[:, c * R:(c + 1) * R],
                                            p_ub[:, 0:1024], op=OP.mult)
                            v.tensor_copy(s_qi[:], s_W0[:])
                            v.tensor_tensor(s_W0[:], s_W0[:], s_qi[:], op=OP.subtract)
                            v.tensor_scalar(s_W1[:], s_W0[:], 0.5, None, op0=OP.is_ge)
                            v.tensor_scalar(s_W2[:], s_W0[:], -0.5, None, op0=OP.is_le)
                            v.tensor_tensor(s_W1[:], s_W1[:], s_W2[:], op=OP.subtract)
                            hq_i = v.tensor_tensor(s_hb[:, c * R:(c + 1) * R],
                                                   s_qi[:], s_W1[:], op=OP.add)
                            sV.inc(hq_i)
                            env.mark(sV, f"hq_{c}")
                        env.mark(sV, "hq_all")
                elif kind == "d" and c1 == NOT:
                    env.wait(v, sT, f"main_{sl}")
                    dscb = s_H0 if j < 8 else s_H1
                    jj = j % 8
                    sV.inc(v.tensor_tensor(dscb[:, jj * R:(jj + 1) * R],
                                           p_acc[:, 0:1024], s_rs1[:], op=OP.mult))
                    env.mark(sV, f"dmul_{j}")
        for ch in range(8 if PHASES >= 4 else 0):
            qt, s4 = ch // 4, ch % 4
            if ch >= 2:
                env.wait(v, sST, f"ost_{ch-2}")
            env.wait(v, sSP, f"y1rl_{ch}")
            for rnd in range(2):
                env.wait(v, sT, f"outtr_{ch}_{rnd}")
                sV.inc(v.tensor_tensor(
                    s_ych[:, (ch % 2) * D + rnd * 1024:(ch % 2) * D + (rnd + 1) * 1024],
                    p_tr[:, 0:1024], s_y1r[:, rnd * 1024:(rnd + 1) * 1024], op=OP.add))
                env.mark(sV, f"ytok_{ch}_{rnd}")
            ysl = s_ych[:, (ch % 2) * D:(ch % 2 + 1) * D]
            v.scalar_tensor_tensor(s_W0[:], ysl[:, 0:1024], 0.0, ysl[:, 0:1024],
                                   op0=OP.add, op1=OP.add, accum_out=s_ln[:, 0:1])
            v.scalar_tensor_tensor(s_W0[:], ysl[:, 1024:2048], 0.0, ysl[:, 1024:2048],
                                   op0=OP.add, op1=OP.add, accum_out=s_ln[:, 1:2])
            v.scalar_tensor_tensor(s_W0[:], ysl[:, 0:1024], 0.0, ysl[:, 0:1024],
                                   op0=OP.add, op1=OP.mult, accum_out=s_ln[:, 2:3])
            vsync(v, v.scalar_tensor_tensor(s_W0[:], ysl[:, 1024:2048], 0.0,
                                            ysl[:, 1024:2048], op0=OP.add, op1=OP.mult,
                                            accum_out=s_ln[:, 3:4]))
            vsync(v, v.tensor_tensor(s_ln[:, 9:10], s_ln[:, 0:1], s_ln[:, 1:2], op=OP.add))
            vsync(v, v.tensor_tensor(s_ln[:, 10:11], s_ln[:, 2:3], s_ln[:, 3:4], op=OP.add))
            vsync(v, v.tensor_scalar(s_ln[:, 9:10], s_ln[:, 9:10], 1.0 / D, None, op0=OP.mult))
            vsync(v, v.tensor_scalar(s_ln[:, 11:12], s_ln[:, 10:11], 1.0 / D, None, op0=OP.mult))
            vsync(v, v.tensor_tensor(s_ln[:, 12:13], s_ln[:, 9:10], s_ln[:, 9:10], op=OP.mult))
            sV.inc(v.tensor_tensor(s_ln[:, 11:12], s_ln[:, 11:12], s_ln[:, 12:13],
                                   op=OP.subtract))
            env.mark(sV, f"ln2v_{ch}")
            env.wait(v, sS, f"ln2sq_{ch}")
            vsync(v, v.reciprocal(s_ln[:, 13:14], s_ln[:, 13:14]))
            vsync(v, v.tensor_tensor(s_ln[:, 14:15], s_ln[:, 9:10], s_ln[:, 13:14], op=OP.mult))
            vsync(v, v.tensor_scalar(s_ln[:, 14:15], s_ln[:, 14:15], -1.0, None, op0=OP.mult))
            sV.inc(v.tensor_scalar(ysl, ysl, s_ln[:, 13:14], s_ln[:, 14:15],
                                   op0=OP.mult, op1=OP.add))
            env.mark(sV, f"out_{ch}")

    progs = [prog_gpsimd, prog_sync, prog_scalar, prog_tensor, prog_vector]
    env.plan = True
    for f in progs:
        f(_PlanEng())
    env.plan = False
    for c in ctrs.values():
        c.v = 0
    A.finalize(es)
    with es:
        for c in ctrs.values():
            c.sem = es.enter_context(nc.semaphore(c.name))
        block = es.enter_context(nc.Block())
        block.gpsimd(prog_gpsimd)
        block.sync(prog_sync)
        block.scalar(prog_scalar)
        block.tensor(prog_tensor)
        block.vector(prog_vector)
    nc._arena_map = {h.name: (off, list(h.shape), h.dtype) for h, off in A.tensors}
    return nc


def _prep_inputs(L0_attn_X, L0_attn_W_Q, L0_attn_hh_b, L0_attn_gn_weight,
                 L0_attn_gn_bias, L0_ffn_w_gate, L0_ffn_w_up, L0_ffn_w_down):
    X = np.ascontiguousarray(np.asarray(L0_attn_X, dtype=np.float32)[0])
    xfT = np.ascontiguousarray(X.T)
    WQ = np.asarray(L0_attn_W_Q, dtype=np.float32)
    wgT = np.asarray(L0_ffn_w_gate, dtype=np.float32).T
    wuT = np.asarray(L0_ffn_w_up, dtype=np.float32).T
    wdT = np.asarray(L0_ffn_w_down, dtype=np.float32).T
    wg_t = np.ascontiguousarray(wgT.reshape(NI, 128, NOT, 128).transpose(2, 1, 0, 3)
                                ).reshape(NOT, 128, NI * 128)
    wu_t = np.ascontiguousarray(wuT.reshape(NI, 128, NOT, 128).transpose(2, 1, 0, 3)
                                ).reshape(NOT, 128, NI * 128)
    wd_t = np.ascontiguousarray(wdT.reshape(NOT, 128, NDT, 128).transpose(2, 1, 0, 3)
                                ).reshape(NDT, 128, NOT * 128)
    hh = np.asarray(L0_attn_hh_b, dtype=np.float32)
    gwv = np.asarray(L0_attn_gn_weight, dtype=np.float32)
    gbv = np.asarray(L0_attn_gn_bias, dtype=np.float32)
    in_maps = []
    for k in range(NC):
        cols = slice(C * k, C * (k + 1))
        wqs_k = np.ascontiguousarray(
            WQ[:, cols].reshape(NI, 128, C).transpose(1, 0, 2)).reshape(128, NI * C)
        q = 256 * k + np.arange(256)
        rows = (4 * q[:, None] + np.arange(4)[None, :]).reshape(-1)
        xres_k = np.ascontiguousarray(X[rows].reshape(256, 4 * D))
        vecs_k = np.zeros((16, 128), dtype=np.float32)
        vecs_k[0] = hh[cols][0:128]
        vecs_k[1] = hh[cols][128:256]
        vecs_k[2] = gwv[cols][0:128]
        vecs_k[3] = gwv[cols][128:256]
        vecs_k[4] = gbv[cols][0:128]
        vecs_k[5] = gbv[cols][128:256]
        vecs_k[6, :] = 1.0 if k == 0 else 0.0
        vecs_k[7, :] = np.arange(128) // 2
        c64 = float(10000.0 ** (-1.0 / 32.0))
        c_hi = np.float32(c64)
        vecs_k[8, :] = c_hi
        vecs_k[9, :] = np.float32(c64 - float(c_hi))
        jj = np.arange(128) // 2
        for i in range(5):
            vecs_k[10 + i, :] = (jj >> i) & 1
        in_maps.append({"xf": xfT, "xres": xres_k, "wqs": wqs_k,
                        "wg": wg_t, "wu": wu_t, "wd": wd_t, "vecs": vecs_k})
    return in_maps


def kernel(L0_attn_X, L0_attn_S_n, L0_attn_W_Q, L0_attn_hh_b, L0_attn_gn_weight,
           L0_attn_gn_bias, L0_ffn_w_gate, L0_ffn_w_up, L0_ffn_w_down):
    in_maps = _prep_inputs(L0_attn_X, L0_attn_W_Q, L0_attn_hh_b, L0_attn_gn_weight,
                           L0_attn_gn_bias, L0_ffn_w_gate, L0_ffn_w_up, L0_ffn_w_down)
    nc = build_nc()
    res = run_bass_kernel_spmd(nc, in_maps, core_ids=list(range(NC)))
    outp = np.empty((1, T, D), dtype=np.float32)
    for k in range(NC):
        o = np.asarray(res.results[k]["out"], dtype=np.float32).reshape(256, 4, D)
        q = 256 * k + np.arange(256)
        rows = (4 * q[:, None] + np.arange(4)[None, :]).reshape(-1)
        outp[0, rows] = o.reshape(1024, D)
    return outp
